# version 08 (final)
# baseline (speedup 1.0000x reference)
import numpy as np
import ml_dtypes

import concourse.bass as bass
import concourse.mybir as mybir
from concourse.bass_utils import run_bass_kernel_spmd
from concourse.mybir import (
    AluOpType as ALU,
    ActivationFunctionType as AF,
    AxisListType as AX,
)

F32 = mybir.dt.float32
BF16 = mybir.dt.bfloat16
I32 = mybir.dt.int32
NPBF = ml_dtypes.bfloat16

NCORES = 8
D = 2048
H = 16
HD = 128
WIN = 512
S = 2048
T = S + WIN
II = 8192
NBLK = S // WIN
EPS = 1e-5
C0 = 12.0
SCALE = HD ** -0.5

P1_LIMIT = 99
P1_ROPE = 3
ENG2ATTR = {"pe": "tensor", "act": "scalar", "dve": "vector",
            "pool": "gpsimd", "sp": "sync"}


class Builder:

    def __init__(self):
        self.prog = {e: [] for e in ENG2ATTR}
        self.counts = {}
        self.know = {e: {} for e in ENG2ATTR}
        self.last_writer = {}
        self.readers = {}
        self.groups = {}
        self.psum_keys = set()
        self.sems = []
        for e in ENG2ATTR:
            self._sem("c_" + e)

    def _sem(self, name):
        if name not in self.counts:
            self.sems.append(name)
            self.counts[name] = 0
        return name

    def _wait(self, eng, sem, val, tok_know):
        if sem.startswith("d_"):
            val = max(val, self.counts[sem])
        if self.know[eng].get(sem, 0) >= val:
            return
        if not (eng == "pe" and sem == "c_pe"):
            self.prog[eng].append(("wait", sem, val))
        k = self.know[eng]
        k[sem] = max(k.get(sem, 0), val)
        if tok_know:
            for s2, v2 in tok_know.items():
                if k.get(s2, 0) < v2:
                    k[s2] = v2

    def _deps(self, eng, reads, writes):
        need = {}

        def add(tok):
            if tok is None:
                return
            sem, val, kn = tok
            cur = need.get(sem)
            if cur is None or val > cur[0]:
                need[sem] = (val, kn)

        for r in reads:
            add(self.last_writer.get(r))
        for w in writes:
            add(self.last_writer.get(w))
            for tok in self.readers.get(w, ()):
                add(tok)
        for sem, (val, kn) in sorted(need.items()):
            self._wait(eng, sem, val, kn)

    def _attach(self, tok, rw_list):
        for rds, wrs in rw_list:
            for w in wrs:
                self.last_writer[w] = tok
                self.readers[w] = []
            for r in rds:
                self.readers.setdefault(r, []).append(tok)

    def reg_psum(self, *keys):
        self.psum_keys.update(keys)

    def op(self, eng, fn, reads=(), writes=(), group=None, final=True):
        locks = tuple(("rdl", r) for r in reads if r in self.psum_keys)
        writes = tuple(writes) + locks
        self._deps(eng, reads, writes)
        if group is not None:
            gk = (eng, group)
            self.groups.setdefault(gk, []).append((tuple(reads), tuple(writes)))
            if not final:
                self.prog[eng].append(("inst", fn, None, 0))
                return None
            sem = "c_" + eng
            self.counts[sem] += 1
            tok = (sem, self.counts[sem], dict(self.know[eng]))
            self.prog[eng].append(("inst", fn, sem, self.counts[sem]))
            self._attach(tok, self.groups.pop(gk))
            return tok
        sem = "c_" + eng
        self.counts[sem] += 1
        tok = (sem, self.counts[sem], dict(self.know[eng]))
        self.prog[eng].append(("inst", fn, sem, self.counts[sem]))
        self._attach(tok, [(tuple(reads), tuple(writes))])
        return tok

    def dma(self, eng, stream, fn, reads=(), writes=()):
        self._deps(eng, reads, writes)
        sem = self._sem("d_" + stream)
        self.counts[sem] += 16
        tok = (sem, self.counts[sem], dict(self.know[eng]))
        self.prog[eng].append(("dma", fn, sem, self.counts[sem]))
        self._attach(tok, [(tuple(reads), tuple(writes))])
        return tok

    def barrier(self):
        for eng in ENG2ATTR:
            for sem in self.sems:
                if self.counts[sem] > 0:
                    self._wait(eng, sem, self.counts[sem], None)

    def finalize(self, nc, es):
        sem_handles = {name: es.enter_context(nc.semaphore(name))
                       for name in self.sems}
        out_waits = [(s, self.counts[s]) for s in self.sems
                     if s.startswith("d_") and self.counts[s] > 0]
        block = es.enter_context(nc.Block())
        for eng, attr in ENG2ATTR.items():
            items = self.prog[eng]

            def body(e, items=items, eng=eng):
                for it in items:
                    if it[0] == "wait":
                        e.wait_ge(sem_handles[it[1]], it[2])
                    else:
                        _, fn, sem, val = it
                        inst = fn(e)
                        if sem is not None:
                            inst.then_inc(sem_handles[sem],
                                          16 if it[0] == "dma" else 1)
                if eng == "sp":
                    for sname, cnt in out_waits:
                        e.wait_ge(sem_handles[sname], cnt)
            getattr(block, attr)(body)


def build_program(up_to=99):
    nc = bass.Bass("TRN2", target_bir_lowering=False, debug=False)
    b = Builder()
    b.reg_psum("p0_tpp", "p0_bcp",
               "p1_qps0", "p1_qps1", "p1_sps0", "p1_sps1", "p1_vps0", "p1_vps1",
               "p2_scp0", "p2_scp1", "p2_ovp0", "p2_ovp1", "p2_smp0", "p2_smp1",
               "p2_bcpp0", "p2_bcpp1",
               "p3_prp0", "p3_prp1", "p3_stp", "p3_sqp", "p3_b3p", "p3_trp", "p3_mip",
               "p4_xbp", "p4_gp0", "p4_gp1", "p4_up0", "p4_up1", "p4_trp", "p4_mip",
               *[f"p5_dnp{i}" for i in range(8)],
               "p6_stp", "p6_sqp", "p6_b6p")
    from contextlib import ExitStack
    root = ExitStack()

    def ein(name, shape, dt):
        return nc.dram_tensor(name, shape, dt, kind="ExternalInput").ap()

    xT = ein("xT", [D, T], F32)
    wqT = ein("wqT", [D, D], F32)
    wkT = ein("wkT", [D, D], F32)
    wvT = ein("wvT", [D, D], F32)
    woT = ein("woT", [D, D], F32)
    wgT = ein("wgT", [D, II], F32)
    wuT = ein("wuT", [D, II], F32)
    wdT = ein("wdT", [II, D], F32)
    ropeC = ein("ropeC", [HD, T], F32)
    ropeS = ein("ropeS", [HD, T], F32)
    shufT = ein("shufT", [HD, HD], BF16)
    mprev_i = ein("mprev", [128, 128], BF16)
    mcur_i = ein("mcur", [128, 128], BF16)
    zcol_i = ein("zcol", [128, 1], F32)
    ones_bf_i = ein("ones_bf", [128, 1], BF16)
    onesrow_i = ein("onesrow_f", [1, 128], F32)
    ident_i = ein("ident_f", [128, 128], F32)

    outT = nc.dram_tensor("outT", [D, S], F32, kind="ExternalOutput").ap()

    def scr(name, shape, dt):
        return nc.dram_tensor(name, shape, dt, kind="Internal").ap()

    qrot_h = scr("qrot_h", [D, S], BF16)
    krot_h = scr("krot_h", [D, T], BF16)
    v_h = scr("v_h", [T, D], BF16)
    ofm_h = scr("ofm_h", [D, S], BF16)
    r1_h = scr("r1_h", [D, S], F32)
    xq_h = scr("xq_h", [D, S], BF16)
    h_h = scr("h_h", [II, S], F32)
    wqd_h = scr("wqd_h", [II, D], BF16)
    wqg_h = scr("wqg_h", [128, 16, II], BF16)
    wqu_h = scr("wqu_h", [128, 16, II], BF16)
    hq_h = scr("hq_h", [II, S], BF16)
    r2_h = scr("r2_h", [D, S], F32)
    xs_h = scr("xs_h", [1, S], F32)
    s2i_h = scr("s2i_h", [1, S], F32)
    dsc_h = scr("dsc_h", [1, S], F32)

    def mkalloc(es):
        def alloc(name, shape, dt):
            return es.enter_context(nc.sbuf_tensor(name, shape, dt))
        return alloc

    pal = mkalloc(root)
    consts = {}
    for nm, src_ap, shape, dt in [
        ("shufT", shufT, [HD, HD], BF16), ("mprev", mprev_i, [128, 128], BF16),
        ("mcur", mcur_i, [128, 128], BF16), ("zcol", zcol_i, [128, 1], F32),
        ("ones_bf", ones_bf_i, [128, 1], BF16),
        ("onesrow_f", onesrow_i, [1, 128], F32),
        ("ident_f", ident_i, [128, 128], F32),
    ]:
        t = pal("c_" + nm, shape, dt)
        b.dma("sp", "cst", lambda e, t=t, s=src_ap: e.dma_start(out=t[:], in_=s),
              writes=["c_" + nm])
        consts[nm] = t
    cbias = pal("cbias", [128, 1], F32)
    epsrow = pal("epsrow", [1, 1], F32)
    b.op("dve", lambda e: e.memset(cbias[:], -C0), writes=["cbias"])
    b.op("dve", lambda e: e.memset(epsrow[:], EPS), writes=["epsrow"])
    wscol = {m: pal(f"wscol_{m}", [128, 1], F32) for m in "gud"}
    taucol = {m: pal(f"taucol_{m}", [128, 1], F32) for m in "gud"}
    ntaucol = {m: pal(f"ntaucol_{m}", [128, 1], F32) for m in "gud"}
    wssc = {m: pal(f"wssc_{m}", [1, 1], F32) for m in "gud"}

    p01 = ExitStack()
    al = mkalloc(p01)
    wt0 = [al(f"p0_wt{i}", [128, 1024], F32) for i in range(2)]
    redc = al("p0_redc", [128, 128], F32)
    lo_t = al("p0_lo", [128, 128], F32)
    ts_t = al("p0_ts", [128, 64], F32)
    tb_t = al("p0_tb", [128, 64], F32)
    te_t = al("p0_te", [128, 64], F32)
    hrow = al("p0_hrow", [1, 128], F32)
    lrow = al("p0_lrow", [1, 128], F32)
    trow = [al(f"p0_trow{i}", [1, 64], F32) for i in range(3)]
    tp_ps = p01.enter_context(nc.psum_tensor("p0_tp", [128, 128], F32))
    bc_ps = p01.enter_context(nc.psum_tensor("p0_bc", [128, 1], F32))

    def twosum_level(hi_ap, lo_ap, hik, lok, n, ts, tbb, te, tsk, tbk, tek):
        h2 = n // 2
        A, B = hi_ap[:, 0:h2], hi_ap[:, h2:n]
        b.op("dve", lambda e: e.tensor_tensor(ts[:, 0:h2], A, B, ALU.add),
             reads=[hik], writes=[tsk])
        b.op("dve", lambda e: e.tensor_tensor(tbb[:, 0:h2], ts[:, 0:h2], A, ALU.subtract),
             reads=[tsk, hik], writes=[tbk])
        b.op("dve", lambda e: e.tensor_tensor(te[:, 0:h2], ts[:, 0:h2], tbb[:, 0:h2], ALU.subtract),
             reads=[tsk, tbk], writes=[tek])
        b.op("dve", lambda e: e.tensor_tensor(te[:, 0:h2], A, te[:, 0:h2], ALU.subtract),
             reads=[hik, tek], writes=[tek])
        b.op("dve", lambda e: e.tensor_tensor(tbb[:, 0:h2], B, tbb[:, 0:h2], ALU.subtract),
             reads=[hik, tbk], writes=[tbk])
        b.op("dve", lambda e: e.tensor_tensor(te[:, 0:h2], te[:, 0:h2], tbb[:, 0:h2], ALU.add),
             reads=[tek, tbk], writes=[tek])
        b.op("dve", lambda e: e.tensor_tensor(tbb[:, 0:h2], lo_ap[:, 0:h2], lo_ap[:, h2:n], ALU.add),
             reads=[lok], writes=[tbk])
        b.op("dve", lambda e: e.tensor_tensor(lo_ap[:, 0:h2], tbb[:, 0:h2], te[:, 0:h2], ALU.add),
             reads=[tbk, tek], writes=[lok])
        b.op("dve", lambda e: e.tensor_copy(hi_ap[:, 0:h2], ts[:, 0:h2]),
             reads=[tsk], writes=[hik])

    def gen_p0a():
        for m, wT_ap, rows in (("g", wgT, D), ("u", wuT, D), ("d", wdT, II)):
            cols = II if rows == D else D
            ti = 0
            for r0 in range(0, rows, 128):
                for c0 in range(0, cols, 1024):
                    sl = ti % 2
                    col = ti % 128
                    if ti >= 128:
                        raise AssertionError("tile count per matrix must be 128")
                    b.dma("sp", f"p0w{sl}",
                          lambda e, sl=sl, r0=r0, c0=c0, wT_ap=wT_ap:
                          e.dma_start(out=wt0[sl][:], in_=wT_ap[r0:r0 + 128, c0:c0 + 1024]),
                          writes=[f"p0_wt{sl}"])
                    b.op("dve", lambda e, sl=sl, col=col: e.tensor_reduce(
                        redc[:, col:col + 1], wt0[sl][:], AX.X, ALU.add,
                        apply_absolute_value=True),
                        reads=[f"p0_wt{sl}"], writes=["p0_redc"])
                    ti += 1
                    yield
            b.op("pool", lambda e: e.memset(lo_t[:], 0.0), writes=["p0_lo"])
            yield
            n = 128
            while n > 1:
                twosum_level(redc, lo_t, "p0_redc", "p0_lo", n, ts_t, tb_t, te_t,
                             "p0_ts", "p0_tb", "p0_te")
                n //= 2
                yield
            b.op("pe", lambda e: e.transpose(tp_ps[0:1, 0:128], redc[:, 0:1],
                                             consts["ident_f"][:]),
                 reads=["p0_redc", "c_ident_f"], writes=["p0_tpp"])
            b.op("act", lambda e: e.copy(hrow[:], tp_ps[0:1, 0:128]),
                 reads=["p0_tpp"], writes=["p0_hrow"])
            b.op("pe", lambda e: e.transpose(tp_ps[0:1, 0:128], lo_t[:, 0:1],
                                             consts["ident_f"][:]),
                 reads=["p0_lo", "c_ident_f", "p0_hrow"], writes=["p0_tpp"])
            b.op("act", lambda e: e.copy(lrow[:], tp_ps[0:1, 0:128]),
                 reads=["p0_tpp"], writes=["p0_lrow"])
            yield
            n = 128
            while n > 1:
                twosum_level(hrow, lrow, "p0_hrow", "p0_lrow", n,
                             trow[0], trow[1], trow[2],
                             "p0_tr0", "p0_tr1", "p0_tr2")
                n //= 2
                yield
            b.op("dve", lambda e, m=m: e.tensor_tensor(wssc[m][:], hrow[0:1, 0:1],
                                                       lrow[0:1, 0:1], ALU.add),
                 reads=["p0_hrow", "p0_lrow"], writes=[f"wssc_{m}"])
            inv_n = 1.0 / float(rows * cols)
            b.op("dve", lambda e, m=m, inv_n=inv_n: e.tensor_scalar(
                wssc[m][:], wssc[m][:], inv_n, EPS, ALU.mult, ALU.add),
                reads=[f"wssc_{m}"], writes=[f"wssc_{m}"])
            b.op("pe", lambda e, m=m: e.matmul(bc_ps[:], consts["onesrow_f"][:],
                                               wssc[m][:], start=True, stop=True),
                 reads=["c_onesrow_f", f"wssc_{m}"], writes=["p0_bcp"])
            b.op("act", lambda e, m=m: e.copy(wscol[m][:], bc_ps[:]),
                 reads=["p0_bcp"], writes=[f"wscol_{m}"])
            b.op("dve", lambda e, m=m: e.tensor_scalar(taucol[m][:], wscol[m][:], 0.5, None, ALU.mult),
                 reads=[f"wscol_{m}"], writes=[f"taucol_{m}"])
            b.op("dve", lambda e, m=m: e.tensor_scalar(ntaucol[m][:], wscol[m][:], -0.5, None, ALU.mult),
                 reads=[f"wscol_{m}"], writes=[f"ntaucol_{m}"])
            yield

    _p0_iter = gen_p0a()
    _p0_done = [False]

    def pump(k=1):
        for _ in range(k):
            if not _p0_done[0]:
                try:
                    next(_p0_iter)
                except StopIteration:
                    _p0_done[0] = True

    if up_to >= 1:
        p1 = ExitStack()
        al = mkalloc(p1)
        xfm = al("p1_xfm", [128, 16 * T], BF16)
        xst = [al(f"p1_xst{i}", [128, T], F32) for i in range(2)]
        ctab = al("p1_ctab", [HD, T], F32)
        stab = al("p1_stab", [HD, T], F32)
        b.dma("sp", "p1c", lambda e: e.dma_start(out=ctab[:], in_=ropeC), writes=["p1_ctab"])
        b.dma("sp", "p1c", lambda e: e.dma_start(out=stab[:], in_=ropeS), writes=["p1_stab"])
        for c in range(16):
            sl = c % 2
            b.dma("sp", f"p1x{sl}", lambda e, sl=sl, c=c: e.dma_start(
                out=xst[sl][:], in_=xT[c * 128:(c + 1) * 128, :]), writes=[f"p1_xst{sl}"])
            b.op("dve", lambda e, sl=sl, c=c: e.tensor_copy(
                xfm[:, c * T:(c + 1) * T], xst[sl][:]),
                reads=[f"p1_xst{sl}"], writes=[("p1_xfm", c)])

        whalf = al("p1_whalf", [128, 16 * 1024], BF16)
        wst = [al(f"p1_wst{i}", [128, 1024], F32) for i in range(2)]
        p1ps = ExitStack()
        q_ps = [p1ps.enter_context(nc.psum_tensor(f"p1_qps{i}", [128, 512], F32)) for i in range(2)]
        s_ps = [p1ps.enter_context(nc.psum_tensor(f"p1_sps{i}", [128, 512], F32)) for i in range(2)]
        v_ps = [p1ps.enter_context(nc.psum_tensor(f"p1_vps{i}", [128, 512], F32)) for i in range(2)]
        qsb = [al(f"p1_qsb{i}", [128, 512], BF16) for i in range(2)]
        t1 = [al(f"p1_t1{i}", [128, 512], F32) for i in range(2)]
        t2 = [al(f"p1_t2{i}", [128, 512], F32) for i in range(2)]
        rout = [al(f"p1_rout{i}", [128, 512], BF16) for i in range(2)]
        vout = [al(f"p1_vout{i}", [128, 512], BF16) for i in range(2)]

        def load_whalf(wT_ap, half, scale, wkey="p1_wh"):
            for ic in range(16):
                sl = ic % 2
                b.dma("sp", f"p1w{sl}", lambda e, ic=ic, sl=sl, half=half, wT_ap=wT_ap:
                      e.dma_start(out=wst[sl][:],
                                  in_=wT_ap[ic * 128:(ic + 1) * 128,
                                            half * 1024:(half + 1) * 1024]),
                      writes=[f"p1_wst{sl}"])
                if scale is None:
                    b.op("dve", lambda e, ic=ic, sl=sl: e.tensor_copy(
                        whalf[:, ic * 1024:(ic + 1) * 1024], wst[sl][:]),
                        reads=[f"p1_wst{sl}"], writes=[(wkey, ic)])
                else:
                    b.op("dve", lambda e, ic=ic, sl=sl, scale=scale: e.tensor_scalar(
                        whalf[:, ic * 1024:(ic + 1) * 1024], wst[sl][:], scale, None, ALU.mult),
                        reads=[f"p1_wst{sl}"], writes=[(wkey, ic)])

        def qk_sweep(dst_h, wT_ap, scale, nblocks, blk_off, wkey="p1_wh"):
            for half in range(2 if P1_LIMIT >= 5 else 1):
                load_whalf(wT_ap, half, scale)
                for blk in range(nblocks if P1_LIMIT >= 5 else 1):
                    tb = blk_off + blk * WIN
                    for ol in range(8):
                        pump(2)
                        oc = half * 8 + ol
                        pp = (blk * 8 + ol) % 2
                        grp = f"qk{oc}_{blk}"
                        for ic in range(16):
                            b.op("pe", lambda e, ic=ic, ol=ol, tb=tb, pp=pp:
                                 e.matmul(q_ps[pp][:],
                                          whalf[:, ic * 1024 + ol * 128: ic * 1024 + (ol + 1) * 128],
                                          xfm[:, ic * T + tb: ic * T + tb + WIN],
                                          start=(ic == 0), stop=(ic == 15)),
                                 reads=[(wkey, ic), ("p1_xfm", ic)],
                                 writes=[f"p1_qps{pp}"], group=grp, final=(ic == 15))
                        if P1_ROPE == 0:
                            b.op("act", lambda e, pp=pp: e.copy(rout[pp][:], q_ps[pp][:]),
                                 reads=[f"p1_qps{pp}"], writes=[f"p1_rout{pp}"])
                        elif P1_ROPE == 1:
                            b.op("act", lambda e, pp=pp: e.copy(qsb[pp][:], q_ps[pp][:]),
                                 reads=[f"p1_qps{pp}"], writes=[f"p1_qsb{pp}"])
                            b.op("pe", lambda e, pp=pp: e.matmul(s_ps[pp][:], consts["shufT"][:],
                                                                 qsb[pp][:], start=True, stop=True),
                                 reads=["c_shufT", f"p1_qsb{pp}"], writes=[f"p1_sps{pp}"])
                            b.op("act", lambda e, pp=pp: e.copy(rout[pp][:], s_ps[pp][:]),
                                 reads=[f"p1_sps{pp}"], writes=[f"p1_rout{pp}"])
                        elif P1_ROPE == 2:
                            b.op("dve", lambda e, pp=pp, tb=tb: e.tensor_tensor(
                                t1[pp][:], q_ps[pp][:], ctab[:, tb:tb + WIN], ALU.mult),
                                reads=[f"p1_qps{pp}", "p1_ctab"], writes=[f"p1_t1{pp}"])
                            b.op("dve", lambda e, pp=pp: e.tensor_copy(rout[pp][:], t1[pp][:]),
                                 reads=[f"p1_t1{pp}"], writes=[f"p1_rout{pp}"])
                        elif P1_ROPE == 4:
                            b.op("act", lambda e, pp=pp: e.copy(qsb[pp][:], q_ps[pp][:]),
                                 reads=[f"p1_qps{pp}"], writes=[f"p1_qsb{pp}"])
                            b.op("pe", lambda e, pp=pp: e.matmul(s_ps[pp][:], consts["shufT"][:],
                                                                 qsb[pp][:], start=True, stop=True),
                                 reads=["c_shufT", f"p1_qsb{pp}"], writes=[f"p1_sps{pp}"])
                            b.op("dve", lambda e, pp=pp, tb=tb: e.tensor_tensor(
                                t1[pp][:], q_ps[pp][:], ctab[:, tb:tb + WIN], ALU.mult),
                                reads=[f"p1_qps{pp}", "p1_ctab"], writes=[f"p1_t1{pp}"])
                            b.op("dve", lambda e, pp=pp, tb=tb: e.tensor_tensor(
                                t2[pp][:], s_ps[pp][:], stab[:, tb:tb + WIN], ALU.mult),
                                reads=[f"p1_sps{pp}", "p1_stab"], writes=[f"p1_t2{pp}"])
                            b.op("dve", lambda e, pp=pp: e.tensor_tensor(
                                rout[pp][:], t1[pp][:], t2[pp][:], ALU.add),
                                reads=[f"p1_t1{pp}", f"p1_t2{pp}"], writes=[f"p1_rout{pp}"])
                        elif P1_ROPE == 5:
                            b.op("act", lambda e, pp=pp: e.copy(qsb[pp][:], q_ps[pp][:]),
                                 reads=[f"p1_qps{pp}"], writes=[f"p1_qsb{pp}"])
                            b.op("pe", lambda e, pp=pp: e.matmul(s_ps[pp][:], consts["shufT"][:],
                                                                 qsb[pp][:], start=True, stop=True),
                                 reads=["c_shufT", f"p1_qsb{pp}"], writes=[f"p1_sps{pp}"])
                            b.op("dve", lambda e, pp=pp, tb=tb: e.tensor_tensor(
                                t2[pp][:], s_ps[pp][:], stab[:, tb:tb + WIN], ALU.mult),
                                reads=[f"p1_sps{pp}", "p1_stab"], writes=[f"p1_t2{pp}"])
                            b.op("dve", lambda e, pp=pp: e.tensor_copy(rout[pp][:], t2[pp][:]),
                                 reads=[f"p1_t2{pp}"], writes=[f"p1_rout{pp}"])
                        else:
                            b.op("act", lambda e, pp=pp: e.copy(qsb[pp][:], q_ps[pp][:]),
                                 reads=[f"p1_qps{pp}"], writes=[f"p1_qsb{pp}"])
                            b.op("pe", lambda e, pp=pp: e.matmul(s_ps[pp][:], consts["shufT"][:],
                                                                 qsb[pp][:], start=True, stop=True),
                                 reads=["c_shufT", f"p1_qsb{pp}"], writes=[f"p1_sps{pp}"])
                            b.op("dve", lambda e, pp=pp, tb=tb: e.tensor_tensor(
                                t1[pp][:], q_ps[pp][:], ctab[:, tb:tb + WIN], ALU.mult),
                                reads=[f"p1_qps{pp}", "p1_ctab"], writes=[f"p1_t1{pp}"])
                            b.op("dve", lambda e, pp=pp, tb=tb: e.tensor_tensor(
                                t2[pp][:], s_ps[pp][:], stab[:, tb:tb + WIN], ALU.mult),
                                reads=[f"p1_sps{pp}", "p1_stab"], writes=[f"p1_t2{pp}"])
                            b.op("dve", lambda e, pp=pp: e.tensor_tensor(
                                rout[pp][:], t1[pp][:], t2[pp][:], ALU.add),
                                reads=[f"p1_t1{pp}", f"p1_t2{pp}"], writes=[f"p1_rout{pp}"])
                        b.dma("act", f"p1o{pp}",
                              lambda e, pp=pp, oc=oc, tb=tb, dst_h=dst_h, blk_off=blk_off:
                              e.dma_start(out=dst_h[oc * 128:(oc + 1) * 128,
                                                    tb - blk_off: tb - blk_off + WIN],
                                          in_=rout[pp][:]),
                              reads=[f"p1_rout{pp}"])

        if P1_LIMIT >= 2:
            qk_sweep(qrot_h, wqT, SCALE, NBLK, WIN)
        if P1_LIMIT >= 3:
            qk_sweep(krot_h, wkT, None, NBLK + 1, 0)

        for half in (range(2) if P1_LIMIT >= 4 else []):
            load_whalf(wvT, half, None)
            for blk in range(NBLK + 1):
                for tt in range(4):
                    tb = blk * WIN + tt * 128
                    for og in range(2):
                        pump(2)
                        oc0 = half * 1024 + og * 512
                        pp = (tt * 2 + og) % 2
                        grp = f"v{blk}_{tt}_{og}"
                        for ic in range(16):
                            b.op("pe", lambda e, ic=ic, tb=tb, og=og, pp=pp:
                                 e.matmul(v_ps[pp][:],
                                          xfm[:, ic * T + tb: ic * T + tb + 128],
                                          whalf[:, ic * 1024 + og * 512: ic * 1024 + (og + 1) * 512],
                                          start=(ic == 0), stop=(ic == 15)),
                                 reads=[("p1_wh", ic), ("p1_xfm", ic)],
                                 writes=[f"p1_vps{pp}"], group=grp, final=(ic == 15))
                        b.op("act", lambda e, pp=pp: e.copy(vout[pp][:], v_ps[pp][:]),
                             reads=[f"p1_vps{pp}"], writes=[f"p1_vout{pp}"])
                        b.dma("act", f"p1v{pp}", lambda e, pp=pp, tb=tb, oc0=oc0:
                              e.dma_start(out=v_h[tb:tb + 128, oc0:oc0 + 512], in_=vout[pp][:]),
                              reads=[f"p1_vout{pp}"])

        pump(10000)
        b.barrier()
        p1ps.close()
        p1.close()
        p01.close()


    if up_to >= 2:
        p2 = ExitStack()
        al = mkalloc(p2)
        qblk = al("p2_q", [128, 16 * WIN], BF16)
        kbuf = [al(f"p2_k{i}", [128, 16 * WIN], BF16) for i in range(2)]
        vbuf = [al(f"p2_v{i}", [128, 4 * D], BF16) for i in range(2)]
        ptl = [al(f"p2_pt{i}", [128, WIN], BF16) for i in range(2)]
        bcs = [al(f"p2_bc{i}", [128, WIN], F32) for i in range(2)]
        rec = [al(f"p2_rec{i}", [1, WIN], F32) for i in range(2)]
        oev = [al(f"p2_oev{i}", [128, WIN], BF16) for i in range(2)]
        wdst = [al(f"p2_wd{i}", [128, 2048], F32) for i in range(2)]
        wqd_sb = [al(f"p2_wqd{i}", [128, 2048], BF16) for i in range(2)]
        qa2 = al("p2_qa", [128, 2048], BF16)
        qb2q = al("p2_qb", [128, 2048], BF16)
        zr512 = al("p2_zr", [128, WIN], BF16)
        b.op("pool", lambda e: e.memset(zr512[:], 0.0), writes=["p2_zr"])
        p2ps = ExitStack()
        sc_ps = [p2ps.enter_context(nc.psum_tensor(f"p2_sc{i}", [128, WIN], F32)) for i in range(2)]
        ov_ps = [p2ps.enter_context(nc.psum_tensor(f"p2_ov{i}", [128, WIN], F32)) for i in range(2)]
        sm_ps = [p2ps.enter_context(nc.psum_tensor(f"p2_sm{i}", [1, WIN], F32)) for i in range(2)]
        bc_ps2 = [p2ps.enter_context(nc.psum_tensor(f"p2_bcp{i}", [128, WIN], F32)) for i in range(2)]

        def load_kv(blk, slot):
            for c in range(16):
                b.dma("sp", f"p2k{slot}", lambda e, c=c, blk=blk, slot=slot: e.dma_start(
                    out=kbuf[slot][:, c * WIN:(c + 1) * WIN],
                    in_=krot_h[c * 128:(c + 1) * 128, blk * WIN:(blk + 1) * WIN]),
                    writes=[(f"p2_k{slot}",)])
            for tt in range(4):
                b.dma("sp", f"p2v{slot}", lambda e, tt=tt, blk=blk, slot=slot: e.dma_start(
                    out=vbuf[slot][:, tt * D:(tt + 1) * D],
                    in_=v_h[blk * WIN + tt * 128: blk * WIN + (tt + 1) * 128, :]),
                    writes=[(f"p2_v{slot}",)])

        load_kv(0, 0)
        for blk in range(NBLK):
            cur = (blk + 1) % 2
            prv = blk % 2
            load_kv(blk + 1, cur)
            for c in range(16):
                b.dma("act", "p2q", lambda e, c=c, blk=blk: e.dma_start(
                    out=qblk[:, c * WIN:(c + 1) * WIN],
                    in_=qrot_h[c * 128:(c + 1) * 128, blk * WIN:(blk + 1) * WIN]),
                    writes=[("p2_q",)])
            def emit_wq_chunk(kind, c):
                sl = c % 2
                if kind == "d":
                    b.dma("sp", f"p2wd{sl}", lambda e, c=c, sl=sl: e.dma_start(
                        out=wdst[sl][:], in_=wdT[c * 128:(c + 1) * 128, :]),
                        writes=[f"p2_wd{sl}"])
                else:
                    rt, cq = c % 16, c // 16
                    b.dma("sp", f"p2wd{sl}", lambda e, rt=rt, cq=cq, sl=sl: e.dma_start(
                        out=wdst[sl][:], in_=wgT[rt * 128:(rt + 1) * 128,
                                                 cq * 2048:(cq + 1) * 2048]),
                        writes=[f"p2_wd{sl}"])
                b.op("dve", lambda e, sl=sl, kind=kind: e.tensor_scalar(
                    qa2[:], wdst[sl][:], taucol[kind][:], None, ALU.is_gt),
                    reads=[f"p2_wd{sl}", f"taucol_{kind}"], writes=["p2_qa"])
                b.op("dve", lambda e, sl=sl, kind=kind: e.tensor_scalar(
                    qb2q[:], wdst[sl][:], ntaucol[kind][:], None, ALU.is_lt),
                    reads=[f"p2_wd{sl}", f"ntaucol_{kind}"], writes=["p2_qb"])
                b.op("dve", lambda e, sl=sl: e.tensor_tensor(
                    wqd_sb[sl][:], qa2[:], qb2q[:], ALU.subtract),
                    reads=["p2_qa", "p2_qb"], writes=[f"p2_wqd{sl}"])
                if kind == "d":
                    b.dma("act", f"p2wdo{sl}", lambda e, c=c, sl=sl: e.dma_start(
                        out=wqd_h[c * 128:(c + 1) * 128, :], in_=wqd_sb[sl][:]),
                        reads=[f"p2_wqd{sl}"])
                else:
                    rt, cq = c % 16, c // 16
                    b.dma("act", f"p2wdo{sl}", lambda e, rt=rt, cq=cq, sl=sl: e.dma_start(
                        out=wqg_h[:, rt:rt + 1, cq * 2048:(cq + 1) * 2048],
                        in_=wqd_sb[sl][:]),
                        reads=[f"p2_wqd{sl}"])

            for hh in range(H):
                emit_wq_chunk("d", blk * 16 + hh)
                emit_wq_chunk("g", blk * 16 + hh)
                pp = hh % 2
                b.op("pe", lambda e, pp=pp: e.matmul(
                    sm_ps[pp][:], consts["ones_bf"][:], zr512[:],
                    start=True, stop=False),
                    reads=["c_ones_bf", "p2_zr"], writes=[f"p2_smp{pp}"])
                b.op("pe", lambda e, pp=pp: e.matmul(
                    ov_ps[pp][:], consts["mprev"][:], zr512[:],
                    start=True, stop=False),
                    reads=["c_mprev", "p2_zr"], writes=[f"p2_ovp{pp}"])
                for jc in range(8):
                    if jc < 4:
                        ksl, kc = prv, jc
                        q0, q1 = 0, (jc + 1) * 128
                        dg = jc * 128
                        mk, mkk = consts["mprev"], "c_mprev"
                    else:
                        ksl, kc = cur, jc - 4
                        q0, q1 = (jc - 4) * 128, WIN
                        dg = (jc - 4) * 128
                        mk, mkk = consts["mcur"], "c_mcur"
                    qw = q1 - q0
                    pj = jc % 2
                    b.op("pe", lambda e, pj=pj, ksl=ksl, kc=kc, hh=hh, q0=q0, q1=q1, qw=qw:
                         e.matmul(sc_ps[pj][:, 0:qw],
                                  kbuf[ksl][:, hh * WIN + kc * 128: hh * WIN + (kc + 1) * 128],
                                  qblk[:, hh * WIN + q0: hh * WIN + q1],
                                  start=True, stop=True),
                         reads=[(f"p2_k{ksl}",), ("p2_q",)], writes=[f"p2_scp{pj}"])
                    b.op("act", lambda e, pj=pj, qw=qw: e.activation(
                        ptl[pj][:, 0:qw], sc_ps[pj][:, 0:qw], AF.Exp, bias=cbias[:]),
                        reads=[f"p2_scp{pj}", "cbias"], writes=[f"p2_pt{pj}"])
                    b.op("dve", lambda e, pj=pj, dg=dg, q0=q0, mk=mk: e.tensor_tensor(
                        ptl[pj][:, dg - q0: dg - q0 + 128],
                        ptl[pj][:, dg - q0: dg - q0 + 128], mk[:], ALU.mult),
                        reads=[f"p2_pt{pj}", mkk], writes=[f"p2_pt{pj}"])
                    if blk == 0 and jc < 4:
                        b.op("dve", lambda e, pj=pj, qw=qw: e.tensor_scalar(
                            ptl[pj][:, 0:qw], ptl[pj][:, 0:qw], consts["zcol"][:], None, ALU.mult),
                            reads=[f"p2_pt{pj}", "c_zcol"], writes=[f"p2_pt{pj}"])
                    segs = []
                    if jc < 4:
                        segs.append((0, (jc + 1) * 128, False))
                    else:
                        segs.append((q0, WIN, False))
                    vsl = prv if jc < 4 else cur
                    vtt = kc
                    last = (jc == 7)
                    for si, (a0, a1, st) in enumerate(segs):
                        fin = last and si == len(segs) - 1
                        b.op("pe", lambda e, pp=pp, pj=pj, a0=a0, a1=a1, q0=q0, fin=fin:
                             e.matmul(sm_ps[pp][:, a0:a1], consts["ones_bf"][:],
                                      ptl[pj][:, a0 - q0:a1 - q0], start=False, stop=fin),
                             reads=["c_ones_bf", f"p2_pt{pj}"],
                             writes=[f"p2_smp{pp}"])
                        b.op("pe", lambda e, pp=pp, pj=pj, a0=a0, a1=a1, q0=q0,
                             vsl=vsl, vtt=vtt, hh=hh, fin=fin:
                             e.matmul(ov_ps[pp][:, a0:a1],
                                      vbuf[vsl][:, vtt * D + hh * 128: vtt * D + (hh + 1) * 128],
                                      ptl[pj][:, a0 - q0:a1 - q0], start=False, stop=fin),
                             reads=[(f"p2_v{vsl}",), f"p2_pt{pj}"],
                             writes=[f"p2_ovp{pp}"])
                b.op("dve", lambda e, pp=pp: e.reciprocal(rec[pp][:], sm_ps[pp][:]),
                     reads=[f"p2_smp{pp}"], writes=[f"p2_rec{pp}"])
                b.op("pe", lambda e, pp=pp: e.matmul(bc_ps2[pp][:], consts["onesrow_f"][:],
                                                     rec[pp][:], start=True, stop=True),
                     reads=["c_onesrow_f", f"p2_rec{pp}"], writes=[f"p2_bcpp{pp}"])
                b.op("act", lambda e, pp=pp: e.copy(bcs[pp][:], bc_ps2[pp][:]),
                     reads=[f"p2_bcpp{pp}"], writes=[f"p2_bc{pp}"])
                b.op("dve", lambda e, pp=pp: e.tensor_tensor(
                    oev[pp][:], ov_ps[pp][:], bcs[pp][:], ALU.mult),
                    reads=[f"p2_ovp{pp}", f"p2_bc{pp}"], writes=[f"p2_oev{pp}"])
                b.dma("act", f"p2o{pp}", lambda e, pp=pp, hh=hh, blk=blk: e.dma_start(
                    out=ofm_h[hh * 128:(hh + 1) * 128, blk * WIN:(blk + 1) * WIN],
                    in_=oev[pp][:]),
                    reads=[f"p2_oev{pp}"])
        b.barrier()
        p2ps.close()
        p2.close()


    if up_to >= 3:
        p3 = ExitStack()
        al = mkalloc(p3)
        wo_sb = al("p3_wo", [128, 16 * D], BF16)
        wst3 = [al(f"p3_wst{i}", [128, 1024], F32) for i in range(2)]
        for ic in range(16):
            for hf in range(2):
                sl = (ic * 2 + hf) % 2
                b.dma("sp", f"p3w{sl}", lambda e, ic=ic, hf=hf, sl=sl: e.dma_start(
                    out=wst3[sl][:],
                    in_=woT[ic * 128:(ic + 1) * 128, hf * 1024:(hf + 1) * 1024]),
                    writes=[f"p3_wst{sl}"])
                b.op("dve", lambda e, ic=ic, hf=hf, sl=sl: e.tensor_copy(
                    wo_sb[:, ic * D + hf * 1024: ic * D + (hf + 1) * 1024], wst3[sl][:]),
                    reads=[f"p3_wst{sl}"], writes=[("p3_wo", ic)])

        ofm_sb = al("p3_ofm", [128, 16 * WIN], BF16)
        xT_t = [al(f"p3_xT{i}", [128, WIN], F32) for i in range(2)]
        r1_sb = al("p3_r1", [128, 16 * WIN], F32)
        xs_row = al("xs_row", [1, WIN], F32)
        xsi_row = al("xsi_row", [1, WIN], F32)
        rbf = [al(f"p3_rbf{i}", [128, WIN], BF16) for i in range(2)]
        sqv = [al(f"p3_sq{i}", [128, WIN], BF16) for i in range(2)]
        srow = al("p3_srow", [1, WIN], F32)
        vrow = al("p3_vrow", [1, WIN], F32)
        rrow = al("p3_rrow", [1, WIN], F32)
        mrow = al("p3_mrow", [1, WIN], F32)
        rstdb = al("p3_rstdb", [128, WIN], F32)
        mrsb = al("p3_mrsb", [128, WIN], F32)
        tmax = al("p3_tmax", [128, WIN], F32)
        tmin = al("p3_tmin", [128, WIN], F32)
        ttmp = al("p3_ttmp", [128, WIN], F32)
        scol = al("p3_scol", [128, 4], F32)
        sicol = al("p3_sicol", [128, 4], F32)
        sib = al("p3_sib", [128, WIN], F32)
        qt = [al(f"p3_qt{i}", [128, WIN], F32) for i in range(2)]
        qi = [al(f"p3_qi{i}", [128, WIN], I32) for i in range(2)]
        qd = [al(f"p3_qd{i}", [128, WIN], F32) for i in range(2)]
        qb2 = [al(f"p3_qb{i}", [128, WIN], F32) for i in range(2)]
        qc = [al(f"p3_qc{i}", [128, WIN], F32) for i in range(2)]
        xqo = [al(f"p3_xqo{i}", [128, WIN], BF16) for i in range(2)]
        wust3 = [al(f"p3_wu{i}", [128, 1024], F32) for i in range(2)]
        qa3 = al("p3_qa", [128, 1024], BF16)
        qb3 = al("p3_qb", [128, 1024], BF16)
        wqu3 = [al(f"p3_wqu{i}", [128, 1024], BF16) for i in range(2)]

        p3ps = ExitStack()
        pr_ps = [p3ps.enter_context(nc.psum_tensor(f"p3_pr{i}", [128, WIN], F32)) for i in range(2)]
        st_ps = p3ps.enter_context(nc.psum_tensor("p3_st", [1, WIN], F32))
        sq_ps = p3ps.enter_context(nc.psum_tensor("p3_sq", [1, WIN], F32))
        b3_ps = p3ps.enter_context(nc.psum_tensor("p3_b3", [128, WIN], F32))
        tr_ps = p3ps.enter_context(nc.psum_tensor("p3_tr", [128, 128], F32))
        mi_ps = p3ps.enter_context(nc.psum_tensor("p3_mi", [1, 128], F32))

        for blk in range(NBLK):
            for c in range(16):
                b.dma("sp", "p3o", lambda e, c=c, blk=blk: e.dma_start(
                    out=ofm_sb[:, c * WIN:(c + 1) * WIN],
                    in_=ofm_h[c * 128:(c + 1) * 128, blk * WIN:(blk + 1) * WIN]),
                    writes=[("p3_ofm",)])
            def emit_wu_chunk(c):
                rt, cq = c % 16, c // 16
                sl = c % 2
                b.dma("sp", f"p3wu{sl}", lambda e, rt=rt, cq=cq, sl=sl: e.dma_start(
                    out=wust3[sl][:], in_=wuT[rt * 128:(rt + 1) * 128,
                                              cq * 1024:(cq + 1) * 1024]),
                    writes=[f"p3_wu{sl}"])
                b.op("dve", lambda e, sl=sl: e.tensor_scalar(
                    qa3[:], wust3[sl][:], taucol["u"][:], None, ALU.is_gt),
                    reads=[f"p3_wu{sl}", "taucol_u"], writes=["p3_qa3"])
                b.op("dve", lambda e, sl=sl: e.tensor_scalar(
                    qb3[:], wust3[sl][:], ntaucol["u"][:], None, ALU.is_lt),
                    reads=[f"p3_wu{sl}", "ntaucol_u"], writes=["p3_qb3"])
                b.op("dve", lambda e, sl=sl: e.tensor_tensor(
                    wqu3[sl][:], qa3[:], qb3[:], ALU.subtract),
                    reads=["p3_qa3", "p3_qb3"], writes=[f"p3_wqu{sl}"])
                b.dma("act", f"p3wuo{sl}", lambda e, rt=rt, cq=cq, sl=sl: e.dma_start(
                    out=wqu_h[:, rt:rt + 1, cq * 1024:(cq + 1) * 1024],
                    in_=wqu3[sl][:]),
                    reads=[f"p3_wqu{sl}"])

            for oc in range(16):
                emit_wu_chunk(blk * 32 + 2 * oc)
                emit_wu_chunk(blk * 32 + 2 * oc + 1)
                pp = oc % 2
                b.dma("act", f"p3x{pp}", lambda e, oc=oc, blk=blk, pp=pp: e.dma_start(
                    out=xT_t[pp][:],
                    in_=xT[oc * 128:(oc + 1) * 128, WIN + blk * WIN: WIN + (blk + 1) * WIN]),
                    writes=[f"p3_xT{pp}"])
                grp = f"pr{blk}_{oc}"
                for ic in range(16):
                    b.op("pe", lambda e, oc=oc, ic=ic, pp=pp: e.matmul(
                        pr_ps[pp][:],
                        wo_sb[:, ic * D + oc * 128: ic * D + (oc + 1) * 128],
                        ofm_sb[:, ic * WIN:(ic + 1) * WIN],
                        start=(ic == 0), stop=(ic == 15)),
                        reads=[("p3_wo", ic), ("p3_ofm",)],
                        writes=[f"p3_prp{pp}"], group=grp, final=(ic == 15))
                b.op("dve", lambda e, oc=oc, pp=pp: e.tensor_tensor(
                    r1_sb[:, oc * WIN:(oc + 1) * WIN], pr_ps[pp][:],
                    xT_t[pp][:], ALU.add),
                    reads=[f"p3_prp{pp}", f"p3_xT{pp}"], writes=[("p3_r1", oc)])
                b.op("act", lambda e, oc=oc, pp=pp: e.copy(
                    rbf[pp][:], r1_sb[:, oc * WIN:(oc + 1) * WIN]),
                    reads=[("p3_r1", oc)], writes=[f"p3_rbf{pp}"])
                b.op("act", lambda e, oc=oc, pp=pp: e.activation(
                    sqv[pp][:], r1_sb[:, oc * WIN:(oc + 1) * WIN], AF.Square),
                    reads=[("p3_r1", oc)], writes=[f"p3_sq{pp}"])
                b.op("pe", lambda e, oc=oc, pp=pp: e.matmul(
                    st_ps[:], consts["ones_bf"][:], rbf[pp][:],
                    start=(oc == 0), stop=(oc == 15)),
                    reads=["c_ones_bf", f"p3_rbf{pp}"], writes=["p3_stp"])
                b.op("pe", lambda e, oc=oc, pp=pp: e.matmul(
                    sq_ps[:], consts["ones_bf"][:], sqv[pp][:],
                    start=(oc == 0), stop=(oc == 15)),
                    reads=["c_ones_bf", f"p3_sq{pp}"], writes=["p3_sqp"])
            b.op("dve", lambda e: e.tensor_scalar(srow[:], st_ps[:], 1.0 / D, None, ALU.mult),
                 reads=["p3_stp"], writes=["p3_srow"])
            b.op("dve", lambda e: e.tensor_scalar(vrow[:], sq_ps[:], 1.0 / D, None, ALU.mult),
                 reads=["p3_sqp"], writes=["p3_vrow"])
            b.op("dve", lambda e: e.tensor_tensor(mrow[:], srow[:], srow[:], ALU.mult),
                 reads=["p3_srow"], writes=["p3_mrow"])
            b.op("dve", lambda e: e.tensor_tensor(vrow[:], vrow[:], mrow[:], ALU.subtract),
                 reads=["p3_vrow", "p3_mrow"], writes=["p3_vrow"])
            b.op("act", lambda e: e.activation(vrow[:], vrow[:], AF.Ln, bias=epsrow[:]),
                 reads=["p3_vrow", "epsrow"], writes=["p3_vrow"])
            b.op("act", lambda e: e.activation(rrow[:], vrow[:], AF.Exp, scale=-0.5),
                 reads=["p3_vrow"], writes=["p3_rrow"])
            b.op("dve", lambda e: e.tensor_tensor(mrow[:], srow[:], rrow[:], ALU.mult),
                 reads=["p3_srow", "p3_rrow"], writes=["p3_mrow"])
            b.op("pe", lambda e: e.matmul(b3_ps[:], consts["onesrow_f"][:], rrow[:],
                                          start=True, stop=True),
                 reads=["c_onesrow_f", "p3_rrow"], writes=["p3_b3p"])
            b.op("act", lambda e: e.copy(rstdb[:], b3_ps[:]), reads=["p3_b3p"], writes=["p3_rstdb"])
            b.op("pe", lambda e: e.matmul(b3_ps[:], consts["onesrow_f"][:], mrow[:],
                                          start=True, stop=True),
                 reads=["c_onesrow_f", "p3_mrow"], writes=["p3_b3p"])
            b.op("act", lambda e: e.copy(mrsb[:], b3_ps[:]), reads=["p3_b3p"], writes=["p3_mrsb"])
            for oc in range(16):
                b.op("dve", lambda e, oc=oc: e.tensor_tensor(
                    r1_sb[:, oc * WIN:(oc + 1) * WIN],
                    r1_sb[:, oc * WIN:(oc + 1) * WIN], rstdb[:], ALU.mult),
                    reads=[("p3_r1", oc), "p3_rstdb"], writes=[("p3_r1", oc)])
                b.op("dve", lambda e, oc=oc: e.tensor_tensor(
                    r1_sb[:, oc * WIN:(oc + 1) * WIN],
                    r1_sb[:, oc * WIN:(oc + 1) * WIN], mrsb[:], ALU.subtract),
                    reads=[("p3_r1", oc), "p3_mrsb"], writes=[("p3_r1", oc)])
                b.dma("sp", "p3r1", lambda e, oc=oc, blk=blk: e.dma_start(
                    out=r1_h[oc * 128:(oc + 1) * 128, blk * WIN:(blk + 1) * WIN],
                    in_=r1_sb[:, oc * WIN:(oc + 1) * WIN]),
                    reads=[("p3_r1", oc)])
                if oc == 0:
                    b.op("dve", lambda e: e.tensor_copy(tmax[:], r1_sb[:, 0:WIN]),
                         reads=[("p3_r1", 0)], writes=["p3_tmax"])
                    b.op("dve", lambda e: e.tensor_copy(tmin[:], r1_sb[:, 0:WIN]),
                         reads=[("p3_r1", 0)], writes=["p3_tmin"])
                else:
                    b.op("dve", lambda e, oc=oc: e.tensor_tensor(
                        tmax[:], tmax[:], r1_sb[:, oc * WIN:(oc + 1) * WIN], ALU.max),
                        reads=["p3_tmax", ("p3_r1", oc)], writes=["p3_tmax"])
                    b.op("dve", lambda e, oc=oc: e.tensor_tensor(
                        tmin[:], tmin[:], r1_sb[:, oc * WIN:(oc + 1) * WIN], ALU.min),
                        reads=["p3_tmin", ("p3_r1", oc)], writes=["p3_tmin"])
            b.op("dve", lambda e: e.tensor_scalar(ttmp[:], tmin[:], -1.0, None, ALU.mult),
                 reads=["p3_tmin"], writes=["p3_ttmp"])
            b.op("dve", lambda e: e.tensor_tensor(ttmp[:], ttmp[:], tmax[:], ALU.max),
                 reads=["p3_ttmp", "p3_tmax"], writes=["p3_ttmp"])
            for q in range(4):
                b.op("pe", lambda e, q=q: e.transpose(tr_ps[:], ttmp[:, q * 128:(q + 1) * 128],
                                                      consts["ident_f"][:]),
                     reads=["p3_ttmp", "c_ident_f"], writes=["p3_trp"])
                b.op("dve", lambda e, q=q: e.tensor_reduce(
                    scol[:, q:q + 1], tr_ps[:], AX.X, ALU.max),
                    reads=["p3_trp"], writes=["p3_scol"])
            b.op("dve", lambda e: e.tensor_scalar(scol[:], scol[:], 1.0 / 127.0, EPS,
                                                  ALU.mult, ALU.add),
                 reads=["p3_scol"], writes=["p3_scol"])
            b.op("dve", lambda e: e.reciprocal(sicol[:], scol[:]),
                 reads=["p3_scol"], writes=["p3_sicol"])
            for q in range(4):
                b.op("pe", lambda e, q=q: e.transpose(mi_ps[0:1, 0:128], scol[:, q:q + 1],
                                                      consts["ident_f"][:]),
                     reads=["p3_scol", "c_ident_f"], writes=["p3_mip"])
                b.op("act", lambda e, q=q: e.copy(
                    xs_row[:, q * 128:(q + 1) * 128], mi_ps[0:1, 0:128]),
                    reads=["p3_mip"], writes=[("xs_row",)])
                b.op("pe", lambda e, q=q: e.transpose(mi_ps[0:1, 0:128], sicol[:, q:q + 1],
                                                      consts["ident_f"][:]),
                     reads=["p3_sicol", "c_ident_f", ("xs_row",)], writes=["p3_mip"])
                b.op("act", lambda e, q=q: e.copy(
                    xsi_row[:, q * 128:(q + 1) * 128], mi_ps[0:1, 0:128]),
                    reads=["p3_mip"], writes=[("xsi_row",)])
            b.dma("sp", "p3xs", lambda e, blk=blk: e.dma_start(
                out=xs_h[0:1, blk * WIN:(blk + 1) * WIN], in_=xs_row[:]),
                reads=[("xs_row",)])
            b.op("pe", lambda e: e.matmul(
                b3_ps[:], consts["onesrow_f"][:], xsi_row[:], start=True, stop=True),
                reads=[("xsi_row",), "c_onesrow_f"], writes=["p3_b3p"])
            b.op("act", lambda e: e.copy(sib[:], b3_ps[:]), reads=["p3_b3p"], writes=["p3_sib"])
            for oc in range(16):
                pp = oc % 2
                b.op("dve", lambda e, oc=oc, pp=pp: e.tensor_tensor(
                    qt[pp][:], r1_sb[:, oc * WIN:(oc + 1) * WIN], sib[:], ALU.mult),
                    reads=[("p3_r1", oc), "p3_sib"], writes=[f"p3_qt{pp}"])
                b.op("act", lambda e, pp=pp: e.activation(qi[pp][:], qt[pp][:], AF.Copy),
                     reads=[f"p3_qt{pp}"], writes=[f"p3_qi{pp}"])
                b.op("dve", lambda e, pp=pp: e.tensor_tensor(qd[pp][:], qt[pp][:], qi[pp][:], ALU.subtract),
                     reads=[f"p3_qt{pp}", f"p3_qi{pp}"], writes=[f"p3_qd{pp}"])
                b.op("dve", lambda e, pp=pp: e.tensor_scalar(qb2[pp][:], qd[pp][:], -0.5, None, ALU.is_lt),
                     reads=[f"p3_qd{pp}"], writes=[f"p3_qb{pp}"])
                b.op("dve", lambda e, pp=pp: e.scalar_tensor_tensor(
                    qc[pp][:], qd[pp][:], 0.5, qb2[pp][:], ALU.is_gt, ALU.subtract),
                    reads=[f"p3_qd{pp}", f"p3_qb{pp}"], writes=[f"p3_qc{pp}"])
                b.op("dve", lambda e, pp=pp: e.tensor_tensor(
                    xqo[pp][:], qi[pp][:], qc[pp][:], ALU.add),
                    reads=[f"p3_qi{pp}", f"p3_qc{pp}"], writes=[f"p3_xqo{pp}"])
                b.dma("act", f"p3xq{pp}", lambda e, oc=oc, blk=blk, pp=pp: e.dma_start(
                    out=xq_h[oc * 128:(oc + 1) * 128, blk * WIN:(blk + 1) * WIN],
                    in_=xqo[pp][:]),
                    reads=[f"p3_xqo{pp}"])
        b.barrier()
        p3ps.close()
        p3.close()


    if up_to >= 4:
        p4 = ExitStack()
        al = mkalloc(p4)
        xq_sb = al("p4_xq", [128, 16 * S], BF16)
        for c in range(16):
            b.dma("sp", "p4xq", lambda e, c=c: e.dma_start(
                out=xq_sb[:, c * S:(c + 1) * S], in_=xq_h[c * 128:(c + 1) * 128, :]),
                writes=[("p4_xq",)])
        xsb = al("p4_xsb", [128, S], F32)
        xsrow4 = al("p4_xsrow", [1, S], F32)
        p4ps = ExitStack()
        xb_es = ExitStack()
        xb_ps = xb_es.enter_context(nc.psum_tensor("p4_xb", [128, 512], F32))
        b.dma("sp", "p4xs", lambda e: e.dma_start(out=xsrow4[:], in_=xs_h),
              writes=[("p4_xsrow",)])
        for g4 in range(4):
            b.op("pe", lambda e, g4=g4: e.matmul(
                xb_ps[:], consts["onesrow_f"][:], xsrow4[:, g4 * 512:(g4 + 1) * 512],
                start=True, stop=True),
                reads=[("p4_xsrow",), "c_onesrow_f"], writes=["p4_xbp"])
            b.op("act", lambda e, g4=g4: e.copy(xsb[:, g4 * 512:(g4 + 1) * 512], xb_ps[:]),
                 reads=["p4_xbp"], writes=[("p4_xsb",)])

        probeb = al("p4_probe", [1, 2], F32)
        wqg = [al(f"p4_wqg{i}", [128, 4096], BF16) for i in range(2)]
        wqu = [al(f"p4_wqu{i}", [128, 4096], BF16) for i in range(2)]
        zg = al("p4_zg", [128, 1024], F32)
        zu = al("p4_zu", [128, 1024], F32)
        sg = al("p4_sg", [128, 1024], F32)
        hsb = [al(f"p4_h{i}", [128, 1024], F32) for i in range(2)]
        hmx = al("p4_hmx", [128, 2048], F32)
        hmn = al("p4_hmn", [128, 2048], F32)
        b.barrier()
        xb_es.close()
        gu_es = ExitStack()
        g_ps = [gu_es.enter_context(nc.psum_tensor(f"p4_g{i}", [128, 1024], F32)) for i in range(2)]
        u_ps = [gu_es.enter_context(nc.psum_tensor(f"p4_u{i}", [128, 1024], F32)) for i in range(2)]

        b.op("pool", lambda e: e.memset(probeb[:], 0.0), writes=["p4_probeb"])

        def p4_wload(op):
            sl = op % 2
            for mat, src_h, dst in (("g", wqg_h, wqg), ("u", wqu_h, wqu)):
                b.dma("sp", f"p4w{mat}{sl}",
                      lambda e, op=op, src_h=src_h, dst=dst, sl=sl:
                      e.dma_start(out=dst[sl][:],
                                  in_=src_h[:, :, op * 256:(op + 1) * 256]),
                      writes=[(f"p4_wq{mat}{sl}",)])
                b.op("pool", lambda e: e.tensor_scalar(
                    probeb[:], probeb[:], 1.0, None, ALU.mult),
                    reads=["p4_probeb", (f"p4_wq{mat}{sl}",)],
                    writes=["p4_probeb", (f"p4_wq{mat}{sl}",)])

        def p4_compute(oc):
            sl = (oc // 2) % 2
            woff = (oc % 2) * 128
            hp = oc % 2
            for half in range(2):
                t0 = half * 1024
                gg = f"g{oc}_{half}"
                for ic in range(16):
                    for q2 in range(2):
                        b.op("pe", lambda e, ic=ic, sl=sl, t0=t0, half=half, q2=q2: e.matmul(
                            g_ps[half][:, q2 * 512:(q2 + 1) * 512],
                            wqg[sl][:, ic * 256 + woff: ic * 256 + woff + 128],
                            xq_sb[:, ic * S + t0 + q2 * 512: ic * S + t0 + (q2 + 1) * 512],
                            start=(ic == 0), stop=(ic == 15)),
                            reads=[(f"p4_wqg{sl}",), ("p4_xq",)],
                            writes=[f"p4_gp{half}"], group=gg,
                            final=(ic == 15 and q2 == 1))
                uu = f"u{oc}_{half}"
                for ic in range(16):
                    for q2 in range(2):
                        b.op("pe", lambda e, ic=ic, sl=sl, t0=t0, half=half, q2=q2: e.matmul(
                            u_ps[half][:, q2 * 512:(q2 + 1) * 512],
                            wqu[sl][:, ic * 256 + woff: ic * 256 + woff + 128],
                            xq_sb[:, ic * S + t0 + q2 * 512: ic * S + t0 + (q2 + 1) * 512],
                            start=(ic == 0), stop=(ic == 15)),
                            reads=[(f"p4_wqu{sl}",), ("p4_xq",)],
                            writes=[f"p4_up{half}"], group=uu,
                            final=(ic == 15 and q2 == 1))
                b.op("dve", lambda e, half=half, t0=t0: e.scalar_tensor_tensor(
                    zg[:], g_ps[half][:], wscol["g"][:], xsb[:, t0:t0 + 1024],
                    ALU.mult, ALU.mult),
                    reads=[f"p4_gp{half}", "wscol_g", ("p4_xsb",)], writes=["p4_zg"])
                b.op("act", lambda e: e.activation(sg[:], zg[:], AF.Silu),
                     reads=["p4_zg"], writes=["p4_sg"])
                b.op("dve", lambda e, half=half, t0=t0: e.scalar_tensor_tensor(
                    zu[:], u_ps[half][:], wscol["u"][:], xsb[:, t0:t0 + 1024],
                    ALU.mult, ALU.mult),
                    reads=[f"p4_up{half}", "wscol_u", ("p4_xsb",)], writes=["p4_zu"])
                hkey = f"p4_h{hp}"
                b.op("dve", lambda e, hp=hp: e.tensor_tensor(
                    hsb[hp][:], sg[:], zu[:], ALU.mult),
                    reads=["p4_sg", "p4_zu"], writes=[hkey])
                b.dma("sp", f"p4h{hp}", lambda e, oc=oc, hp=hp, t0=t0: e.dma_start(
                    out=h_h[oc * 128:(oc + 1) * 128, t0:t0 + 1024], in_=hsb[hp][:]),
                    reads=[hkey])
                if oc == 0:
                    b.op("dve", lambda e, hp=hp, t0=t0: e.tensor_copy(
                        hmx[:, t0:t0 + 1024], hsb[hp][:]),
                        reads=[hkey], writes=[("p4_hmx", half)])
                    b.op("dve", lambda e, hp=hp, t0=t0: e.tensor_copy(
                        hmn[:, t0:t0 + 1024], hsb[hp][:]),
                        reads=[hkey], writes=[("p4_hmn", half)])
                else:
                    b.op("dve", lambda e, hp=hp, t0=t0: e.tensor_tensor(
                        hmx[:, t0:t0 + 1024], hmx[:, t0:t0 + 1024],
                        hsb[hp][:], ALU.max),
                        reads=[("p4_hmx", half), hkey], writes=[("p4_hmx", half)])
                    b.op("dve", lambda e, hp=hp, t0=t0: e.tensor_tensor(
                        hmn[:, t0:t0 + 1024], hmn[:, t0:t0 + 1024],
                        hsb[hp][:], ALU.min),
                        reads=[("p4_hmn", half), hkey], writes=[("p4_hmn", half)])

        p4_wload(0)
        for op in range(II // 256):
            if op + 1 < II // 256:
                p4_wload(op + 1)
            p4_compute(2 * op)
            p4_compute(2 * op + 1)

        b.barrier()
        gu_es.close()
        b.op("dve", lambda e: e.tensor_scalar(hmn[:], hmn[:], -1.0, None, ALU.mult),
             reads=[("p4_hmn", 0), ("p4_hmn", 1)], writes=[("p4_hmn", 0), ("p4_hmn", 1)])
        b.op("dve", lambda e: e.tensor_tensor(hmx[:], hmx[:], hmn[:], ALU.max),
             reads=[("p4_hmx", 0), ("p4_hmx", 1), ("p4_hmn", 0), ("p4_hmn", 1)],
             writes=[("p4_hmx", 0), ("p4_hmx", 1)])
        tr4 = p4ps.enter_context(nc.psum_tensor("p4_tr", [128, 128], F32))
        mi4 = p4ps.enter_context(nc.psum_tensor("p4_mi", [1, 128], F32))
        s2col = al("p4_s2col", [128, 16], F32)
        s2_row = al("p4_s2row", [1, S], F32)
        s2i_row = al("p4_s2irow", [1, S], F32)
        dsc_row = al("p4_dscrow", [1, S], F32)
        for q in range(16):
            b.op("pe", lambda e, q=q: e.transpose(tr4[:], hmx[:, q * 128:(q + 1) * 128],
                                                  consts["ident_f"][:]),
                 reads=[("p4_hmx", 0), ("p4_hmx", 1), "c_ident_f"], writes=["p4_trp"])
            b.op("dve", lambda e, q=q: e.tensor_reduce(
                s2col[:, q:q + 1], tr4[:], AX.X, ALU.max),
                reads=["p4_trp"], writes=["p4_s2col"])
        b.op("dve", lambda e: e.tensor_scalar(s2col[:], s2col[:], 1.0 / 127.0, EPS,
                                              ALU.mult, ALU.add),
             reads=["p4_s2col"], writes=["p4_s2col"])
        for q in range(16):
            b.op("pe", lambda e, q=q: e.transpose(mi4[0:1, 0:128], s2col[:, q:q + 1],
                                                  consts["ident_f"][:]),
                 reads=["p4_s2col", "c_ident_f"], writes=["p4_mip"])
            b.op("act", lambda e, q=q: e.copy(s2_row[:, q * 128:(q + 1) * 128], mi4[:]),
                 reads=["p4_mip"], writes=[("s2_row",)])
        b.op("dve", lambda e: e.reciprocal(s2i_row[:], s2_row[:]),
             reads=[("s2_row",)], writes=[("s2i_row",)])
        b.op("dve", lambda e: e.tensor_scalar(dsc_row[:], s2_row[:],
                                              wssc["d"][0:1, 0:1], None, ALU.mult),
             reads=[("s2_row",), "wssc_d"], writes=[("dsc_row",)])
        b.dma("sp", "p4s2", lambda e: e.dma_start(out=s2i_h, in_=s2i_row[:]),
              reads=[("s2i_row",)])
        b.dma("sp", "p4s2", lambda e: e.dma_start(out=dsc_h, in_=dsc_row[:]),
              reads=[("dsc_row",)])

        b.barrier()
        p4ps.close()
        p4.close()


    if up_to >= 5:
        p5 = ExitStack()
        al = mkalloc(p5)
        wqd_sl = al("p5_wqd", [128, 64 * 1024], BF16)
        s2i_row5 = al("p5_s2irow", [1, S], F32)
        dsc_row5 = al("p5_dscrow", [1, S], F32)
        hload = [al(f"p5_hl{i}", [128, WIN], F32) for i in range(2)]
        s2ib = al("p5_s2ib", [128, WIN], F32)
        dscb = al("p5_dscb", [128, WIN], F32)
        r1t = [al(f"p5_r1t{i}", [128, WIN], F32) for i in range(2)]
        r2t = [al(f"p5_r2t{i}", [128, WIN], F32) for i in range(2)]
        yv = al("p5_yv", [128, WIN], F32)
        q5t = [al(f"p5_qt{i}", [128, WIN], F32) for i in range(2)]
        q5i = [al(f"p5_qi{i}", [128, WIN], I32) for i in range(2)]
        q5d = [al(f"p5_qd{i}", [128, WIN], F32) for i in range(2)]
        q5b = [al(f"p5_qb{i}", [128, WIN], F32) for i in range(2)]
        q5c = [al(f"p5_qc{i}", [128, WIN], F32) for i in range(2)]
        p5ps = ExitStack()
        dn_ps = [p5ps.enter_context(nc.psum_tensor(f"p5_dn{i}", [128, WIN], F32)) for i in range(8)]

        b.dma("sp", "p5rows", lambda e: e.dma_start(out=s2i_row5[:], in_=s2i_h),
              writes=[("s2i_row",)])
        b.dma("sp", "p5rows", lambda e: e.dma_start(out=dsc_row5[:], in_=dsc_h),
              writes=[("dsc_row",)])
        hq_rq = [al(f"p5_hqr{i}", [128, WIN], BF16) for i in range(4)]
        hq_m = [al(f"p5_hqm{i}", [128, WIN], BF16) for i in range(4)]
        for oslab in range(2):
            for ic in range(64):
                b.dma("sp", f"p5w{ic % 2}", lambda e, ic=ic, oslab=oslab: e.dma_start(
                    out=wqd_sl[:, ic * 1024:(ic + 1) * 1024],
                    in_=wqd_h[ic * 128:(ic + 1) * 128, oslab * 1024:(oslab + 1) * 1024]),
                    writes=[("p5_wqd",)])
            for ts in range(4):
                b.op("pe", lambda e, ts=ts: e.matmul(
                    dn_ps[0][:], consts["onesrow_f"][:], dsc_row5[:, ts * WIN:(ts + 1) * WIN],
                    start=True, stop=True),
                    reads=[("dsc_row",), "c_onesrow_f"], writes=["p5_dnp0"])
                b.op("act", lambda e: e.copy(dscb[:], dn_ps[0][:]), reads=["p5_dnp0"],
                     writes=["p5_dscb"])
                if oslab == 0:
                    b.op("pe", lambda e, ts=ts: e.matmul(
                        dn_ps[0][:], consts["onesrow_f"][:], s2i_row5[:, ts * WIN:(ts + 1) * WIN],
                        start=True, stop=True),
                        reads=[("s2i_row",), "c_onesrow_f"], writes=["p5_dnp0"])
                    b.op("act", lambda e: e.copy(s2ib[:], dn_ps[0][:]), reads=["p5_dnp0"],
                         writes=["p5_s2ib"])
                for ic in range(64):
                    sl = ic % 2
                    q4 = ic % 4
                    if oslab == 0:
                        hkey = f"p5_hqr{q4}"
                        b.dma("sp", f"p5h{sl}", lambda e, ic=ic, ts=ts, sl=sl:
                              e.dma_start(out=hload[sl][:],
                                          in_=h_h[ic * 128:(ic + 1) * 128,
                                                  ts * WIN:(ts + 1) * WIN]),
                              writes=[f"p5_hl{sl}"])
                        b.op("dve", lambda e, sl=sl: e.tensor_tensor(
                            q5t[sl][:], hload[sl][:], s2ib[:], ALU.mult),
                            reads=[f"p5_hl{sl}", "p5_s2ib"], writes=[f"p5_qt{sl}"])
                        b.op("act", lambda e, sl=sl: e.activation(q5i[sl][:], q5t[sl][:], AF.Copy),
                             reads=[f"p5_qt{sl}"], writes=[f"p5_qi{sl}"])
                        b.op("dve", lambda e, sl=sl: e.tensor_tensor(q5d[sl][:], q5t[sl][:], q5i[sl][:], ALU.subtract),
                             reads=[f"p5_qt{sl}", f"p5_qi{sl}"], writes=[f"p5_qd{sl}"])
                        b.op("dve", lambda e, sl=sl: e.tensor_scalar(q5b[sl][:], q5d[sl][:], -0.5, None, ALU.is_lt),
                             reads=[f"p5_qd{sl}"], writes=[f"p5_qb{sl}"])
                        b.op("dve", lambda e, sl=sl: e.scalar_tensor_tensor(
                            q5c[sl][:], q5d[sl][:], 0.5, q5b[sl][:], ALU.is_gt, ALU.subtract),
                            reads=[f"p5_qd{sl}", f"p5_qb{sl}"], writes=[f"p5_qc{sl}"])
                        b.op("dve", lambda e, sl=sl, q4=q4: e.tensor_tensor(
                            hq_rq[q4][:], q5i[sl][:], q5c[sl][:], ALU.add),
                            reads=[f"p5_qi{sl}", f"p5_qc{sl}"], writes=[hkey])
                        b.dma("sp", f"p5hqo{q4}", lambda e, ic=ic, ts=ts, q4=q4:
                              e.dma_start(out=hq_h[ic * 128:(ic + 1) * 128,
                                                   ts * WIN:(ts + 1) * WIN],
                                          in_=hq_rq[q4][:]),
                              reads=[hkey], writes=[("hqh", ts, ic)])
                        hq_ap = hq_rq[q4]
                    else:
                        hkey = f"p5_hqm{q4}"
                        b.dma("sp", f"p5hq{q4}", lambda e, ic=ic, ts=ts, q4=q4:
                              e.dma_start(out=hq_m[q4][:],
                                          in_=hq_h[ic * 128:(ic + 1) * 128,
                                                   ts * WIN:(ts + 1) * WIN]),
                              reads=[("hqh", ts, ic)], writes=[hkey])
                        hq_ap = hq_m[q4]
                    for ohc in range(8):
                        gname = f"dn{oslab}_{ts}_{ic}"
                        b.op("pe", lambda e, ic=ic, hq_ap=hq_ap, ohc=ohc: e.matmul(
                            dn_ps[ohc][:], wqd_sl[:, ic * 1024 + ohc * 128: ic * 1024 + (ohc + 1) * 128],
                            hq_ap[:], start=(ic == 0), stop=(ic == 63)),
                            reads=[("p5_wqd",), hkey],
                            writes=[f"p5_dnp{ohc}"], group=gname, final=(ohc == 7))
                for ohc in range(8):
                    orow = oslab * 1024 + ohc * 128
                    sl2 = ohc % 2
                    b.dma("sp", f"p5r1{sl2}", lambda e, orow=orow, ts=ts, sl2=sl2: e.dma_start(
                        out=r1t[sl2][:], in_=r1_h[orow:orow + 128, ts * WIN:(ts + 1) * WIN]),
                        writes=[f"p5_r1t{sl2}"])
                    b.op("dve", lambda e, ohc=ohc: e.tensor_tensor(
                        yv[:], dn_ps[ohc][:], dscb[:], ALU.mult),
                        reads=[f"p5_dnp{ohc}", "p5_dscb"], writes=["p5_yv"])
                    b.op("dve", lambda e, sl2=sl2: e.tensor_tensor(
                        r2t[sl2][:], yv[:], r1t[sl2][:], ALU.add),
                        reads=["p5_yv", f"p5_r1t{sl2}"], writes=[f"p5_r2t{sl2}"])
                    b.dma("sp", f"p5r2{sl2}", lambda e, orow=orow, ts=ts, sl2=sl2: e.dma_start(
                        out=r2_h[orow:orow + 128, ts * WIN:(ts + 1) * WIN],
                        in_=r2t[sl2][:]),
                        reads=[f"p5_r2t{sl2}"])

        b.barrier()
        p5ps.close()
        p5.close()


    if up_to >= 6:
        p6 = ExitStack()
        al = mkalloc(p6)
        r2_sb = al("p6_r2", [128, 16 * WIN], F32)
        rbf6 = [al(f"p6_rbf{i}", [128, WIN], BF16) for i in range(2)]
        sq6 = [al(f"p6_sq{i}", [128, WIN], BF16) for i in range(2)]
        srow6 = al("p6_srow", [1, WIN], F32)
        vrow6 = al("p6_vrow", [1, WIN], F32)
        rrow6 = al("p6_rrow", [1, WIN], F32)
        mrow6 = al("p6_mrow", [1, WIN], F32)
        rstdb6 = al("p6_rstdb", [128, WIN], F32)
        mrsb6 = al("p6_mrsb", [128, WIN], F32)
        oout = [al(f"p6_o{i}", [128, WIN], F32) for i in range(2)]
        p6ps = ExitStack()
        st6 = p6ps.enter_context(nc.psum_tensor("p6_st", [1, WIN], F32))
        sq6p = p6ps.enter_context(nc.psum_tensor("p6_sq", [1, WIN], F32))
        b6 = p6ps.enter_context(nc.psum_tensor("p6_b6", [128, WIN], F32))

        for blk in range(NBLK):
            for c in range(16):
                b.dma("sp", "p6r2", lambda e, c=c, blk=blk: e.dma_start(
                    out=r2_sb[:, c * WIN:(c + 1) * WIN],
                    in_=r2_h[c * 128:(c + 1) * 128, blk * WIN:(blk + 1) * WIN]),
                    writes=[("p6_r2",)])
            for oc in range(16):
                pp = oc % 2
                b.op("act", lambda e, oc=oc, pp=pp: e.copy(
                    rbf6[pp][:], r2_sb[:, oc * WIN:(oc + 1) * WIN]),
                    reads=[("p6_r2",)], writes=[f"p6_rbf{pp}"])
                b.op("act", lambda e, oc=oc, pp=pp: e.activation(
                    sq6[pp][:], r2_sb[:, oc * WIN:(oc + 1) * WIN], AF.Square),
                    reads=[("p6_r2",)], writes=[f"p6_sq{pp}"])
                b.op("pe", lambda e, oc=oc, pp=pp: e.matmul(
                    st6[:], consts["ones_bf"][:], rbf6[pp][:],
                    start=(oc == 0), stop=(oc == 15)),
                    reads=["c_ones_bf", f"p6_rbf{pp}"], writes=["p6_stp"])
                b.op("pe", lambda e, oc=oc, pp=pp: e.matmul(
                    sq6p[:], consts["ones_bf"][:], sq6[pp][:],
                    start=(oc == 0), stop=(oc == 15)),
                    reads=["c_ones_bf", f"p6_sq{pp}"], writes=["p6_sqp"])
            b.op("dve", lambda e: e.tensor_scalar(srow6[:], st6[:], 1.0 / D, None, ALU.mult),
                 reads=["p6_stp"], writes=["p6_srow"])
            b.op("dve", lambda e: e.tensor_scalar(vrow6[:], sq6p[:], 1.0 / D, None, ALU.mult),
                 reads=["p6_sqp"], writes=["p6_vrow"])
            b.op("dve", lambda e: e.tensor_tensor(mrow6[:], srow6[:], srow6[:], ALU.mult),
                 reads=["p6_srow"], writes=["p6_mrow"])
            b.op("dve", lambda e: e.tensor_tensor(vrow6[:], vrow6[:], mrow6[:], ALU.subtract),
                 reads=["p6_vrow", "p6_mrow"], writes=["p6_vrow"])
            b.op("act", lambda e: e.activation(vrow6[:], vrow6[:], AF.Ln, bias=epsrow[:]),
                 reads=["p6_vrow", "epsrow"], writes=["p6_vrow"])
            b.op("act", lambda e: e.activation(rrow6[:], vrow6[:], AF.Exp, scale=-0.5),
                 reads=["p6_vrow"], writes=["p6_rrow"])
            b.op("dve", lambda e: e.tensor_tensor(mrow6[:], srow6[:], rrow6[:], ALU.mult),
                 reads=["p6_srow", "p6_rrow"], writes=["p6_mrow"])
            b.op("pe", lambda e: e.matmul(b6[:], consts["onesrow_f"][:], rrow6[:],
                                          start=True, stop=True),
                 reads=["c_onesrow_f", "p6_rrow"], writes=["p6_b6p"])
            b.op("act", lambda e: e.copy(rstdb6[:], b6[:]), reads=["p6_b6p"], writes=["p6_rstdb"])
            b.op("pe", lambda e: e.matmul(b6[:], consts["onesrow_f"][:], mrow6[:],
                                          start=True, stop=True),
                 reads=["c_onesrow_f", "p6_mrow"], writes=["p6_b6p"])
            b.op("act", lambda e: e.copy(mrsb6[:], b6[:]), reads=["p6_b6p"], writes=["p6_mrsb"])
            for oc in range(16):
                pp = oc % 2
                b.op("dve", lambda e, oc=oc, pp=pp: e.tensor_tensor(
                    oout[pp][:], r2_sb[:, oc * WIN:(oc + 1) * WIN], rstdb6[:], ALU.mult),
                    reads=[("p6_r2",), "p6_rstdb"], writes=[f"p6_o{pp}"])
                b.op("dve", lambda e, pp=pp: e.tensor_tensor(
                    oout[pp][:], oout[pp][:], mrsb6[:], ALU.subtract),
                    reads=[f"p6_o{pp}", "p6_mrsb"], writes=[f"p6_o{pp}"])
                b.dma("sp", f"p6o{pp}", lambda e, oc=oc, blk=blk, pp=pp: e.dma_start(
                    out=outT[oc * 128:(oc + 1) * 128, blk * WIN:(blk + 1) * WIN],
                    in_=oout[pp][:]),
                    reads=[f"p6_o{pp}"])

        b.barrier()
        p6ps.close()
        p6.close()


    if up_to < 1:
        pump(10000)
        b.barrier()
        p01.close()
    if up_to < 6:
        dzs = ExitStack()
        dal = mkalloc(dzs)
        zout = dal("dz_out", [128, WIN], F32)
        b.op("dve", lambda e: e.memset(zout[:], 0.0), writes=["dz_out"])
        for blk in range(NBLK):
            for oc in range(16):
                b.dma("sp", "dzo", lambda e, oc=oc, blk=blk: e.dma_start(
                    out=outT[oc * 128:(oc + 1) * 128, blk * WIN:(blk + 1) * WIN],
                    in_=zout[:]),
                    reads=["dz_out"])
        b.barrier()
        dzs.close()

    b.finalize(nc, root)
    root.close()
    return nc



def _host_inputs(inputs):
    x = np.asarray(inputs["L0_attn_x"])[0]
    xT_full = np.ascontiguousarray(x.T)
    shared = {
        "wqT": np.ascontiguousarray(np.asarray(inputs["L0_attn_Wq"]).T),
        "wkT": np.ascontiguousarray(np.asarray(inputs["L0_attn_Wk"]).T),
        "wvT": np.ascontiguousarray(np.asarray(inputs["L0_attn_Wv"]).T),
        "woT": np.ascontiguousarray(np.asarray(inputs["L0_attn_Wo"]).T),
        "wgT": np.ascontiguousarray(np.asarray(inputs["L0_ffn_w_gate"]).T),
        "wuT": np.ascontiguousarray(np.asarray(inputs["L0_ffn_w_up"]).T),
        "wdT": np.ascontiguousarray(np.asarray(inputs["L0_ffn_w_down"]).T),
    }
    shuf = np.zeros((HD, HD), dtype=np.float32)
    for i in range(HD // 2):
        shuf[2 * i, 2 * i + 1] = -1.0
        shuf[2 * i + 1, 2 * i] = 1.0
    shared["shufT"] = np.ascontiguousarray(shuf.T).astype(NPBF)
    jj = np.arange(128)[:, None]
    ii = np.arange(128)[None, :]
    shared["mprev"] = (jj > ii).astype(NPBF)
    shared["mcur"] = (jj <= ii).astype(NPBF)
    shared["ones_bf"] = np.ones((128, 1), dtype=NPBF)
    shared["onesrow_f"] = np.ones((1, 128), dtype=np.float32)
    shared["ident_f"] = np.eye(128, dtype=np.float32)

    invfreq = (1.0 / 10000.0 ** (np.arange(0, HD, 2, dtype=np.float32) / HD))
    freq_col = np.repeat(invfreq, 2)[:, None]

    in_maps = []
    for c in range(NCORES):
        g0 = c * S
        if c == 0:
            halo = np.zeros((D, WIN), dtype=np.float32)
        else:
            halo = xT_full[:, g0 - WIN:g0]
        xT_c = np.ascontiguousarray(
            np.concatenate([halo, xT_full[:, g0:g0 + S]], axis=1))
        pos = np.arange(g0 - WIN, g0 + S, dtype=np.float32)[None, :]
        ang = pos * freq_col
        m = dict(shared)
        m["xT"] = xT_c
        m["ropeC"] = np.cos(ang).astype(np.float32)
        m["ropeS"] = np.sin(ang).astype(np.float32)
        m["zcol"] = np.full((128, 1), 0.0 if c == 0 else 1.0, dtype=np.float32)
        in_maps.append(m)
    return in_maps


def kernel(**inputs):
    in_maps = _host_inputs(inputs)
    nc = build_program()
    res = run_bass_kernel_spmd(nc, in_maps, core_ids=list(range(NCORES)))
    parts = [np.asarray(r["outT"]).T for r in res.results]
    out = np.concatenate(parts, axis=0)[None]
    return np.ascontiguousarray(out, dtype=np.float32)


# version 09
# speedup vs baseline: 1.0049x; 1.0049x over previous
import numpy as np
import ml_dtypes

import concourse.bass as bass
import concourse.mybir as mybir
from concourse.bass_utils import run_bass_kernel_spmd
from concourse.mybir import (
    AluOpType as ALU,
    ActivationFunctionType as AF,
    AxisListType as AX,
)

F32 = mybir.dt.float32
BF16 = mybir.dt.bfloat16
I32 = mybir.dt.int32
NPBF = ml_dtypes.bfloat16

NCORES = 8
D = 2048
H = 16
HD = 128
WIN = 512
S = 2048
T = S + WIN
II = 8192
NBLK = S // WIN
EPS = 1e-5
C0 = 12.0
SCALE = HD ** -0.5

P1_LIMIT = 99
P1_ROPE = 3
ENG2ATTR = {"pe": "tensor", "act": "scalar", "dve": "vector",
            "pool": "gpsimd", "sp": "sync"}


class Builder:

    def __init__(self):
        self.prog = {e: [] for e in ENG2ATTR}
        self.counts = {}
        self.know = {e: {} for e in ENG2ATTR}
        self.last_writer = {}
        self.readers = {}
        self.groups = {}
        self.psum_keys = set()
        self.sems = []
        for e in ENG2ATTR:
            self._sem("c_" + e)

    def _sem(self, name):
        if name not in self.counts:
            self.sems.append(name)
            self.counts[name] = 0
        return name

    def _wait(self, eng, sem, val, tok_know):
        if sem.startswith("d_"):
            val = max(val, self.counts[sem])
        if self.know[eng].get(sem, 0) >= val:
            return
        if not (eng == "pe" and sem == "c_pe"):
            self.prog[eng].append(("wait", sem, val))
        k = self.know[eng]
        k[sem] = max(k.get(sem, 0), val)
        if tok_know:
            for s2, v2 in tok_know.items():
                if k.get(s2, 0) < v2:
                    k[s2] = v2

    def _deps(self, eng, reads, writes):
        need = {}

        def add(tok):
            if tok is None:
                return
            sem, val, kn = tok
            cur = need.get(sem)
            if cur is None or val > cur[0]:
                need[sem] = (val, kn)

        for r in reads:
            add(self.last_writer.get(r))
        for w in writes:
            add(self.last_writer.get(w))
            for tok in self.readers.get(w, ()):
                add(tok)
        for sem, (val, kn) in sorted(need.items()):
            self._wait(eng, sem, val, kn)

    def _attach(self, tok, rw_list):
        for rds, wrs in rw_list:
            for w in wrs:
                self.last_writer[w] = tok
                self.readers[w] = []
            for r in rds:
                self.readers.setdefault(r, []).append(tok)

    def reg_psum(self, *keys):
        self.psum_keys.update(keys)

    def op(self, eng, fn, reads=(), writes=(), group=None, final=True):
        locks = tuple(("rdl", r) for r in reads if r in self.psum_keys)
        writes = tuple(writes) + locks
        self._deps(eng, reads, writes)
        if group is not None:
            gk = (eng, group)
            self.groups.setdefault(gk, []).append((tuple(reads), tuple(writes)))
            if not final:
                self.prog[eng].append(("inst", fn, None, 0))
                return None
            sem = "c_" + eng
            self.counts[sem] += 1
            tok = (sem, self.counts[sem], dict(self.know[eng]))
            self.prog[eng].append(("inst", fn, sem, self.counts[sem]))
            self._attach(tok, self.groups.pop(gk))
            return tok
        sem = "c_" + eng
        self.counts[sem] += 1
        tok = (sem, self.counts[sem], dict(self.know[eng]))
        self.prog[eng].append(("inst", fn, sem, self.counts[sem]))
        self._attach(tok, [(tuple(reads), tuple(writes))])
        return tok

    def dma(self, eng, stream, fn, reads=(), writes=()):
        self._deps(eng, reads, writes)
        sem = self._sem("d_" + stream)
        self.counts[sem] += 16
        tok = (sem, self.counts[sem], dict(self.know[eng]))
        self.prog[eng].append(("dma", fn, sem, self.counts[sem]))
        self._attach(tok, [(tuple(reads), tuple(writes))])
        return tok

    def barrier(self):
        for eng in ENG2ATTR:
            for sem in self.sems:
                if self.counts[sem] > 0:
                    self._wait(eng, sem, self.counts[sem], None)

    def finalize(self, nc, es):
        sem_handles = {name: es.enter_context(nc.semaphore(name))
                       for name in self.sems}
        out_waits = [(s, self.counts[s]) for s in self.sems
                     if s.startswith("d_") and self.counts[s] > 0]
        block = es.enter_context(nc.Block())
        for eng, attr in ENG2ATTR.items():
            items = self.prog[eng]

            def body(e, items=items, eng=eng):
                for it in items:
                    if it[0] == "wait":
                        e.wait_ge(sem_handles[it[1]], it[2])
                    else:
                        _, fn, sem, val = it
                        inst = fn(e)
                        if sem is not None:
                            inst.then_inc(sem_handles[sem],
                                          16 if it[0] == "dma" else 1)
                if eng == "sp":
                    for sname, cnt in out_waits:
                        e.wait_ge(sem_handles[sname], cnt)
            getattr(block, attr)(body)


def build_program(up_to=99):
    nc = bass.Bass("TRN2", target_bir_lowering=False, debug=False)
    b = Builder()
    b.reg_psum("p0_tpp", "p0_bcp",
               "p1_qps0", "p1_qps1", "p1_sps0", "p1_sps1", "p1_vps0", "p1_vps1",
               "p2_scp0", "p2_scp1", "p2_ovp0", "p2_ovp1", "p2_smp0", "p2_smp1",
               "p2_bcpp0", "p2_bcpp1",
               "p3_prp0", "p3_prp1", "p3_stp", "p3_sqp", "p3_b3p", "p3_trp", "p3_mip",
               "p4_xbp", "p4_gp0", "p4_gp1", "p4_up0", "p4_up1", "p4_trp", "p4_mip",
               *[f"p5_dnp{i}" for i in range(8)],
               "p6_stp", "p6_sqp", "p6_b6p")
    from contextlib import ExitStack
    root = ExitStack()

    def ein(name, shape, dt):
        return nc.dram_tensor(name, shape, dt, kind="ExternalInput").ap()

    xT = ein("xT", [D, T], F32)
    wqT = ein("wqT", [D, D], F32)
    wkT = ein("wkT", [D, D], F32)
    wvT = ein("wvT", [D, D], F32)
    woT = ein("woT", [D, D], F32)
    wgT = ein("wgT", [D, II], F32)
    wuT = ein("wuT", [D, II], F32)
    wdT = ein("wdT", [II, D], F32)
    ropeC = ein("ropeC", [HD, T], F32)
    ropeS = ein("ropeS", [HD, T], F32)
    shufT = ein("shufT", [HD, HD], BF16)
    mprev_i = ein("mprev", [128, 128], BF16)
    mcur_i = ein("mcur", [128, 128], BF16)
    zcol_i = ein("zcol", [128, 1], F32)
    ones_bf_i = ein("ones_bf", [128, 1], BF16)
    onesrow_i = ein("onesrow_f", [1, 128], F32)
    ident_i = ein("ident_f", [128, 128], F32)

    outT = nc.dram_tensor("outT", [D, S], F32, kind="ExternalOutput").ap()

    def scr(name, shape, dt):
        return nc.dram_tensor(name, shape, dt, kind="Internal").ap()

    qrot_h = scr("qrot_h", [D, S], BF16)
    krot_h = scr("krot_h", [D, T], BF16)
    v_h = scr("v_h", [T, D], BF16)
    ofm_h = scr("ofm_h", [D, S], BF16)
    r1_h = scr("r1_h", [D, S], F32)
    xq_h = scr("xq_h", [D, S], BF16)
    h_h = scr("h_h", [II, S], F32)
    wqd_h = scr("wqd_h", [II, D], BF16)
    wqg_h = scr("wqg_h", [128, 16, II], BF16)
    wqu_h = scr("wqu_h", [128, 16, II], BF16)
    hq_h = scr("hq_h", [II, S], BF16)
    r2_h = scr("r2_h", [D, S], F32)
    xs_h = scr("xs_h", [1, S], F32)
    s2i_h = scr("s2i_h", [1, S], F32)
    dsc_h = scr("dsc_h", [1, S], F32)

    def mkalloc(es):
        def alloc(name, shape, dt):
            return es.enter_context(nc.sbuf_tensor(name, shape, dt))
        return alloc

    pal = mkalloc(root)
    consts = {}
    for nm, src_ap, shape, dt in [
        ("shufT", shufT, [HD, HD], BF16), ("mprev", mprev_i, [128, 128], BF16),
        ("mcur", mcur_i, [128, 128], BF16), ("zcol", zcol_i, [128, 1], F32),
        ("ones_bf", ones_bf_i, [128, 1], BF16),
        ("onesrow_f", onesrow_i, [1, 128], F32),
        ("ident_f", ident_i, [128, 128], F32),
    ]:
        t = pal("c_" + nm, shape, dt)
        b.dma("sp", "cst", lambda e, t=t, s=src_ap: e.dma_start(out=t[:], in_=s),
              writes=["c_" + nm])
        consts[nm] = t
    cbias = pal("cbias", [128, 1], F32)
    epsrow = pal("epsrow", [1, 1], F32)
    b.op("dve", lambda e: e.memset(cbias[:], -C0), writes=["cbias"])
    b.op("dve", lambda e: e.memset(epsrow[:], EPS), writes=["epsrow"])
    wscol = {m: pal(f"wscol_{m}", [128, 1], F32) for m in "gud"}
    taucol = {m: pal(f"taucol_{m}", [128, 1], F32) for m in "gud"}
    ntaucol = {m: pal(f"ntaucol_{m}", [128, 1], F32) for m in "gud"}
    wssc = {m: pal(f"wssc_{m}", [1, 1], F32) for m in "gud"}

    p01 = ExitStack()
    al = mkalloc(p01)
    wt0 = [al(f"p0_wt{i}", [128, 1024], F32) for i in range(2)]
    redc = al("p0_redc", [128, 128], F32)
    lo_t = al("p0_lo", [128, 128], F32)
    ts_t = al("p0_ts", [128, 64], F32)
    tb_t = al("p0_tb", [128, 64], F32)
    te_t = al("p0_te", [128, 64], F32)
    hrow = al("p0_hrow", [1, 128], F32)
    lrow = al("p0_lrow", [1, 128], F32)
    trow = [al(f"p0_trow{i}", [1, 64], F32) for i in range(3)]
    tp_ps = p01.enter_context(nc.psum_tensor("p0_tp", [128, 128], F32))
    bc_ps = p01.enter_context(nc.psum_tensor("p0_bc", [128, 1], F32))

    def twosum_level(hi_ap, lo_ap, hik, lok, n, ts, tbb, te, tsk, tbk, tek):
        h2 = n // 2
        A, B = hi_ap[:, 0:h2], hi_ap[:, h2:n]
        b.op("dve", lambda e: e.tensor_tensor(ts[:, 0:h2], A, B, ALU.add),
             reads=[hik], writes=[tsk])
        b.op("dve", lambda e: e.tensor_tensor(tbb[:, 0:h2], ts[:, 0:h2], A, ALU.subtract),
             reads=[tsk, hik], writes=[tbk])
        b.op("dve", lambda e: e.tensor_tensor(te[:, 0:h2], ts[:, 0:h2], tbb[:, 0:h2], ALU.subtract),
             reads=[tsk, tbk], writes=[tek])
        b.op("dve", lambda e: e.tensor_tensor(te[:, 0:h2], A, te[:, 0:h2], ALU.subtract),
             reads=[hik, tek], writes=[tek])
        b.op("dve", lambda e: e.tensor_tensor(tbb[:, 0:h2], B, tbb[:, 0:h2], ALU.subtract),
             reads=[hik, tbk], writes=[tbk])
        b.op("dve", lambda e: e.tensor_tensor(te[:, 0:h2], te[:, 0:h2], tbb[:, 0:h2], ALU.add),
             reads=[tek, tbk], writes=[tek])
        b.op("dve", lambda e: e.tensor_tensor(tbb[:, 0:h2], lo_ap[:, 0:h2], lo_ap[:, h2:n], ALU.add),
             reads=[lok], writes=[tbk])
        b.op("dve", lambda e: e.tensor_tensor(lo_ap[:, 0:h2], tbb[:, 0:h2], te[:, 0:h2], ALU.add),
             reads=[tbk, tek], writes=[lok])
        b.op("dve", lambda e: e.tensor_copy(hi_ap[:, 0:h2], ts[:, 0:h2]),
             reads=[tsk], writes=[hik])

    def gen_p0a():
        for m, wT_ap, rows in (("g", wgT, D), ("u", wuT, D), ("d", wdT, II)):
            cols = II if rows == D else D
            ti = 0
            for r0 in range(0, rows, 128):
                for c0 in range(0, cols, 1024):
                    sl = ti % 2
                    col = ti % 128
                    if ti >= 128:
                        raise AssertionError("tile count per matrix must be 128")
                    b.dma("sp", f"p0w{sl}",
                          lambda e, sl=sl, r0=r0, c0=c0, wT_ap=wT_ap:
                          e.dma_start(out=wt0[sl][:], in_=wT_ap[r0:r0 + 128, c0:c0 + 1024]),
                          writes=[f"p0_wt{sl}"])
                    b.op("dve", lambda e, sl=sl, col=col: e.tensor_reduce(
                        redc[:, col:col + 1], wt0[sl][:], AX.X, ALU.add,
                        apply_absolute_value=True),
                        reads=[f"p0_wt{sl}"], writes=["p0_redc"])
                    ti += 1
                    yield
            b.op("pool", lambda e: e.memset(lo_t[:], 0.0), writes=["p0_lo"])
            yield
            n = 128
            while n > 1:
                twosum_level(redc, lo_t, "p0_redc", "p0_lo", n, ts_t, tb_t, te_t,
                             "p0_ts", "p0_tb", "p0_te")
                n //= 2
                yield
            b.op("pe", lambda e: e.transpose(tp_ps[0:1, 0:128], redc[:, 0:1],
                                             consts["ident_f"][:]),
                 reads=["p0_redc", "c_ident_f"], writes=["p0_tpp"])
            b.op("act", lambda e: e.copy(hrow[:], tp_ps[0:1, 0:128]),
                 reads=["p0_tpp"], writes=["p0_hrow"])
            b.op("pe", lambda e: e.transpose(tp_ps[0:1, 0:128], lo_t[:, 0:1],
                                             consts["ident_f"][:]),
                 reads=["p0_lo", "c_ident_f", "p0_hrow"], writes=["p0_tpp"])
            b.op("act", lambda e: e.copy(lrow[:], tp_ps[0:1, 0:128]),
                 reads=["p0_tpp"], writes=["p0_lrow"])
            yield
            n = 128
            while n > 1:
                twosum_level(hrow, lrow, "p0_hrow", "p0_lrow", n,
                             trow[0], trow[1], trow[2],
                             "p0_tr0", "p0_tr1", "p0_tr2")
                n //= 2
                yield
            b.op("dve", lambda e, m=m: e.tensor_tensor(wssc[m][:], hrow[0:1, 0:1],
                                                       lrow[0:1, 0:1], ALU.add),
                 reads=["p0_hrow", "p0_lrow"], writes=[f"wssc_{m}"])
            inv_n = 1.0 / float(rows * cols)
            b.op("dve", lambda e, m=m, inv_n=inv_n: e.tensor_scalar(
                wssc[m][:], wssc[m][:], inv_n, EPS, ALU.mult, ALU.add),
                reads=[f"wssc_{m}"], writes=[f"wssc_{m}"])
            b.op("pe", lambda e, m=m: e.matmul(bc_ps[:], consts["onesrow_f"][:],
                                               wssc[m][:], start=True, stop=True),
                 reads=["c_onesrow_f", f"wssc_{m}"], writes=["p0_bcp"])
            b.op("act", lambda e, m=m: e.copy(wscol[m][:], bc_ps[:]),
                 reads=["p0_bcp"], writes=[f"wscol_{m}"])
            b.op("dve", lambda e, m=m: e.tensor_scalar(taucol[m][:], wscol[m][:], 0.5, None, ALU.mult),
                 reads=[f"wscol_{m}"], writes=[f"taucol_{m}"])
            b.op("dve", lambda e, m=m: e.tensor_scalar(ntaucol[m][:], wscol[m][:], -0.5, None, ALU.mult),
                 reads=[f"wscol_{m}"], writes=[f"ntaucol_{m}"])
            yield

    _p0_iter = gen_p0a()
    _p0_done = [False]

    def pump(k=1):
        for _ in range(k):
            if not _p0_done[0]:
                try:
                    next(_p0_iter)
                except StopIteration:
                    _p0_done[0] = True

    if up_to >= 1:
        p1 = ExitStack()
        al = mkalloc(p1)
        xfm = al("p1_xfm", [128, 16 * T], BF16)
        xst = [al(f"p1_xst{i}", [128, T], F32) for i in range(2)]
        ctab = al("p1_ctab", [HD, T], F32)
        stab = al("p1_stab", [HD, T], F32)
        b.dma("sp", "p1c", lambda e: e.dma_start(out=ctab[:], in_=ropeC), writes=["p1_ctab"])
        b.dma("sp", "p1c", lambda e: e.dma_start(out=stab[:], in_=ropeS), writes=["p1_stab"])
        for c in range(16):
            sl = c % 2
            b.dma("sp", f"p1x{sl}", lambda e, sl=sl, c=c: e.dma_start(
                out=xst[sl][:], in_=xT[c * 128:(c + 1) * 128, :]), writes=[f"p1_xst{sl}"])
            b.op("dve", lambda e, sl=sl, c=c: e.tensor_copy(
                xfm[:, c * T:(c + 1) * T], xst[sl][:]),
                reads=[f"p1_xst{sl}"], writes=[("p1_xfm", c)])

        whalf = al("p1_whalf", [128, 16 * 1024], BF16)
        wst = [al(f"p1_wst{i}", [128, 1024], F32) for i in range(2)]
        p1ps = ExitStack()
        q_ps = [p1ps.enter_context(nc.psum_tensor(f"p1_qps{i}", [128, 512], F32)) for i in range(2)]
        s_ps = [p1ps.enter_context(nc.psum_tensor(f"p1_sps{i}", [128, 512], F32)) for i in range(2)]
        v_ps = [p1ps.enter_context(nc.psum_tensor(f"p1_vps{i}", [128, 512], F32)) for i in range(2)]
        qsb = [al(f"p1_qsb{i}", [128, 512], BF16) for i in range(2)]
        t1 = [al(f"p1_t1{i}", [128, 512], F32) for i in range(2)]
        t2 = [al(f"p1_t2{i}", [128, 512], F32) for i in range(2)]
        rout = [al(f"p1_rout{i}", [128, 512], BF16) for i in range(2)]
        vout = [al(f"p1_vout{i}", [128, 512], BF16) for i in range(2)]

        def load_whalf(wT_ap, half, scale, wkey="p1_wh"):
            for ic in range(16):
                sl = ic % 2
                b.dma("sp", f"p1w{sl}", lambda e, ic=ic, sl=sl, half=half, wT_ap=wT_ap:
                      e.dma_start(out=wst[sl][:],
                                  in_=wT_ap[ic * 128:(ic + 1) * 128,
                                            half * 1024:(half + 1) * 1024]),
                      writes=[f"p1_wst{sl}"])
                if scale is None:
                    b.op("dve", lambda e, ic=ic, sl=sl: e.tensor_copy(
                        whalf[:, ic * 1024:(ic + 1) * 1024], wst[sl][:]),
                        reads=[f"p1_wst{sl}"], writes=[(wkey, ic)])
                else:
                    b.op("dve", lambda e, ic=ic, sl=sl, scale=scale: e.tensor_scalar(
                        whalf[:, ic * 1024:(ic + 1) * 1024], wst[sl][:], scale, None, ALU.mult),
                        reads=[f"p1_wst{sl}"], writes=[(wkey, ic)])

        def qk_sweep(dst_h, wT_ap, scale, nblocks, blk_off, wkey="p1_wh"):
            for half in range(2 if P1_LIMIT >= 5 else 1):
                load_whalf(wT_ap, half, scale)
                for blk in range(nblocks if P1_LIMIT >= 5 else 1):
                    tb = blk_off + blk * WIN
                    for ol in range(8):
                        pump(2)
                        oc = half * 8 + ol
                        pp = (blk * 8 + ol) % 2
                        grp = f"qk{oc}_{blk}"
                        for ic in range(16):
                            b.op("pe", lambda e, ic=ic, ol=ol, tb=tb, pp=pp:
                                 e.matmul(q_ps[pp][:],
                                          whalf[:, ic * 1024 + ol * 128: ic * 1024 + (ol + 1) * 128],
                                          xfm[:, ic * T + tb: ic * T + tb + WIN],
                                          start=(ic == 0), stop=(ic == 15)),
                                 reads=[(wkey, ic), ("p1_xfm", ic)],
                                 writes=[f"p1_qps{pp}"], group=grp, final=(ic == 15))
                        if P1_ROPE == 0:
                            b.op("act", lambda e, pp=pp: e.copy(rout[pp][:], q_ps[pp][:]),
                                 reads=[f"p1_qps{pp}"], writes=[f"p1_rout{pp}"])
                        elif P1_ROPE == 1:
                            b.op("act", lambda e, pp=pp: e.copy(qsb[pp][:], q_ps[pp][:]),
                                 reads=[f"p1_qps{pp}"], writes=[f"p1_qsb{pp}"])
                            b.op("pe", lambda e, pp=pp: e.matmul(s_ps[pp][:], consts["shufT"][:],
                                                                 qsb[pp][:], start=True, stop=True),
                                 reads=["c_shufT", f"p1_qsb{pp}"], writes=[f"p1_sps{pp}"])
                            b.op("act", lambda e, pp=pp: e.copy(rout[pp][:], s_ps[pp][:]),
                                 reads=[f"p1_sps{pp}"], writes=[f"p1_rout{pp}"])
                        elif P1_ROPE == 2:
                            b.op("dve", lambda e, pp=pp, tb=tb: e.tensor_tensor(
                                t1[pp][:], q_ps[pp][:], ctab[:, tb:tb + WIN], ALU.mult),
                                reads=[f"p1_qps{pp}", "p1_ctab"], writes=[f"p1_t1{pp}"])
                            b.op("dve", lambda e, pp=pp: e.tensor_copy(rout[pp][:], t1[pp][:]),
                                 reads=[f"p1_t1{pp}"], writes=[f"p1_rout{pp}"])
                        elif P1_ROPE == 4:
                            b.op("act", lambda e, pp=pp: e.copy(qsb[pp][:], q_ps[pp][:]),
                                 reads=[f"p1_qps{pp}"], writes=[f"p1_qsb{pp}"])
                            b.op("pe", lambda e, pp=pp: e.matmul(s_ps[pp][:], consts["shufT"][:],
                                                                 qsb[pp][:], start=True, stop=True),
                                 reads=["c_shufT", f"p1_qsb{pp}"], writes=[f"p1_sps{pp}"])
                            b.op("dve", lambda e, pp=pp, tb=tb: e.tensor_tensor(
                                t1[pp][:], q_ps[pp][:], ctab[:, tb:tb + WIN], ALU.mult),
                                reads=[f"p1_qps{pp}", "p1_ctab"], writes=[f"p1_t1{pp}"])
                            b.op("dve", lambda e, pp=pp, tb=tb: e.tensor_tensor(
                                t2[pp][:], s_ps[pp][:], stab[:, tb:tb + WIN], ALU.mult),
                                reads=[f"p1_sps{pp}", "p1_stab"], writes=[f"p1_t2{pp}"])
                            b.op("dve", lambda e, pp=pp: e.tensor_tensor(
                                rout[pp][:], t1[pp][:], t2[pp][:], ALU.add),
                                reads=[f"p1_t1{pp}", f"p1_t2{pp}"], writes=[f"p1_rout{pp}"])
                        elif P1_ROPE == 5:
                            b.op("act", lambda e, pp=pp: e.copy(qsb[pp][:], q_ps[pp][:]),
                                 reads=[f"p1_qps{pp}"], writes=[f"p1_qsb{pp}"])
                            b.op("pe", lambda e, pp=pp: e.matmul(s_ps[pp][:], consts["shufT"][:],
                                                                 qsb[pp][:], start=True, stop=True),
                                 reads=["c_shufT", f"p1_qsb{pp}"], writes=[f"p1_sps{pp}"])
                            b.op("dve", lambda e, pp=pp, tb=tb: e.tensor_tensor(
                                t2[pp][:], s_ps[pp][:], stab[:, tb:tb + WIN], ALU.mult),
                                reads=[f"p1_sps{pp}", "p1_stab"], writes=[f"p1_t2{pp}"])
                            b.op("dve", lambda e, pp=pp: e.tensor_copy(rout[pp][:], t2[pp][:]),
                                 reads=[f"p1_t2{pp}"], writes=[f"p1_rout{pp}"])
                        else:
                            b.op("act", lambda e, pp=pp: e.copy(qsb[pp][:], q_ps[pp][:]),
                                 reads=[f"p1_qps{pp}"], writes=[f"p1_qsb{pp}"])
                            b.op("pe", lambda e, pp=pp: e.matmul(s_ps[pp][:], consts["shufT"][:],
                                                                 qsb[pp][:], start=True, stop=True),
                                 reads=["c_shufT", f"p1_qsb{pp}"], writes=[f"p1_sps{pp}"])
                            b.op("dve", lambda e, pp=pp, tb=tb: e.tensor_tensor(
                                t1[pp][:], q_ps[pp][:], ctab[:, tb:tb + WIN], ALU.mult),
                                reads=[f"p1_qps{pp}", "p1_ctab"], writes=[f"p1_t1{pp}"])
                            b.op("dve", lambda e, pp=pp, tb=tb: e.tensor_tensor(
                                t2[pp][:], s_ps[pp][:], stab[:, tb:tb + WIN], ALU.mult),
                                reads=[f"p1_sps{pp}", "p1_stab"], writes=[f"p1_t2{pp}"])
                            b.op("dve", lambda e, pp=pp: e.tensor_tensor(
                                rout[pp][:], t1[pp][:], t2[pp][:], ALU.add),
                                reads=[f"p1_t1{pp}", f"p1_t2{pp}"], writes=[f"p1_rout{pp}"])
                        b.dma("act", f"p1o{pp}",
                              lambda e, pp=pp, oc=oc, tb=tb, dst_h=dst_h, blk_off=blk_off:
                              e.dma_start(out=dst_h[oc * 128:(oc + 1) * 128,
                                                    tb - blk_off: tb - blk_off + WIN],
                                          in_=rout[pp][:]),
                              reads=[f"p1_rout{pp}"])

        if P1_LIMIT >= 2:
            qk_sweep(qrot_h, wqT, SCALE, NBLK, WIN)
        if P1_LIMIT >= 3:
            qk_sweep(krot_h, wkT, None, NBLK + 1, 0)

        for half in (range(2) if P1_LIMIT >= 4 else []):
            load_whalf(wvT, half, None)
            for blk in range(NBLK + 1):
                for tt in range(4):
                    tb = blk * WIN + tt * 128
                    for og in range(2):
                        pump(2)
                        oc0 = half * 1024 + og * 512
                        pp = (tt * 2 + og) % 2
                        grp = f"v{blk}_{tt}_{og}"
                        for ic in range(16):
                            b.op("pe", lambda e, ic=ic, tb=tb, og=og, pp=pp:
                                 e.matmul(v_ps[pp][:],
                                          xfm[:, ic * T + tb: ic * T + tb + 128],
                                          whalf[:, ic * 1024 + og * 512: ic * 1024 + (og + 1) * 512],
                                          start=(ic == 0), stop=(ic == 15)),
                                 reads=[("p1_wh", ic), ("p1_xfm", ic)],
                                 writes=[f"p1_vps{pp}"], group=grp, final=(ic == 15))
                        b.op("act", lambda e, pp=pp: e.copy(vout[pp][:], v_ps[pp][:]),
                             reads=[f"p1_vps{pp}"], writes=[f"p1_vout{pp}"])
                        b.dma("act", f"p1v{pp}", lambda e, pp=pp, tb=tb, oc0=oc0:
                              e.dma_start(out=v_h[tb:tb + 128, oc0:oc0 + 512], in_=vout[pp][:]),
                              reads=[f"p1_vout{pp}"])

        pump(10000)
        b.barrier()
        p1ps.close()
        p1.close()
        p01.close()


    if up_to >= 2:
        p2 = ExitStack()
        al = mkalloc(p2)
        qblk = al("p2_q", [128, 16 * WIN], BF16)
        kbuf = [al(f"p2_k{i}", [128, 16 * WIN], BF16) for i in range(2)]
        vbuf = [al(f"p2_v{i}", [128, 4 * D], BF16) for i in range(2)]
        ptl = [al(f"p2_pt{i}", [128, WIN], BF16) for i in range(2)]
        bcs = [al(f"p2_bc{i}", [128, WIN], F32) for i in range(2)]
        rec = [al(f"p2_rec{i}", [1, WIN], F32) for i in range(2)]
        oev = [al(f"p2_oev{i}", [128, WIN], BF16) for i in range(2)]
        wdst = [al(f"p2_wd{i}", [128, 2048], F32) for i in range(2)]
        wqd_sb = [al(f"p2_wqd{i}", [128, 2048], BF16) for i in range(2)]
        qa2 = al("p2_qa", [128, 2048], BF16)
        qb2q = al("p2_qb", [128, 2048], BF16)
        zr512 = al("p2_zr", [128, WIN], BF16)
        b.op("pool", lambda e: e.memset(zr512[:], 0.0), writes=["p2_zr"])
        p2ps = ExitStack()
        sc_ps = [p2ps.enter_context(nc.psum_tensor(f"p2_sc{i}", [128, WIN], F32)) for i in range(2)]
        ov_ps = [p2ps.enter_context(nc.psum_tensor(f"p2_ov{i}", [128, WIN], F32)) for i in range(2)]
        sm_ps = [p2ps.enter_context(nc.psum_tensor(f"p2_sm{i}", [1, WIN], F32)) for i in range(2)]
        bc_ps2 = [p2ps.enter_context(nc.psum_tensor(f"p2_bcp{i}", [128, WIN], F32)) for i in range(2)]

        def load_kv(blk, slot):
            for c in range(16):
                b.dma("sp", f"p2k{slot}", lambda e, c=c, blk=blk, slot=slot: e.dma_start(
                    out=kbuf[slot][:, c * WIN:(c + 1) * WIN],
                    in_=krot_h[c * 128:(c + 1) * 128, blk * WIN:(blk + 1) * WIN]),
                    writes=[(f"p2_k{slot}",)])
            for tt in range(4):
                b.dma("sp", f"p2v{slot}", lambda e, tt=tt, blk=blk, slot=slot: e.dma_start(
                    out=vbuf[slot][:, tt * D:(tt + 1) * D],
                    in_=v_h[blk * WIN + tt * 128: blk * WIN + (tt + 1) * 128, :]),
                    writes=[(f"p2_v{slot}",)])

        load_kv(0, 0)
        for blk in range(NBLK):
            cur = (blk + 1) % 2
            prv = blk % 2
            load_kv(blk + 1, cur)
            for c in range(16):
                b.dma("act", "p2q", lambda e, c=c, blk=blk: e.dma_start(
                    out=qblk[:, c * WIN:(c + 1) * WIN],
                    in_=qrot_h[c * 128:(c + 1) * 128, blk * WIN:(blk + 1) * WIN]),
                    writes=[("p2_q",)])
            def emit_wq_chunk(kind, c):
                sl = c % 2
                if kind == "d":
                    b.dma("sp", f"p2wd{sl}", lambda e, c=c, sl=sl: e.dma_start(
                        out=wdst[sl][:], in_=wdT[c * 128:(c + 1) * 128, :]),
                        writes=[f"p2_wd{sl}"])
                else:
                    rt, cq = c % 16, c // 16
                    b.dma("sp", f"p2wd{sl}", lambda e, rt=rt, cq=cq, sl=sl: e.dma_start(
                        out=wdst[sl][:], in_=wgT[rt * 128:(rt + 1) * 128,
                                                 cq * 2048:(cq + 1) * 2048]),
                        writes=[f"p2_wd{sl}"])
                b.op("dve", lambda e, sl=sl, kind=kind: e.tensor_scalar(
                    qa2[:], wdst[sl][:], taucol[kind][:], None, ALU.is_gt),
                    reads=[f"p2_wd{sl}", f"taucol_{kind}"], writes=["p2_qa"])
                b.op("dve", lambda e, sl=sl, kind=kind: e.tensor_scalar(
                    qb2q[:], wdst[sl][:], ntaucol[kind][:], None, ALU.is_lt),
                    reads=[f"p2_wd{sl}", f"ntaucol_{kind}"], writes=["p2_qb"])
                b.op("dve", lambda e, sl=sl: e.tensor_tensor(
                    wqd_sb[sl][:], qa2[:], qb2q[:], ALU.subtract),
                    reads=["p2_qa", "p2_qb"], writes=[f"p2_wqd{sl}"])
                if kind == "d":
                    b.dma("act", f"p2wdo{sl}", lambda e, c=c, sl=sl: e.dma_start(
                        out=wqd_h[c * 128:(c + 1) * 128, :], in_=wqd_sb[sl][:]),
                        reads=[f"p2_wqd{sl}"])
                else:
                    rt, cq = c % 16, c // 16
                    b.dma("act", f"p2wdo{sl}", lambda e, rt=rt, cq=cq, sl=sl: e.dma_start(
                        out=wqg_h[:, rt:rt + 1, cq * 2048:(cq + 1) * 2048],
                        in_=wqd_sb[sl][:]),
                        reads=[f"p2_wqd{sl}"])

            for hh in range(H):
                emit_wq_chunk("d", blk * 16 + hh)
                emit_wq_chunk("g", blk * 16 + hh)
                pp = hh % 2
                b.op("pe", lambda e, pp=pp: e.matmul(
                    sm_ps[pp][:], consts["ones_bf"][:], zr512[:],
                    start=True, stop=False),
                    reads=["c_ones_bf", "p2_zr"], writes=[f"p2_smp{pp}"])
                b.op("pe", lambda e, pp=pp: e.matmul(
                    ov_ps[pp][:], consts["mprev"][:], zr512[:],
                    start=True, stop=False),
                    reads=["c_mprev", "p2_zr"], writes=[f"p2_ovp{pp}"])
                for jc in range(8):
                    if jc < 4:
                        ksl, kc = prv, jc
                        q0, q1 = 0, (jc + 1) * 128
                        dg = jc * 128
                        mk, mkk = consts["mprev"], "c_mprev"
                    else:
                        ksl, kc = cur, jc - 4
                        q0, q1 = (jc - 4) * 128, WIN
                        dg = (jc - 4) * 128
                        mk, mkk = consts["mcur"], "c_mcur"
                    qw = q1 - q0
                    pj = jc % 2
                    b.op("pe", lambda e, pj=pj, ksl=ksl, kc=kc, hh=hh, q0=q0, q1=q1, qw=qw:
                         e.matmul(sc_ps[pj][:, 0:qw],
                                  kbuf[ksl][:, hh * WIN + kc * 128: hh * WIN + (kc + 1) * 128],
                                  qblk[:, hh * WIN + q0: hh * WIN + q1],
                                  start=True, stop=True),
                         reads=[(f"p2_k{ksl}",), ("p2_q",)], writes=[f"p2_scp{pj}"])
                    b.op("act", lambda e, pj=pj, qw=qw: e.activation(
                        ptl[pj][:, 0:qw], sc_ps[pj][:, 0:qw], AF.Exp, bias=cbias[:]),
                        reads=[f"p2_scp{pj}", "cbias"], writes=[f"p2_pt{pj}"])
                    b.op("dve", lambda e, pj=pj, dg=dg, q0=q0, mk=mk: e.tensor_tensor(
                        ptl[pj][:, dg - q0: dg - q0 + 128],
                        ptl[pj][:, dg - q0: dg - q0 + 128], mk[:], ALU.mult),
                        reads=[f"p2_pt{pj}", mkk], writes=[f"p2_pt{pj}"])
                    if blk == 0 and jc < 4:
                        b.op("dve", lambda e, pj=pj, qw=qw: e.tensor_scalar(
                            ptl[pj][:, 0:qw], ptl[pj][:, 0:qw], consts["zcol"][:], None, ALU.mult),
                            reads=[f"p2_pt{pj}", "c_zcol"], writes=[f"p2_pt{pj}"])
                    segs = []
                    if jc < 4:
                        segs.append((0, (jc + 1) * 128, False))
                    else:
                        segs.append((q0, WIN, False))
                    vsl = prv if jc < 4 else cur
                    vtt = kc
                    last = (jc == 7)
                    for si, (a0, a1, st) in enumerate(segs):
                        fin = last and si == len(segs) - 1
                        b.op("pe", lambda e, pp=pp, pj=pj, a0=a0, a1=a1, q0=q0, fin=fin:
                             e.matmul(sm_ps[pp][:, a0:a1], consts["ones_bf"][:],
                                      ptl[pj][:, a0 - q0:a1 - q0], start=False, stop=fin),
                             reads=["c_ones_bf", f"p2_pt{pj}"],
                             writes=[f"p2_smp{pp}"])
                        b.op("pe", lambda e, pp=pp, pj=pj, a0=a0, a1=a1, q0=q0,
                             vsl=vsl, vtt=vtt, hh=hh, fin=fin:
                             e.matmul(ov_ps[pp][:, a0:a1],
                                      vbuf[vsl][:, vtt * D + hh * 128: vtt * D + (hh + 1) * 128],
                                      ptl[pj][:, a0 - q0:a1 - q0], start=False, stop=fin),
                             reads=[(f"p2_v{vsl}",), f"p2_pt{pj}"],
                             writes=[f"p2_ovp{pp}"])
                b.op("dve", lambda e, pp=pp: e.reciprocal(rec[pp][:], sm_ps[pp][:]),
                     reads=[f"p2_smp{pp}"], writes=[f"p2_rec{pp}"])
                b.op("pe", lambda e, pp=pp: e.matmul(bc_ps2[pp][:], consts["onesrow_f"][:],
                                                     rec[pp][:], start=True, stop=True),
                     reads=["c_onesrow_f", f"p2_rec{pp}"], writes=[f"p2_bcpp{pp}"])
                b.op("act", lambda e, pp=pp: e.copy(bcs[pp][:], bc_ps2[pp][:]),
                     reads=[f"p2_bcpp{pp}"], writes=[f"p2_bc{pp}"])
                b.op("dve", lambda e, pp=pp: e.tensor_tensor(
                    oev[pp][:], ov_ps[pp][:], bcs[pp][:], ALU.mult),
                    reads=[f"p2_ovp{pp}", f"p2_bc{pp}"], writes=[f"p2_oev{pp}"])
                b.dma("act", f"p2o{pp}", lambda e, pp=pp, hh=hh, blk=blk: e.dma_start(
                    out=ofm_h[hh * 128:(hh + 1) * 128, blk * WIN:(blk + 1) * WIN],
                    in_=oev[pp][:]),
                    reads=[f"p2_oev{pp}"])
        b.barrier()
        p2ps.close()
        p2.close()


    if up_to >= 3:
        p3 = ExitStack()
        al = mkalloc(p3)
        wo_sb = al("p3_wo", [128, 16 * D], BF16)
        wst3 = [al(f"p3_wst{i}", [128, 1024], F32) for i in range(2)]
        for ic in range(16):
            for hf in range(2):
                sl = (ic * 2 + hf) % 2
                b.dma("sp", f"p3w{sl}", lambda e, ic=ic, hf=hf, sl=sl: e.dma_start(
                    out=wst3[sl][:],
                    in_=woT[ic * 128:(ic + 1) * 128, hf * 1024:(hf + 1) * 1024]),
                    writes=[f"p3_wst{sl}"])
                b.op("dve", lambda e, ic=ic, hf=hf, sl=sl: e.tensor_copy(
                    wo_sb[:, ic * D + hf * 1024: ic * D + (hf + 1) * 1024], wst3[sl][:]),
                    reads=[f"p3_wst{sl}"], writes=[("p3_wo", ic)])

        ofm_sb = al("p3_ofm", [128, 16 * WIN], BF16)
        xT_t = [al(f"p3_xT{i}", [128, WIN], F32) for i in range(2)]
        r1_sb = al("p3_r1", [128, 16 * WIN], F32)
        xs_row = al("xs_row", [1, WIN], F32)
        xsi_row = al("xsi_row", [1, WIN], F32)
        rbf = [al(f"p3_rbf{i}", [128, WIN], BF16) for i in range(2)]
        sqv = [al(f"p3_sq{i}", [128, WIN], BF16) for i in range(2)]
        srow = al("p3_srow", [1, WIN], F32)
        vrow = al("p3_vrow", [1, WIN], F32)
        rrow = al("p3_rrow", [1, WIN], F32)
        mrow = al("p3_mrow", [1, WIN], F32)
        rstdb = al("p3_rstdb", [128, WIN], F32)
        mrsb = al("p3_mrsb", [128, WIN], F32)
        tmax = al("p3_tmax", [128, WIN], F32)
        tmin = al("p3_tmin", [128, WIN], F32)
        ttmp = al("p3_ttmp", [128, WIN], F32)
        scol = al("p3_scol", [128, 4], F32)
        sicol = al("p3_sicol", [128, 4], F32)
        sib = al("p3_sib", [128, WIN], F32)
        qt = [al(f"p3_qt{i}", [128, WIN], F32) for i in range(2)]
        qi = [al(f"p3_qi{i}", [128, WIN], I32) for i in range(2)]
        qd = [al(f"p3_qd{i}", [128, WIN], F32) for i in range(2)]
        qb2 = [al(f"p3_qb{i}", [128, WIN], F32) for i in range(2)]
        qc = [al(f"p3_qc{i}", [128, WIN], F32) for i in range(2)]
        xqo = [al(f"p3_xqo{i}", [128, WIN], BF16) for i in range(2)]
        wust3 = [al(f"p3_wu{i}", [128, 1024], F32) for i in range(2)]
        qa3 = al("p3_qa", [128, 1024], BF16)
        qb3 = al("p3_qb", [128, 1024], BF16)
        wqu3 = [al(f"p3_wqu{i}", [128, 1024], BF16) for i in range(2)]

        p3ps = ExitStack()
        pr_ps = [p3ps.enter_context(nc.psum_tensor(f"p3_pr{i}", [128, WIN], F32)) for i in range(2)]
        st_ps = p3ps.enter_context(nc.psum_tensor("p3_st", [1, WIN], F32))
        sq_ps = p3ps.enter_context(nc.psum_tensor("p3_sq", [1, WIN], F32))
        b3_ps = p3ps.enter_context(nc.psum_tensor("p3_b3", [128, WIN], F32))
        tr_ps = p3ps.enter_context(nc.psum_tensor("p3_tr", [128, 128], F32))
        mi_ps = p3ps.enter_context(nc.psum_tensor("p3_mi", [1, 128], F32))

        for blk in range(NBLK):
            for c in range(16):
                b.dma("sp", "p3o", lambda e, c=c, blk=blk: e.dma_start(
                    out=ofm_sb[:, c * WIN:(c + 1) * WIN],
                    in_=ofm_h[c * 128:(c + 1) * 128, blk * WIN:(blk + 1) * WIN]),
                    writes=[("p3_ofm",)])
            def emit_wu_chunk(c):
                rt, cq = c % 16, c // 16
                sl = c % 2
                b.dma("sp", f"p3wu{sl}", lambda e, rt=rt, cq=cq, sl=sl: e.dma_start(
                    out=wust3[sl][:], in_=wuT[rt * 128:(rt + 1) * 128,
                                              cq * 1024:(cq + 1) * 1024]),
                    writes=[f"p3_wu{sl}"])
                b.op("dve", lambda e, sl=sl: e.tensor_scalar(
                    qa3[:], wust3[sl][:], taucol["u"][:], None, ALU.is_gt),
                    reads=[f"p3_wu{sl}", "taucol_u"], writes=["p3_qa3"])
                b.op("dve", lambda e, sl=sl: e.tensor_scalar(
                    qb3[:], wust3[sl][:], ntaucol["u"][:], None, ALU.is_lt),
                    reads=[f"p3_wu{sl}", "ntaucol_u"], writes=["p3_qb3"])
                b.op("dve", lambda e, sl=sl: e.tensor_tensor(
                    wqu3[sl][:], qa3[:], qb3[:], ALU.subtract),
                    reads=["p3_qa3", "p3_qb3"], writes=[f"p3_wqu{sl}"])
                b.dma("act", f"p3wuo{sl}", lambda e, rt=rt, cq=cq, sl=sl: e.dma_start(
                    out=wqu_h[:, rt:rt + 1, cq * 1024:(cq + 1) * 1024],
                    in_=wqu3[sl][:]),
                    reads=[f"p3_wqu{sl}"])

            for oc in range(16):
                emit_wu_chunk(blk * 32 + 2 * oc)
                emit_wu_chunk(blk * 32 + 2 * oc + 1)
                pp = oc % 2
                b.dma("act", f"p3x{pp}", lambda e, oc=oc, blk=blk, pp=pp: e.dma_start(
                    out=xT_t[pp][:],
                    in_=xT[oc * 128:(oc + 1) * 128, WIN + blk * WIN: WIN + (blk + 1) * WIN]),
                    writes=[f"p3_xT{pp}"])
                grp = f"pr{blk}_{oc}"
                for ic in range(16):
                    b.op("pe", lambda e, oc=oc, ic=ic, pp=pp: e.matmul(
                        pr_ps[pp][:],
                        wo_sb[:, ic * D + oc * 128: ic * D + (oc + 1) * 128],
                        ofm_sb[:, ic * WIN:(ic + 1) * WIN],
                        start=(ic == 0), stop=(ic == 15)),
                        reads=[("p3_wo", ic), ("p3_ofm",)],
                        writes=[f"p3_prp{pp}"], group=grp, final=(ic == 15))
                b.op("dve", lambda e, oc=oc, pp=pp: e.tensor_tensor(
                    r1_sb[:, oc * WIN:(oc + 1) * WIN], pr_ps[pp][:],
                    xT_t[pp][:], ALU.add),
                    reads=[f"p3_prp{pp}", f"p3_xT{pp}"], writes=[("p3_r1", oc)])
                b.op("act", lambda e, oc=oc, pp=pp: e.copy(
                    rbf[pp][:], r1_sb[:, oc * WIN:(oc + 1) * WIN]),
                    reads=[("p3_r1", oc)], writes=[f"p3_rbf{pp}"])
                b.op("act", lambda e, oc=oc, pp=pp: e.activation(
                    sqv[pp][:], r1_sb[:, oc * WIN:(oc + 1) * WIN], AF.Square),
                    reads=[("p3_r1", oc)], writes=[f"p3_sq{pp}"])
                b.op("pe", lambda e, oc=oc, pp=pp: e.matmul(
                    st_ps[:], consts["ones_bf"][:], rbf[pp][:],
                    start=(oc == 0), stop=(oc == 15)),
                    reads=["c_ones_bf", f"p3_rbf{pp}"], writes=["p3_stp"])
                b.op("pe", lambda e, oc=oc, pp=pp: e.matmul(
                    sq_ps[:], consts["ones_bf"][:], sqv[pp][:],
                    start=(oc == 0), stop=(oc == 15)),
                    reads=["c_ones_bf", f"p3_sq{pp}"], writes=["p3_sqp"])
            b.op("dve", lambda e: e.tensor_scalar(srow[:], st_ps[:], 1.0 / D, None, ALU.mult),
                 reads=["p3_stp"], writes=["p3_srow"])
            b.op("dve", lambda e: e.tensor_scalar(vrow[:], sq_ps[:], 1.0 / D, None, ALU.mult),
                 reads=["p3_sqp"], writes=["p3_vrow"])
            b.op("dve", lambda e: e.tensor_tensor(mrow[:], srow[:], srow[:], ALU.mult),
                 reads=["p3_srow"], writes=["p3_mrow"])
            b.op("dve", lambda e: e.tensor_tensor(vrow[:], vrow[:], mrow[:], ALU.subtract),
                 reads=["p3_vrow", "p3_mrow"], writes=["p3_vrow"])
            b.op("act", lambda e: e.activation(vrow[:], vrow[:], AF.Ln, bias=epsrow[:]),
                 reads=["p3_vrow", "epsrow"], writes=["p3_vrow"])
            b.op("act", lambda e: e.activation(rrow[:], vrow[:], AF.Exp, scale=-0.5),
                 reads=["p3_vrow"], writes=["p3_rrow"])
            b.op("dve", lambda e: e.tensor_tensor(mrow[:], srow[:], rrow[:], ALU.mult),
                 reads=["p3_srow", "p3_rrow"], writes=["p3_mrow"])
            b.op("pe", lambda e: e.matmul(b3_ps[:], consts["onesrow_f"][:], rrow[:],
                                          start=True, stop=True),
                 reads=["c_onesrow_f", "p3_rrow"], writes=["p3_b3p"])
            b.op("act", lambda e: e.copy(rstdb[:], b3_ps[:]), reads=["p3_b3p"], writes=["p3_rstdb"])
            b.op("pe", lambda e: e.matmul(b3_ps[:], consts["onesrow_f"][:], mrow[:],
                                          start=True, stop=True),
                 reads=["c_onesrow_f", "p3_mrow"], writes=["p3_b3p"])
            b.op("act", lambda e: e.copy(mrsb[:], b3_ps[:]), reads=["p3_b3p"], writes=["p3_mrsb"])
            for oc in range(16):
                b.op("dve", lambda e, oc=oc: e.tensor_tensor(
                    r1_sb[:, oc * WIN:(oc + 1) * WIN],
                    r1_sb[:, oc * WIN:(oc + 1) * WIN], rstdb[:], ALU.mult),
                    reads=[("p3_r1", oc), "p3_rstdb"], writes=[("p3_r1", oc)])
                b.op("dve", lambda e, oc=oc: e.tensor_tensor(
                    r1_sb[:, oc * WIN:(oc + 1) * WIN],
                    r1_sb[:, oc * WIN:(oc + 1) * WIN], mrsb[:], ALU.subtract),
                    reads=[("p3_r1", oc), "p3_mrsb"], writes=[("p3_r1", oc)])
                b.dma("sp", "p3r1", lambda e, oc=oc, blk=blk: e.dma_start(
                    out=r1_h[oc * 128:(oc + 1) * 128, blk * WIN:(blk + 1) * WIN],
                    in_=r1_sb[:, oc * WIN:(oc + 1) * WIN]),
                    reads=[("p3_r1", oc)])
                if oc == 0:
                    b.op("dve", lambda e: e.tensor_copy(tmax[:], r1_sb[:, 0:WIN]),
                         reads=[("p3_r1", 0)], writes=["p3_tmax"])
                    b.op("dve", lambda e: e.tensor_copy(tmin[:], r1_sb[:, 0:WIN]),
                         reads=[("p3_r1", 0)], writes=["p3_tmin"])
                else:
                    b.op("dve", lambda e, oc=oc: e.tensor_tensor(
                        tmax[:], tmax[:], r1_sb[:, oc * WIN:(oc + 1) * WIN], ALU.max),
                        reads=["p3_tmax", ("p3_r1", oc)], writes=["p3_tmax"])
                    b.op("dve", lambda e, oc=oc: e.tensor_tensor(
                        tmin[:], tmin[:], r1_sb[:, oc * WIN:(oc + 1) * WIN], ALU.min),
                        reads=["p3_tmin", ("p3_r1", oc)], writes=["p3_tmin"])
            b.op("dve", lambda e: e.tensor_scalar(ttmp[:], tmin[:], -1.0, None, ALU.mult),
                 reads=["p3_tmin"], writes=["p3_ttmp"])
            b.op("dve", lambda e: e.tensor_tensor(ttmp[:], ttmp[:], tmax[:], ALU.max),
                 reads=["p3_ttmp", "p3_tmax"], writes=["p3_ttmp"])
            for q in range(4):
                b.op("pe", lambda e, q=q: e.transpose(tr_ps[:], ttmp[:, q * 128:(q + 1) * 128],
                                                      consts["ident_f"][:]),
                     reads=["p3_ttmp", "c_ident_f"], writes=["p3_trp"])
                b.op("dve", lambda e, q=q: e.tensor_reduce(
                    scol[:, q:q + 1], tr_ps[:], AX.X, ALU.max),
                    reads=["p3_trp"], writes=["p3_scol"])
            b.op("dve", lambda e: e.tensor_scalar(scol[:], scol[:], 1.0 / 127.0, EPS,
                                                  ALU.mult, ALU.add),
                 reads=["p3_scol"], writes=["p3_scol"])
            b.op("dve", lambda e: e.reciprocal(sicol[:], scol[:]),
                 reads=["p3_scol"], writes=["p3_sicol"])
            for q in range(4):
                b.op("pe", lambda e, q=q: e.transpose(mi_ps[0:1, 0:128], scol[:, q:q + 1],
                                                      consts["ident_f"][:]),
                     reads=["p3_scol", "c_ident_f"], writes=["p3_mip"])
                b.op("act", lambda e, q=q: e.copy(
                    xs_row[:, q * 128:(q + 1) * 128], mi_ps[0:1, 0:128]),
                    reads=["p3_mip"], writes=[("xs_row",)])
                b.op("pe", lambda e, q=q: e.transpose(mi_ps[0:1, 0:128], sicol[:, q:q + 1],
                                                      consts["ident_f"][:]),
                     reads=["p3_sicol", "c_ident_f", ("xs_row",)], writes=["p3_mip"])
                b.op("act", lambda e, q=q: e.copy(
                    xsi_row[:, q * 128:(q + 1) * 128], mi_ps[0:1, 0:128]),
                    reads=["p3_mip"], writes=[("xsi_row",)])
            b.dma("sp", "p3xs", lambda e, blk=blk: e.dma_start(
                out=xs_h[0:1, blk * WIN:(blk + 1) * WIN], in_=xs_row[:]),
                reads=[("xs_row",)])
            b.op("pe", lambda e: e.matmul(
                b3_ps[:], consts["onesrow_f"][:], xsi_row[:], start=True, stop=True),
                reads=[("xsi_row",), "c_onesrow_f"], writes=["p3_b3p"])
            b.op("act", lambda e: e.copy(sib[:], b3_ps[:]), reads=["p3_b3p"], writes=["p3_sib"])
            for oc in range(16):
                pp = oc % 2
                b.op("dve", lambda e, oc=oc, pp=pp: e.tensor_tensor(
                    qt[pp][:], r1_sb[:, oc * WIN:(oc + 1) * WIN], sib[:], ALU.mult),
                    reads=[("p3_r1", oc), "p3_sib"], writes=[f"p3_qt{pp}"])
                b.op("act", lambda e, pp=pp: e.activation(qi[pp][:], qt[pp][:], AF.Copy),
                     reads=[f"p3_qt{pp}"], writes=[f"p3_qi{pp}"])
                b.op("dve", lambda e, pp=pp: e.tensor_tensor(qd[pp][:], qt[pp][:], qi[pp][:], ALU.subtract),
                     reads=[f"p3_qt{pp}", f"p3_qi{pp}"], writes=[f"p3_qd{pp}"])
                b.op("dve", lambda e, pp=pp: e.tensor_scalar(qb2[pp][:], qd[pp][:], -0.5, None, ALU.is_lt),
                     reads=[f"p3_qd{pp}"], writes=[f"p3_qb{pp}"])
                b.op("dve", lambda e, pp=pp: e.scalar_tensor_tensor(
                    qc[pp][:], qd[pp][:], 0.5, qb2[pp][:], ALU.is_gt, ALU.subtract),
                    reads=[f"p3_qd{pp}", f"p3_qb{pp}"], writes=[f"p3_qc{pp}"])
                b.op("dve", lambda e, pp=pp: e.tensor_tensor(
                    xqo[pp][:], qi[pp][:], qc[pp][:], ALU.add),
                    reads=[f"p3_qi{pp}", f"p3_qc{pp}"], writes=[f"p3_xqo{pp}"])
                b.dma("act", f"p3xq{pp}", lambda e, oc=oc, blk=blk, pp=pp: e.dma_start(
                    out=xq_h[oc * 128:(oc + 1) * 128, blk * WIN:(blk + 1) * WIN],
                    in_=xqo[pp][:]),
                    reads=[f"p3_xqo{pp}"])
        b.barrier()
        p3ps.close()
        p3.close()


    if up_to >= 4:
        p4 = ExitStack()
        al = mkalloc(p4)
        xq_sb = al("p4_xq", [128, 16 * S], BF16)
        for c in range(16):
            b.dma("sp", "p4xq", lambda e, c=c: e.dma_start(
                out=xq_sb[:, c * S:(c + 1) * S], in_=xq_h[c * 128:(c + 1) * 128, :]),
                writes=[("p4_xq",)])
        xsb = al("p4_xsb", [128, S], F32)
        xsrow4 = al("p4_xsrow", [1, S], F32)
        p4ps = ExitStack()
        xb_es = ExitStack()
        xb_ps = xb_es.enter_context(nc.psum_tensor("p4_xb", [128, 512], F32))
        b.dma("sp", "p4xs", lambda e: e.dma_start(out=xsrow4[:], in_=xs_h),
              writes=[("p4_xsrow",)])
        for g4 in range(4):
            b.op("pe", lambda e, g4=g4: e.matmul(
                xb_ps[:], consts["onesrow_f"][:], xsrow4[:, g4 * 512:(g4 + 1) * 512],
                start=True, stop=True),
                reads=[("p4_xsrow",), "c_onesrow_f"], writes=["p4_xbp"])
            b.op("act", lambda e, g4=g4: e.copy(xsb[:, g4 * 512:(g4 + 1) * 512], xb_ps[:]),
                 reads=["p4_xbp"], writes=[("p4_xsb",)])

        probeb = al("p4_probe", [1, 2], F32)
        wqg = [al(f"p4_wqg{i}", [128, 4096], BF16) for i in range(2)]
        wqu = [al(f"p4_wqu{i}", [128, 4096], BF16) for i in range(2)]
        zg = al("p4_zg", [128, 1024], F32)
        zu = al("p4_zu", [128, 1024], F32)
        sg = al("p4_sg", [128, 1024], F32)
        hsb = [al(f"p4_h{i}", [128, 1024], F32) for i in range(2)]
        hmx = al("p4_hmx", [128, 2048], F32)
        hmn = al("p4_hmn", [128, 2048], F32)
        b.barrier()
        xb_es.close()
        gu_es = ExitStack()
        g_ps = [gu_es.enter_context(nc.psum_tensor(f"p4_g{i}", [128, 1024], F32)) for i in range(2)]
        u_ps = [gu_es.enter_context(nc.psum_tensor(f"p4_u{i}", [128, 1024], F32)) for i in range(2)]

        b.op("pool", lambda e: e.memset(probeb[:], 0.0), writes=["p4_probeb"])

        def p4_wload(op):
            sl = op % 2
            for mat, src_h, dst in (("g", wqg_h, wqg), ("u", wqu_h, wqu)):
                b.dma("sp", f"p4w{mat}{sl}",
                      lambda e, op=op, src_h=src_h, dst=dst, sl=sl:
                      e.dma_start(out=dst[sl][:],
                                  in_=src_h[:, :, op * 256:(op + 1) * 256]),
                      writes=[(f"p4_wq{mat}{sl}",)])
                b.op("pool", lambda e: e.tensor_scalar(
                    probeb[:], probeb[:], 1.0, None, ALU.mult),
                    reads=["p4_probeb", (f"p4_wq{mat}{sl}",)],
                    writes=["p4_probeb", (f"p4_wq{mat}{sl}",)])

        def p4_compute(oc):
            sl = (oc // 2) % 2
            woff = (oc % 2) * 128
            hp = oc % 2
            for half in range(2):
                t0 = half * 1024
                gg = f"g{oc}_{half}"
                for ic in range(16):
                    for q2 in range(2):
                        b.op("pe", lambda e, ic=ic, sl=sl, t0=t0, half=half, q2=q2: e.matmul(
                            g_ps[half][:, q2 * 512:(q2 + 1) * 512],
                            wqg[sl][:, ic * 256 + woff: ic * 256 + woff + 128],
                            xq_sb[:, ic * S + t0 + q2 * 512: ic * S + t0 + (q2 + 1) * 512],
                            start=(ic == 0), stop=(ic == 15)),
                            reads=[(f"p4_wqg{sl}",), ("p4_xq",)],
                            writes=[f"p4_gp{half}"], group=gg,
                            final=(ic == 15 and q2 == 1))
                uu = f"u{oc}_{half}"
                for ic in range(16):
                    for q2 in range(2):
                        b.op("pe", lambda e, ic=ic, sl=sl, t0=t0, half=half, q2=q2: e.matmul(
                            u_ps[half][:, q2 * 512:(q2 + 1) * 512],
                            wqu[sl][:, ic * 256 + woff: ic * 256 + woff + 128],
                            xq_sb[:, ic * S + t0 + q2 * 512: ic * S + t0 + (q2 + 1) * 512],
                            start=(ic == 0), stop=(ic == 15)),
                            reads=[(f"p4_wqu{sl}",), ("p4_xq",)],
                            writes=[f"p4_up{half}"], group=uu,
                            final=(ic == 15 and q2 == 1))
                b.op("dve", lambda e, half=half, t0=t0: e.scalar_tensor_tensor(
                    zg[:], g_ps[half][:], wscol["g"][:], xsb[:, t0:t0 + 1024],
                    ALU.mult, ALU.mult),
                    reads=[f"p4_gp{half}", "wscol_g", ("p4_xsb",)], writes=["p4_zg"])
                b.op("act", lambda e: e.activation(sg[:], zg[:], AF.Silu),
                     reads=["p4_zg"], writes=["p4_sg"])
                b.op("dve", lambda e, half=half, t0=t0: e.scalar_tensor_tensor(
                    zu[:], u_ps[half][:], wscol["u"][:], xsb[:, t0:t0 + 1024],
                    ALU.mult, ALU.mult),
                    reads=[f"p4_up{half}", "wscol_u", ("p4_xsb",)], writes=["p4_zu"])
                hkey = f"p4_h{hp}"
                b.op("dve", lambda e, hp=hp: e.tensor_tensor(
                    hsb[hp][:], sg[:], zu[:], ALU.mult),
                    reads=["p4_sg", "p4_zu"], writes=[hkey])
                b.dma("sp", f"p4h{hp}", lambda e, oc=oc, hp=hp, t0=t0: e.dma_start(
                    out=h_h[oc * 128:(oc + 1) * 128, t0:t0 + 1024], in_=hsb[hp][:]),
                    reads=[hkey])
                if oc == 0:
                    b.op("dve", lambda e, hp=hp, t0=t0: e.tensor_copy(
                        hmx[:, t0:t0 + 1024], hsb[hp][:]),
                        reads=[hkey], writes=[("p4_hmx", half)])
                    b.op("dve", lambda e, hp=hp, t0=t0: e.tensor_copy(
                        hmn[:, t0:t0 + 1024], hsb[hp][:]),
                        reads=[hkey], writes=[("p4_hmn", half)])
                else:
                    b.op("dve", lambda e, hp=hp, t0=t0: e.tensor_tensor(
                        hmx[:, t0:t0 + 1024], hmx[:, t0:t0 + 1024],
                        hsb[hp][:], ALU.max),
                        reads=[("p4_hmx", half), hkey], writes=[("p4_hmx", half)])
                    b.op("dve", lambda e, hp=hp, t0=t0: e.tensor_tensor(
                        hmn[:, t0:t0 + 1024], hmn[:, t0:t0 + 1024],
                        hsb[hp][:], ALU.min),
                        reads=[("p4_hmn", half), hkey], writes=[("p4_hmn", half)])

        p4_wload(0)
        for op in range(II // 256):
            if op + 1 < II // 256:
                p4_wload(op + 1)
            p4_compute(2 * op)
            p4_compute(2 * op + 1)

        b.barrier()
        gu_es.close()
        b.op("dve", lambda e: e.tensor_scalar(hmn[:], hmn[:], -1.0, None, ALU.mult),
             reads=[("p4_hmn", 0), ("p4_hmn", 1)], writes=[("p4_hmn", 0), ("p4_hmn", 1)])
        b.op("dve", lambda e: e.tensor_tensor(hmx[:], hmx[:], hmn[:], ALU.max),
             reads=[("p4_hmx", 0), ("p4_hmx", 1), ("p4_hmn", 0), ("p4_hmn", 1)],
             writes=[("p4_hmx", 0), ("p4_hmx", 1)])
        tr4 = p4ps.enter_context(nc.psum_tensor("p4_tr", [128, 128], F32))
        mi4 = p4ps.enter_context(nc.psum_tensor("p4_mi", [1, 128], F32))
        s2col = al("p4_s2col", [128, 16], F32)
        s2_row = al("p4_s2row", [1, S], F32)
        s2i_row = al("p4_s2irow", [1, S], F32)
        dsc_row = al("p4_dscrow", [1, S], F32)
        for q in range(16):
            b.op("pe", lambda e, q=q: e.transpose(tr4[:], hmx[:, q * 128:(q + 1) * 128],
                                                  consts["ident_f"][:]),
                 reads=[("p4_hmx", 0), ("p4_hmx", 1), "c_ident_f"], writes=["p4_trp"])
            b.op("dve", lambda e, q=q: e.tensor_reduce(
                s2col[:, q:q + 1], tr4[:], AX.X, ALU.max),
                reads=["p4_trp"], writes=["p4_s2col"])
        b.op("dve", lambda e: e.tensor_scalar(s2col[:], s2col[:], 1.0 / 127.0, EPS,
                                              ALU.mult, ALU.add),
             reads=["p4_s2col"], writes=["p4_s2col"])
        for q in range(16):
            b.op("pe", lambda e, q=q: e.transpose(mi4[0:1, 0:128], s2col[:, q:q + 1],
                                                  consts["ident_f"][:]),
                 reads=["p4_s2col", "c_ident_f"], writes=["p4_mip"])
            b.op("act", lambda e, q=q: e.copy(s2_row[:, q * 128:(q + 1) * 128], mi4[:]),
                 reads=["p4_mip"], writes=[("s2_row",)])
        b.op("dve", lambda e: e.reciprocal(s2i_row[:], s2_row[:]),
             reads=[("s2_row",)], writes=[("s2i_row",)])
        b.op("dve", lambda e: e.tensor_scalar(dsc_row[:], s2_row[:],
                                              wssc["d"][0:1, 0:1], None, ALU.mult),
             reads=[("s2_row",), "wssc_d"], writes=[("dsc_row",)])
        b.dma("sp", "p4s2", lambda e: e.dma_start(out=s2i_h, in_=s2i_row[:]),
              reads=[("s2i_row",)])
        b.dma("sp", "p4s2", lambda e: e.dma_start(out=dsc_h, in_=dsc_row[:]),
              reads=[("dsc_row",)])

        b.barrier()
        p4ps.close()
        p4.close()


    if up_to >= 5:
        p5 = ExitStack()
        al = mkalloc(p5)
        wqd_sl = al("p5_wqd", [128, 64 * 1024], BF16)
        s2i_row5 = al("p5_s2irow", [1, S], F32)
        dsc_row5 = al("p5_dscrow", [1, S], F32)
        hload = [al(f"p5_hl{i}", [128, WIN], F32) for i in range(2)]
        s2ib = al("p5_s2ib", [128, WIN], F32)
        dscb = al("p5_dscb", [128, WIN], F32)
        r1t = [al(f"p5_r1t{i}", [128, WIN], F32) for i in range(2)]
        r2t = [al(f"p5_r2t{i}", [128, WIN], F32) for i in range(2)]
        yv = al("p5_yv", [128, WIN], F32)
        q5t = [al(f"p5_qt{i}", [128, WIN], F32) for i in range(2)]
        q5i = [al(f"p5_qi{i}", [128, WIN], I32) for i in range(2)]
        q5d = [al(f"p5_qd{i}", [128, WIN], F32) for i in range(2)]
        q5b = [al(f"p5_qb{i}", [128, WIN], F32) for i in range(2)]
        q5c = [al(f"p5_qc{i}", [128, WIN], F32) for i in range(2)]
        p5ps = ExitStack()
        dn_ps = [p5ps.enter_context(nc.psum_tensor(f"p5_dn{i}", [128, WIN], F32)) for i in range(8)]

        b.dma("sp", "p5rows", lambda e: e.dma_start(out=s2i_row5[:], in_=s2i_h),
              writes=[("s2i_row",)])
        b.dma("sp", "p5rows", lambda e: e.dma_start(out=dsc_row5[:], in_=dsc_h),
              writes=[("dsc_row",)])
        hq_rq = [al(f"p5_hqr{i}", [128, WIN], BF16) for i in range(4)]
        hq_m = [al(f"p5_hqm{i}", [128, WIN], BF16) for i in range(4)]
        for oslab in range(2):
            for ic in range(64):
                b.dma("sp", f"p5w{ic % 2}", lambda e, ic=ic, oslab=oslab: e.dma_start(
                    out=wqd_sl[:, ic * 1024:(ic + 1) * 1024],
                    in_=wqd_h[ic * 128:(ic + 1) * 128, oslab * 1024:(oslab + 1) * 1024]),
                    writes=[("p5_wqd",)])
            for ts in range(4):
                b.op("pe", lambda e, ts=ts: e.matmul(
                    dn_ps[0][:], consts["onesrow_f"][:], dsc_row5[:, ts * WIN:(ts + 1) * WIN],
                    start=True, stop=True),
                    reads=[("dsc_row",), "c_onesrow_f"], writes=["p5_dnp0"])
                b.op("act", lambda e: e.copy(dscb[:], dn_ps[0][:]), reads=["p5_dnp0"],
                     writes=["p5_dscb"])
                if oslab == 0:
                    b.op("pe", lambda e, ts=ts: e.matmul(
                        dn_ps[0][:], consts["onesrow_f"][:], s2i_row5[:, ts * WIN:(ts + 1) * WIN],
                        start=True, stop=True),
                        reads=[("s2i_row",), "c_onesrow_f"], writes=["p5_dnp0"])
                    b.op("act", lambda e: e.copy(s2ib[:], dn_ps[0][:]), reads=["p5_dnp0"],
                         writes=["p5_s2ib"])
                for ic in range(64):
                    sl = ic % 2
                    q4 = ic % 4
                    if oslab == 0:
                        hkey = f"p5_hqr{q4}"
                        b.dma("sp", f"p5h{sl}", lambda e, ic=ic, ts=ts, sl=sl:
                              e.dma_start(out=hload[sl][:],
                                          in_=h_h[ic * 128:(ic + 1) * 128,
                                                  ts * WIN:(ts + 1) * WIN]),
                              writes=[f"p5_hl{sl}"])
                        b.op("dve", lambda e, sl=sl: e.tensor_tensor(
                            q5t[sl][:], hload[sl][:], s2ib[:], ALU.mult),
                            reads=[f"p5_hl{sl}", "p5_s2ib"], writes=[f"p5_qt{sl}"])
                        b.op("act", lambda e, sl=sl: e.activation(q5i[sl][:], q5t[sl][:], AF.Copy),
                             reads=[f"p5_qt{sl}"], writes=[f"p5_qi{sl}"])
                        b.op("dve", lambda e, sl=sl: e.tensor_tensor(q5d[sl][:], q5t[sl][:], q5i[sl][:], ALU.subtract),
                             reads=[f"p5_qt{sl}", f"p5_qi{sl}"], writes=[f"p5_qd{sl}"])
                        b.op("dve", lambda e, sl=sl: e.tensor_scalar(q5b[sl][:], q5d[sl][:], -0.5, None, ALU.is_lt),
                             reads=[f"p5_qd{sl}"], writes=[f"p5_qb{sl}"])
                        b.op("dve", lambda e, sl=sl: e.scalar_tensor_tensor(
                            q5c[sl][:], q5d[sl][:], 0.5, q5b[sl][:], ALU.is_gt, ALU.subtract),
                            reads=[f"p5_qd{sl}", f"p5_qb{sl}"], writes=[f"p5_qc{sl}"])
                        b.op("dve", lambda e, sl=sl, q4=q4: e.tensor_tensor(
                            hq_rq[q4][:], q5i[sl][:], q5c[sl][:], ALU.add),
                            reads=[f"p5_qi{sl}", f"p5_qc{sl}"], writes=[hkey])
                        b.dma("sp", f"p5hqo{q4}", lambda e, ic=ic, ts=ts, q4=q4:
                              e.dma_start(out=hq_h[ic * 128:(ic + 1) * 128,
                                                   ts * WIN:(ts + 1) * WIN],
                                          in_=hq_rq[q4][:]),
                              reads=[hkey], writes=[("hqh", ts, ic)])
                        hq_ap = hq_rq[q4]
                    else:
                        hkey = f"p5_hqm{q4}"
                        b.dma("sp", f"p5hq{q4}", lambda e, ic=ic, ts=ts, q4=q4:
                              e.dma_start(out=hq_m[q4][:],
                                          in_=hq_h[ic * 128:(ic + 1) * 128,
                                                   ts * WIN:(ts + 1) * WIN]),
                              reads=[("hqh", ts, ic)], writes=[hkey])
                        hq_ap = hq_m[q4]
                    for ohc in range(8):
                        gname = f"dn{oslab}_{ts}_{ic}"
                        b.op("pe", lambda e, ic=ic, hq_ap=hq_ap, ohc=ohc: e.matmul(
                            dn_ps[ohc][:], wqd_sl[:, ic * 1024 + ohc * 128: ic * 1024 + (ohc + 1) * 128],
                            hq_ap[:], start=(ic == 0), stop=(ic == 63)),
                            reads=[("p5_wqd",), hkey],
                            writes=[f"p5_dnp{ohc}"], group=gname, final=(ohc == 7))
                for ohc in range(8):
                    orow = oslab * 1024 + ohc * 128
                    sl2 = ohc % 2
                    b.dma("sp", f"p5r1{sl2}", lambda e, orow=orow, ts=ts, sl2=sl2: e.dma_start(
                        out=r1t[sl2][:], in_=r1_h[orow:orow + 128, ts * WIN:(ts + 1) * WIN]),
                        writes=[f"p5_r1t{sl2}"])
                    b.op("dve", lambda e, ohc=ohc: e.tensor_tensor(
                        yv[:], dn_ps[ohc][:], dscb[:], ALU.mult),
                        reads=[f"p5_dnp{ohc}", "p5_dscb"], writes=["p5_yv"])
                    b.op("dve", lambda e, sl2=sl2: e.tensor_tensor(
                        r2t[sl2][:], yv[:], r1t[sl2][:], ALU.add),
                        reads=["p5_yv", f"p5_r1t{sl2}"], writes=[f"p5_r2t{sl2}"])
                    b.dma("sp", f"p5r2{sl2}", lambda e, orow=orow, ts=ts, sl2=sl2: e.dma_start(
                        out=r2_h[orow:orow + 128, ts * WIN:(ts + 1) * WIN],
                        in_=r2t[sl2][:]),
                        reads=[f"p5_r2t{sl2}"])

        b.barrier()
        p5ps.close()
        p5.close()


    if up_to >= 6:
        p6 = ExitStack()
        al = mkalloc(p6)
        r2sb = [al(f"p6_r2_{i}", [128, 16 * WIN], F32) for i in range(2)]
        rbf6 = [al(f"p6_rbf{i}", [128, WIN], BF16) for i in range(2)]
        sq6 = [al(f"p6_sq{i}", [128, WIN], BF16) for i in range(2)]
        srow6 = al("p6_srow", [1, WIN], F32)
        vrow6 = al("p6_vrow", [1, WIN], F32)
        rrow6 = al("p6_rrow", [1, WIN], F32)
        mrow6 = al("p6_mrow", [1, WIN], F32)
        rstdb6 = al("p6_rstdb", [128, WIN], F32)
        mrsb6 = al("p6_mrsb", [128, WIN], F32)
        oout = [al(f"p6_o{i}", [128, WIN], F32) for i in range(2)]
        p6ps = ExitStack()
        st6 = p6ps.enter_context(nc.psum_tensor("p6_st", [1, WIN], F32))
        sq6p = p6ps.enter_context(nc.psum_tensor("p6_sq", [1, WIN], F32))
        b6 = p6ps.enter_context(nc.psum_tensor("p6_b6", [128, WIN], F32))

        def p6_load(blk, bb):
            for c in range(16):
                b.dma("sp", f"p6r2{bb}", lambda e, c=c, blk=blk, bb=bb: e.dma_start(
                    out=r2sb[bb][:, c * WIN:(c + 1) * WIN],
                    in_=r2_h[c * 128:(c + 1) * 128, blk * WIN:(blk + 1) * WIN]),
                    writes=[("p6_r2", bb)])

        p6_load(0, 0)
        for blk in range(NBLK):
            bb = blk % 2
            for oc in range(16):
                pp = oc % 2
                b.op("act", lambda e, oc=oc, pp=pp, bb=bb: e.copy(
                    rbf6[pp][:], r2sb[bb][:, oc * WIN:(oc + 1) * WIN]),
                    reads=[("p6_r2", bb)], writes=[f"p6_rbf{pp}"])
                b.op("act", lambda e, oc=oc, pp=pp, bb=bb: e.activation(
                    sq6[pp][:], r2sb[bb][:, oc * WIN:(oc + 1) * WIN], AF.Square),
                    reads=[("p6_r2", bb)], writes=[f"p6_sq{pp}"])
                b.op("pe", lambda e, oc=oc, pp=pp: e.matmul(
                    st6[:], consts["ones_bf"][:], rbf6[pp][:],
                    start=(oc == 0), stop=(oc == 15)),
                    reads=["c_ones_bf", f"p6_rbf{pp}"], writes=["p6_stp"])
                b.op("pe", lambda e, oc=oc, pp=pp: e.matmul(
                    sq6p[:], consts["ones_bf"][:], sq6[pp][:],
                    start=(oc == 0), stop=(oc == 15)),
                    reads=["c_ones_bf", f"p6_sq{pp}"], writes=["p6_sqp"])
            if blk + 1 < NBLK:
                p6_load(blk + 1, (blk + 1) % 2)
            b.op("dve", lambda e: e.tensor_scalar(srow6[:], st6[:], 1.0 / D, None, ALU.mult),
                 reads=["p6_stp"], writes=["p6_srow"])
            b.op("dve", lambda e: e.tensor_scalar(vrow6[:], sq6p[:], 1.0 / D, None, ALU.mult),
                 reads=["p6_sqp"], writes=["p6_vrow"])
            b.op("dve", lambda e: e.tensor_tensor(mrow6[:], srow6[:], srow6[:], ALU.mult),
                 reads=["p6_srow"], writes=["p6_mrow"])
            b.op("dve", lambda e: e.tensor_tensor(vrow6[:], vrow6[:], mrow6[:], ALU.subtract),
                 reads=["p6_vrow", "p6_mrow"], writes=["p6_vrow"])
            b.op("act", lambda e: e.activation(vrow6[:], vrow6[:], AF.Ln, bias=epsrow[:]),
                 reads=["p6_vrow", "epsrow"], writes=["p6_vrow"])
            b.op("act", lambda e: e.activation(rrow6[:], vrow6[:], AF.Exp, scale=-0.5),
                 reads=["p6_vrow"], writes=["p6_rrow"])
            b.op("dve", lambda e: e.tensor_tensor(mrow6[:], srow6[:], rrow6[:], ALU.mult),
                 reads=["p6_srow", "p6_rrow"], writes=["p6_mrow"])
            b.op("pe", lambda e: e.matmul(b6[:], consts["onesrow_f"][:], rrow6[:],
                                          start=True, stop=True),
                 reads=["c_onesrow_f", "p6_rrow"], writes=["p6_b6p"])
            b.op("act", lambda e: e.copy(rstdb6[:], b6[:]), reads=["p6_b6p"], writes=["p6_rstdb"])
            b.op("pe", lambda e: e.matmul(b6[:], consts["onesrow_f"][:], mrow6[:],
                                          start=True, stop=True),
                 reads=["c_onesrow_f", "p6_mrow"], writes=["p6_b6p"])
            b.op("act", lambda e: e.copy(mrsb6[:], b6[:]), reads=["p6_b6p"], writes=["p6_mrsb"])
            for oc in range(16):
                pp = oc % 2
                b.op("dve", lambda e, oc=oc, pp=pp, bb=bb: e.tensor_tensor(
                    oout[pp][:], r2sb[bb][:, oc * WIN:(oc + 1) * WIN], rstdb6[:], ALU.mult),
                    reads=[("p6_r2", bb), "p6_rstdb"], writes=[f"p6_o{pp}"])
                b.op("dve", lambda e, pp=pp: e.tensor_tensor(
                    oout[pp][:], oout[pp][:], mrsb6[:], ALU.subtract),
                    reads=[f"p6_o{pp}", "p6_mrsb"], writes=[f"p6_o{pp}"])
                b.dma("sp", f"p6o{pp}", lambda e, oc=oc, blk=blk, pp=pp: e.dma_start(
                    out=outT[oc * 128:(oc + 1) * 128, blk * WIN:(blk + 1) * WIN],
                    in_=oout[pp][:]),
                    reads=[f"p6_o{pp}"])

        b.barrier()
        p6ps.close()
        p6.close()


    if up_to < 1:
        pump(10000)
        b.barrier()
        p01.close()
    if up_to < 6:
        dzs = ExitStack()
        dal = mkalloc(dzs)
        zout = dal("dz_out", [128, WIN], F32)
        b.op("dve", lambda e: e.memset(zout[:], 0.0), writes=["dz_out"])
        for blk in range(NBLK):
            for oc in range(16):
                b.dma("sp", "dzo", lambda e, oc=oc, blk=blk: e.dma_start(
                    out=outT[oc * 128:(oc + 1) * 128, blk * WIN:(blk + 1) * WIN],
                    in_=zout[:]),
                    reads=["dz_out"])
        b.barrier()
        dzs.close()

    b.finalize(nc, root)
    root.close()
    return nc



def _host_inputs(inputs):
    x = np.asarray(inputs["L0_attn_x"])[0]
    xT_full = np.ascontiguousarray(x.T)
    shared = {
        "wqT": np.ascontiguousarray(np.asarray(inputs["L0_attn_Wq"]).T),
        "wkT": np.ascontiguousarray(np.asarray(inputs["L0_attn_Wk"]).T),
        "wvT": np.ascontiguousarray(np.asarray(inputs["L0_attn_Wv"]).T),
        "woT": np.ascontiguousarray(np.asarray(inputs["L0_attn_Wo"]).T),
        "wgT": np.ascontiguousarray(np.asarray(inputs["L0_ffn_w_gate"]).T),
        "wuT": np.ascontiguousarray(np.asarray(inputs["L0_ffn_w_up"]).T),
        "wdT": np.ascontiguousarray(np.asarray(inputs["L0_ffn_w_down"]).T),
    }
    shuf = np.zeros((HD, HD), dtype=np.float32)
    for i in range(HD // 2):
        shuf[2 * i, 2 * i + 1] = -1.0
        shuf[2 * i + 1, 2 * i] = 1.0
    shared["shufT"] = np.ascontiguousarray(shuf.T).astype(NPBF)
    jj = np.arange(128)[:, None]
    ii = np.arange(128)[None, :]
    shared["mprev"] = (jj > ii).astype(NPBF)
    shared["mcur"] = (jj <= ii).astype(NPBF)
    shared["ones_bf"] = np.ones((128, 1), dtype=NPBF)
    shared["onesrow_f"] = np.ones((1, 128), dtype=np.float32)
    shared["ident_f"] = np.eye(128, dtype=np.float32)

    invfreq = (1.0 / 10000.0 ** (np.arange(0, HD, 2, dtype=np.float32) / HD))
    freq_col = np.repeat(invfreq, 2)[:, None]

    in_maps = []
    for c in range(NCORES):
        g0 = c * S
        if c == 0:
            halo = np.zeros((D, WIN), dtype=np.float32)
        else:
            halo = xT_full[:, g0 - WIN:g0]
        xT_c = np.ascontiguousarray(
            np.concatenate([halo, xT_full[:, g0:g0 + S]], axis=1))
        pos = np.arange(g0 - WIN, g0 + S, dtype=np.float32)[None, :]
        ang = pos * freq_col
        m = dict(shared)
        m["xT"] = xT_c
        m["ropeC"] = np.cos(ang).astype(np.float32)
        m["ropeS"] = np.sin(ang).astype(np.float32)
        m["zcol"] = np.full((128, 1), 0.0 if c == 0 else 1.0, dtype=np.float32)
        in_maps.append(m)
    return in_maps


def kernel(**inputs):
    in_maps = _host_inputs(inputs)
    nc = build_program()
    res = run_bass_kernel_spmd(nc, in_maps, core_ids=list(range(NCORES)))
    parts = [np.asarray(r["outT"]).T for r in res.results]
    out = np.concatenate(parts, axis=0)[None]
    return np.ascontiguousarray(out, dtype=np.float32)


# version 10
# speedup vs baseline: 1.0221x; 1.0171x over previous
import numpy as np
import ml_dtypes

import concourse.bass as bass
import concourse.mybir as mybir
from concourse.bass_utils import run_bass_kernel_spmd
from concourse.mybir import (
    AluOpType as ALU,
    ActivationFunctionType as AF,
    AxisListType as AX,
)

F32 = mybir.dt.float32
BF16 = mybir.dt.bfloat16
I32 = mybir.dt.int32
NPBF = ml_dtypes.bfloat16

NCORES = 8
D = 2048
H = 16
HD = 128
WIN = 512
S = 2048
T = S + WIN
II = 8192
NBLK = S // WIN
EPS = 1e-5
C0 = 12.0
SCALE = HD ** -0.5

P1_LIMIT = 99
P1_ROPE = 3
ENG2ATTR = {"pe": "tensor", "act": "scalar", "dve": "vector",
            "pool": "gpsimd", "sp": "sync"}


class Builder:

    def __init__(self):
        self.prog = {e: [] for e in ENG2ATTR}
        self.counts = {}
        self.know = {e: {} for e in ENG2ATTR}
        self.last_writer = {}
        self.readers = {}
        self.groups = {}
        self.psum_keys = set()
        self.sems = []
        for e in ENG2ATTR:
            self._sem("c_" + e)

    def _sem(self, name):
        if name not in self.counts:
            self.sems.append(name)
            self.counts[name] = 0
        return name

    def _wait(self, eng, sem, val, tok_know):
        if sem.startswith("d_"):
            val = max(val, self.counts[sem])
        if self.know[eng].get(sem, 0) >= val:
            return
        if not (eng == "pe" and sem == "c_pe"):
            self.prog[eng].append(("wait", sem, val))
        k = self.know[eng]
        k[sem] = max(k.get(sem, 0), val)
        if tok_know:
            for s2, v2 in tok_know.items():
                if k.get(s2, 0) < v2:
                    k[s2] = v2

    def _deps(self, eng, reads, writes):
        need = {}

        def add(tok):
            if tok is None:
                return
            sem, val, kn = tok
            cur = need.get(sem)
            if cur is None or val > cur[0]:
                need[sem] = (val, kn)

        for r in reads:
            add(self.last_writer.get(r))
        for w in writes:
            add(self.last_writer.get(w))
            for tok in self.readers.get(w, ()):
                add(tok)
        for sem, (val, kn) in sorted(need.items()):
            self._wait(eng, sem, val, kn)

    def _attach(self, tok, rw_list):
        for rds, wrs in rw_list:
            for w in wrs:
                self.last_writer[w] = tok
                self.readers[w] = []
            for r in rds:
                self.readers.setdefault(r, []).append(tok)

    def reg_psum(self, *keys):
        self.psum_keys.update(keys)

    def op(self, eng, fn, reads=(), writes=(), group=None, final=True):
        locks = tuple(("rdl", r) for r in reads if r in self.psum_keys)
        writes = tuple(writes) + locks
        self._deps(eng, reads, writes)
        if group is not None:
            gk = (eng, group)
            self.groups.setdefault(gk, []).append((tuple(reads), tuple(writes)))
            if not final:
                self.prog[eng].append(("inst", fn, None, 0))
                return None
            sem = "c_" + eng
            self.counts[sem] += 1
            tok = (sem, self.counts[sem], dict(self.know[eng]))
            self.prog[eng].append(("inst", fn, sem, self.counts[sem]))
            self._attach(tok, self.groups.pop(gk))
            return tok
        sem = "c_" + eng
        self.counts[sem] += 1
        tok = (sem, self.counts[sem], dict(self.know[eng]))
        self.prog[eng].append(("inst", fn, sem, self.counts[sem]))
        self._attach(tok, [(tuple(reads), tuple(writes))])
        return tok

    def dma(self, eng, stream, fn, reads=(), writes=()):
        self._deps(eng, reads, writes)
        sem = self._sem("d_" + stream)
        self.counts[sem] += 16
        tok = (sem, self.counts[sem], dict(self.know[eng]))
        self.prog[eng].append(("dma", fn, sem, self.counts[sem]))
        self._attach(tok, [(tuple(reads), tuple(writes))])
        return tok

    def barrier(self):
        for eng in ENG2ATTR:
            for sem in self.sems:
                if self.counts[sem] > 0:
                    self._wait(eng, sem, self.counts[sem], None)

    def finalize(self, nc, es):
        sem_handles = {name: es.enter_context(nc.semaphore(name))
                       for name in self.sems}
        out_waits = [(s, self.counts[s]) for s in self.sems
                     if s.startswith("d_") and self.counts[s] > 0]
        block = es.enter_context(nc.Block())
        for eng, attr in ENG2ATTR.items():
            items = self.prog[eng]

            def body(e, items=items, eng=eng):
                for it in items:
                    if it[0] == "wait":
                        e.wait_ge(sem_handles[it[1]], it[2])
                    else:
                        _, fn, sem, val = it
                        inst = fn(e)
                        if sem is not None:
                            inst.then_inc(sem_handles[sem],
                                          16 if it[0] == "dma" else 1)
                if eng == "sp":
                    for sname, cnt in out_waits:
                        e.wait_ge(sem_handles[sname], cnt)
            getattr(block, attr)(body)


def build_program(up_to=99):
    nc = bass.Bass("TRN2", target_bir_lowering=False, debug=False)
    b = Builder()
    b.reg_psum("p0_tpp", "p0_bcp",
               "p1_qps0", "p1_qps1", "p1_sps0", "p1_sps1", "p1_vps0", "p1_vps1",
               "p2_scp0", "p2_scp1", "p2_ovp0", "p2_ovp1", "p2_smp0", "p2_smp1",
               "p2_bcpp0", "p2_bcpp1",
               "p3_prp0", "p3_prp1", "p3_stp", "p3_sqp", "p3_b3p", "p3_trp", "p3_mip",
               "p4_xbp", "p4_gp0", "p4_gp1", "p4_up0", "p4_up1", "p4_trp", "p4_mip",
               *[f"p5_dnp{i}" for i in range(8)],
               "p6_stp", "p6_sqp", "p6_b6p")
    from contextlib import ExitStack
    root = ExitStack()

    def ein(name, shape, dt):
        return nc.dram_tensor(name, shape, dt, kind="ExternalInput").ap()

    xT = ein("xT", [D, T], F32)
    wqT = ein("wqT", [D, D], F32)
    wkT = ein("wkT", [D, D], F32)
    wvT = ein("wvT", [D, D], F32)
    woT = ein("woT", [D, D], F32)
    wgT = ein("wgT", [D, II], F32)
    wuT = ein("wuT", [D, II], F32)
    wdT = ein("wdT", [II, D], F32)
    ropeC = ein("ropeC", [HD, T], F32)
    ropeS = ein("ropeS", [HD, T], F32)
    shufT = ein("shufT", [HD, HD], BF16)
    mprev_i = ein("mprev", [128, 128], BF16)
    mcur_i = ein("mcur", [128, 128], BF16)
    zcol_i = ein("zcol", [128, 1], F32)
    ones_bf_i = ein("ones_bf", [128, 1], BF16)
    onesrow_i = ein("onesrow_f", [1, 128], F32)
    ident_i = ein("ident_f", [128, 128], F32)

    outT = nc.dram_tensor("outT", [D, S], F32, kind="ExternalOutput").ap()

    def scr(name, shape, dt):
        return nc.dram_tensor(name, shape, dt, kind="Internal").ap()

    qrot_h = scr("qrot_h", [D, S], BF16)
    krot_h = scr("krot_h", [D, T], BF16)
    v_h = scr("v_h", [T, D], BF16)
    ofm_h = scr("ofm_h", [D, S], BF16)
    r1_h = scr("r1_h", [D, S], F32)
    xq_h = scr("xq_h", [D, S], BF16)
    h_h = scr("h_h", [II, S], F32)
    wqd_h = scr("wqd_h", [II, D], BF16)
    wqg_h = scr("wqg_h", [128, 16, II], BF16)
    wqu_h = scr("wqu_h", [128, 16, II], BF16)
    hq_h = scr("hq_h", [II, S], BF16)
    r2_h = scr("r2_h", [D, S], F32)
    xs_h = scr("xs_h", [1, S], F32)
    s2i_h = scr("s2i_h", [1, S], F32)
    dsc_h = scr("dsc_h", [1, S], F32)

    def mkalloc(es):
        def alloc(name, shape, dt):
            return es.enter_context(nc.sbuf_tensor(name, shape, dt))
        return alloc

    pal = mkalloc(root)
    consts = {}
    for nm, src_ap, shape, dt in [
        ("shufT", shufT, [HD, HD], BF16), ("mprev", mprev_i, [128, 128], BF16),
        ("mcur", mcur_i, [128, 128], BF16), ("zcol", zcol_i, [128, 1], F32),
        ("ones_bf", ones_bf_i, [128, 1], BF16),
        ("onesrow_f", onesrow_i, [1, 128], F32),
        ("ident_f", ident_i, [128, 128], F32),
    ]:
        t = pal("c_" + nm, shape, dt)
        b.dma("sp", "cst", lambda e, t=t, s=src_ap: e.dma_start(out=t[:], in_=s),
              writes=["c_" + nm])
        consts[nm] = t
    cbias = pal("cbias", [128, 1], F32)
    epsrow = pal("epsrow", [1, 1], F32)
    b.op("dve", lambda e: e.memset(cbias[:], -C0), writes=["cbias"])
    b.op("dve", lambda e: e.memset(epsrow[:], EPS), writes=["epsrow"])
    wscol = {m: pal(f"wscol_{m}", [128, 1], F32) for m in "gud"}
    taucol = {m: pal(f"taucol_{m}", [128, 1], F32) for m in "gud"}
    ntaucol = {m: pal(f"ntaucol_{m}", [128, 1], F32) for m in "gud"}
    wssc = {m: pal(f"wssc_{m}", [1, 1], F32) for m in "gud"}

    p01 = ExitStack()
    al = mkalloc(p01)
    wt0 = [al(f"p0_wt{i}", [128, 1024], F32) for i in range(2)]
    redc = al("p0_redc", [128, 128], F32)
    lo_t = al("p0_lo", [128, 128], F32)
    ts_t = al("p0_ts", [128, 64], F32)
    tb_t = al("p0_tb", [128, 64], F32)
    te_t = al("p0_te", [128, 64], F32)
    hrow = al("p0_hrow", [1, 128], F32)
    lrow = al("p0_lrow", [1, 128], F32)
    trow = [al(f"p0_trow{i}", [1, 64], F32) for i in range(3)]
    tp_ps = p01.enter_context(nc.psum_tensor("p0_tp", [128, 128], F32))
    bc_ps = p01.enter_context(nc.psum_tensor("p0_bc", [128, 1], F32))

    def twosum_level(hi_ap, lo_ap, hik, lok, n, ts, tbb, te, tsk, tbk, tek):
        h2 = n // 2
        A, B = hi_ap[:, 0:h2], hi_ap[:, h2:n]
        b.op("dve", lambda e: e.tensor_tensor(ts[:, 0:h2], A, B, ALU.add),
             reads=[hik], writes=[tsk])
        b.op("dve", lambda e: e.tensor_tensor(tbb[:, 0:h2], ts[:, 0:h2], A, ALU.subtract),
             reads=[tsk, hik], writes=[tbk])
        b.op("dve", lambda e: e.tensor_tensor(te[:, 0:h2], ts[:, 0:h2], tbb[:, 0:h2], ALU.subtract),
             reads=[tsk, tbk], writes=[tek])
        b.op("dve", lambda e: e.tensor_tensor(te[:, 0:h2], A, te[:, 0:h2], ALU.subtract),
             reads=[hik, tek], writes=[tek])
        b.op("dve", lambda e: e.tensor_tensor(tbb[:, 0:h2], B, tbb[:, 0:h2], ALU.subtract),
             reads=[hik, tbk], writes=[tbk])
        b.op("dve", lambda e: e.tensor_tensor(te[:, 0:h2], te[:, 0:h2], tbb[:, 0:h2], ALU.add),
             reads=[tek, tbk], writes=[tek])
        b.op("dve", lambda e: e.tensor_tensor(tbb[:, 0:h2], lo_ap[:, 0:h2], lo_ap[:, h2:n], ALU.add),
             reads=[lok], writes=[tbk])
        b.op("dve", lambda e: e.tensor_tensor(lo_ap[:, 0:h2], tbb[:, 0:h2], te[:, 0:h2], ALU.add),
             reads=[tbk, tek], writes=[lok])
        b.op("dve", lambda e: e.tensor_copy(hi_ap[:, 0:h2], ts[:, 0:h2]),
             reads=[tsk], writes=[hik])

    def gen_p0a():
        for m, wT_ap, rows in (("g", wgT, D), ("u", wuT, D), ("d", wdT, II)):
            cols = II if rows == D else D
            ti = 0
            for r0 in range(0, rows, 128):
                for c0 in range(0, cols, 1024):
                    sl = ti % 2
                    col = ti % 128
                    if ti >= 128:
                        raise AssertionError("tile count per matrix must be 128")
                    b.dma("sp", f"p0w{sl}",
                          lambda e, sl=sl, r0=r0, c0=c0, wT_ap=wT_ap:
                          e.dma_start(out=wt0[sl][:], in_=wT_ap[r0:r0 + 128, c0:c0 + 1024]),
                          writes=[f"p0_wt{sl}"])
                    b.op("dve", lambda e, sl=sl, col=col: e.tensor_reduce(
                        redc[:, col:col + 1], wt0[sl][:], AX.X, ALU.add,
                        apply_absolute_value=True),
                        reads=[f"p0_wt{sl}"], writes=["p0_redc"])
                    ti += 1
                    yield
            b.op("pool", lambda e: e.memset(lo_t[:], 0.0), writes=["p0_lo"])
            yield
            n = 128
            while n > 1:
                twosum_level(redc, lo_t, "p0_redc", "p0_lo", n, ts_t, tb_t, te_t,
                             "p0_ts", "p0_tb", "p0_te")
                n //= 2
                yield
            b.op("pe", lambda e: e.transpose(tp_ps[0:1, 0:128], redc[:, 0:1],
                                             consts["ident_f"][:]),
                 reads=["p0_redc", "c_ident_f"], writes=["p0_tpp"])
            b.op("act", lambda e: e.copy(hrow[:], tp_ps[0:1, 0:128]),
                 reads=["p0_tpp"], writes=["p0_hrow"])
            b.op("pe", lambda e: e.transpose(tp_ps[0:1, 0:128], lo_t[:, 0:1],
                                             consts["ident_f"][:]),
                 reads=["p0_lo", "c_ident_f", "p0_hrow"], writes=["p0_tpp"])
            b.op("act", lambda e: e.copy(lrow[:], tp_ps[0:1, 0:128]),
                 reads=["p0_tpp"], writes=["p0_lrow"])
            yield
            n = 128
            while n > 1:
                twosum_level(hrow, lrow, "p0_hrow", "p0_lrow", n,
                             trow[0], trow[1], trow[2],
                             "p0_tr0", "p0_tr1", "p0_tr2")
                n //= 2
                yield
            b.op("dve", lambda e, m=m: e.tensor_tensor(wssc[m][:], hrow[0:1, 0:1],
                                                       lrow[0:1, 0:1], ALU.add),
                 reads=["p0_hrow", "p0_lrow"], writes=[f"wssc_{m}"])
            inv_n = 1.0 / float(rows * cols)
            b.op("dve", lambda e, m=m, inv_n=inv_n: e.tensor_scalar(
                wssc[m][:], wssc[m][:], inv_n, EPS, ALU.mult, ALU.add),
                reads=[f"wssc_{m}"], writes=[f"wssc_{m}"])
            b.op("pe", lambda e, m=m: e.matmul(bc_ps[:], consts["onesrow_f"][:],
                                               wssc[m][:], start=True, stop=True),
                 reads=["c_onesrow_f", f"wssc_{m}"], writes=["p0_bcp"])
            b.op("act", lambda e, m=m: e.copy(wscol[m][:], bc_ps[:]),
                 reads=["p0_bcp"], writes=[f"wscol_{m}"])
            b.op("dve", lambda e, m=m: e.tensor_scalar(taucol[m][:], wscol[m][:], 0.5, None, ALU.mult),
                 reads=[f"wscol_{m}"], writes=[f"taucol_{m}"])
            b.op("dve", lambda e, m=m: e.tensor_scalar(ntaucol[m][:], wscol[m][:], -0.5, None, ALU.mult),
                 reads=[f"wscol_{m}"], writes=[f"ntaucol_{m}"])
            yield

    _p0_iter = gen_p0a()
    _p0_done = [False]

    def pump(k=1):
        for _ in range(k):
            if not _p0_done[0]:
                try:
                    next(_p0_iter)
                except StopIteration:
                    _p0_done[0] = True

    if up_to >= 1:
        p1 = ExitStack()
        al = mkalloc(p1)
        xfm = al("p1_xfm", [128, 16 * T], BF16)
        xst = [al(f"p1_xst{i}", [128, T], F32) for i in range(2)]
        ctab = al("p1_ctab", [HD, T], F32)
        stab = al("p1_stab", [HD, T], F32)
        b.dma("sp", "p1c", lambda e: e.dma_start(out=ctab[:], in_=ropeC), writes=["p1_ctab"])
        b.dma("sp", "p1c", lambda e: e.dma_start(out=stab[:], in_=ropeS), writes=["p1_stab"])
        for c in range(16):
            sl = c % 2
            b.dma("sp", f"p1x{sl}", lambda e, sl=sl, c=c: e.dma_start(
                out=xst[sl][:], in_=xT[c * 128:(c + 1) * 128, :]), writes=[f"p1_xst{sl}"])
            b.op("dve", lambda e, sl=sl, c=c: e.tensor_copy(
                xfm[:, c * T:(c + 1) * T], xst[sl][:]),
                reads=[f"p1_xst{sl}"], writes=[("p1_xfm", c)])

        whalf = al("p1_whalf", [128, 16 * 1024], BF16)
        wst = [al(f"p1_wst{i}", [128, 1024], F32) for i in range(2)]
        p1ps = ExitStack()
        q_ps = [p1ps.enter_context(nc.psum_tensor(f"p1_qps{i}", [128, 512], F32)) for i in range(2)]
        s_ps = [p1ps.enter_context(nc.psum_tensor(f"p1_sps{i}", [128, 512], F32)) for i in range(2)]
        v_ps = [p1ps.enter_context(nc.psum_tensor(f"p1_vps{i}", [128, 512], F32)) for i in range(2)]
        qsb = [al(f"p1_qsb{i}", [128, 512], BF16) for i in range(2)]
        t1 = [al(f"p1_t1{i}", [128, 512], F32) for i in range(2)]
        t2 = [al(f"p1_t2{i}", [128, 512], F32) for i in range(2)]
        rout = [al(f"p1_rout{i}", [128, 512], BF16) for i in range(2)]
        vout = [al(f"p1_vout{i}", [128, 512], BF16) for i in range(2)]

        def load_whalf(wT_ap, half, scale, wkey="p1_wh"):
            for ic in range(16):
                sl = ic % 2
                b.dma("sp", f"p1w{sl}", lambda e, ic=ic, sl=sl, half=half, wT_ap=wT_ap:
                      e.dma_start(out=wst[sl][:],
                                  in_=wT_ap[ic * 128:(ic + 1) * 128,
                                            half * 1024:(half + 1) * 1024]),
                      writes=[f"p1_wst{sl}"])
                if scale is None:
                    b.op("dve", lambda e, ic=ic, sl=sl: e.tensor_copy(
                        whalf[:, ic * 1024:(ic + 1) * 1024], wst[sl][:]),
                        reads=[f"p1_wst{sl}"], writes=[(wkey, ic)])
                else:
                    b.op("dve", lambda e, ic=ic, sl=sl, scale=scale: e.tensor_scalar(
                        whalf[:, ic * 1024:(ic + 1) * 1024], wst[sl][:], scale, None, ALU.mult),
                        reads=[f"p1_wst{sl}"], writes=[(wkey, ic)])

        def qk_sweep(dst_h, wT_ap, scale, nblocks, blk_off, wkey="p1_wh"):
            for half in range(2 if P1_LIMIT >= 5 else 1):
                load_whalf(wT_ap, half, scale)
                for blk in range(nblocks if P1_LIMIT >= 5 else 1):
                    tb = blk_off + blk * WIN
                    for ol in range(8):
                        pump(2)
                        oc = half * 8 + ol
                        pp = (blk * 8 + ol) % 2
                        grp = f"qk{oc}_{blk}"
                        for ic in range(16):
                            b.op("pe", lambda e, ic=ic, ol=ol, tb=tb, pp=pp:
                                 e.matmul(q_ps[pp][:],
                                          whalf[:, ic * 1024 + ol * 128: ic * 1024 + (ol + 1) * 128],
                                          xfm[:, ic * T + tb: ic * T + tb + WIN],
                                          start=(ic == 0), stop=(ic == 15)),
                                 reads=[(wkey, ic), ("p1_xfm", ic)],
                                 writes=[f"p1_qps{pp}"], group=grp, final=(ic == 15))
                        if P1_ROPE == 0:
                            b.op("act", lambda e, pp=pp: e.copy(rout[pp][:], q_ps[pp][:]),
                                 reads=[f"p1_qps{pp}"], writes=[f"p1_rout{pp}"])
                        elif P1_ROPE == 1:
                            b.op("act", lambda e, pp=pp: e.copy(qsb[pp][:], q_ps[pp][:]),
                                 reads=[f"p1_qps{pp}"], writes=[f"p1_qsb{pp}"])
                            b.op("pe", lambda e, pp=pp: e.matmul(s_ps[pp][:], consts["shufT"][:],
                                                                 qsb[pp][:], start=True, stop=True),
                                 reads=["c_shufT", f"p1_qsb{pp}"], writes=[f"p1_sps{pp}"])
                            b.op("act", lambda e, pp=pp: e.copy(rout[pp][:], s_ps[pp][:]),
                                 reads=[f"p1_sps{pp}"], writes=[f"p1_rout{pp}"])
                        elif P1_ROPE == 2:
                            b.op("dve", lambda e, pp=pp, tb=tb: e.tensor_tensor(
                                t1[pp][:], q_ps[pp][:], ctab[:, tb:tb + WIN], ALU.mult),
                                reads=[f"p1_qps{pp}", "p1_ctab"], writes=[f"p1_t1{pp}"])
                            b.op("dve", lambda e, pp=pp: e.tensor_copy(rout[pp][:], t1[pp][:]),
                                 reads=[f"p1_t1{pp}"], writes=[f"p1_rout{pp}"])
                        elif P1_ROPE == 4:
                            b.op("act", lambda e, pp=pp: e.copy(qsb[pp][:], q_ps[pp][:]),
                                 reads=[f"p1_qps{pp}"], writes=[f"p1_qsb{pp}"])
                            b.op("pe", lambda e, pp=pp: e.matmul(s_ps[pp][:], consts["shufT"][:],
                                                                 qsb[pp][:], start=True, stop=True),
                                 reads=["c_shufT", f"p1_qsb{pp}"], writes=[f"p1_sps{pp}"])
                            b.op("dve", lambda e, pp=pp, tb=tb: e.tensor_tensor(
                                t1[pp][:], q_ps[pp][:], ctab[:, tb:tb + WIN], ALU.mult),
                                reads=[f"p1_qps{pp}", "p1_ctab"], writes=[f"p1_t1{pp}"])
                            b.op("dve", lambda e, pp=pp, tb=tb: e.tensor_tensor(
                                t2[pp][:], s_ps[pp][:], stab[:, tb:tb + WIN], ALU.mult),
                                reads=[f"p1_sps{pp}", "p1_stab"], writes=[f"p1_t2{pp}"])
                            b.op("dve", lambda e, pp=pp: e.tensor_tensor(
                                rout[pp][:], t1[pp][:], t2[pp][:], ALU.add),
                                reads=[f"p1_t1{pp}", f"p1_t2{pp}"], writes=[f"p1_rout{pp}"])
                        elif P1_ROPE == 5:
                            b.op("act", lambda e, pp=pp: e.copy(qsb[pp][:], q_ps[pp][:]),
                                 reads=[f"p1_qps{pp}"], writes=[f"p1_qsb{pp}"])
                            b.op("pe", lambda e, pp=pp: e.matmul(s_ps[pp][:], consts["shufT"][:],
                                                                 qsb[pp][:], start=True, stop=True),
                                 reads=["c_shufT", f"p1_qsb{pp}"], writes=[f"p1_sps{pp}"])
                            b.op("dve", lambda e, pp=pp, tb=tb: e.tensor_tensor(
                                t2[pp][:], s_ps[pp][:], stab[:, tb:tb + WIN], ALU.mult),
                                reads=[f"p1_sps{pp}", "p1_stab"], writes=[f"p1_t2{pp}"])
                            b.op("dve", lambda e, pp=pp: e.tensor_copy(rout[pp][:], t2[pp][:]),
                                 reads=[f"p1_t2{pp}"], writes=[f"p1_rout{pp}"])
                        else:
                            b.op("act", lambda e, pp=pp: e.copy(qsb[pp][:], q_ps[pp][:]),
                                 reads=[f"p1_qps{pp}"], writes=[f"p1_qsb{pp}"])
                            b.op("pe", lambda e, pp=pp: e.matmul(s_ps[pp][:], consts["shufT"][:],
                                                                 qsb[pp][:], start=True, stop=True),
                                 reads=["c_shufT", f"p1_qsb{pp}"], writes=[f"p1_sps{pp}"])
                            b.op("dve", lambda e, pp=pp, tb=tb: e.tensor_tensor(
                                t1[pp][:], q_ps[pp][:], ctab[:, tb:tb + WIN], ALU.mult),
                                reads=[f"p1_qps{pp}", "p1_ctab"], writes=[f"p1_t1{pp}"])
                            b.op("dve", lambda e, pp=pp, tb=tb: e.tensor_tensor(
                                t2[pp][:], s_ps[pp][:], stab[:, tb:tb + WIN], ALU.mult),
                                reads=[f"p1_sps{pp}", "p1_stab"], writes=[f"p1_t2{pp}"])
                            b.op("dve", lambda e, pp=pp: e.tensor_tensor(
                                rout[pp][:], t1[pp][:], t2[pp][:], ALU.add),
                                reads=[f"p1_t1{pp}", f"p1_t2{pp}"], writes=[f"p1_rout{pp}"])
                        b.dma("act", f"p1o{pp}",
                              lambda e, pp=pp, oc=oc, tb=tb, dst_h=dst_h, blk_off=blk_off:
                              e.dma_start(out=dst_h[oc * 128:(oc + 1) * 128,
                                                    tb - blk_off: tb - blk_off + WIN],
                                          in_=rout[pp][:]),
                              reads=[f"p1_rout{pp}"])

        if P1_LIMIT >= 2:
            qk_sweep(qrot_h, wqT, SCALE, NBLK, WIN)
        if P1_LIMIT >= 3:
            qk_sweep(krot_h, wkT, None, NBLK + 1, 0)

        for half in (range(2) if P1_LIMIT >= 4 else []):
            load_whalf(wvT, half, None)
            for blk in range(NBLK + 1):
                for tt in range(4):
                    tb = blk * WIN + tt * 128
                    for og in range(2):
                        pump(2)
                        oc0 = half * 1024 + og * 512
                        pp = (tt * 2 + og) % 2
                        grp = f"v{blk}_{tt}_{og}"
                        for ic in range(16):
                            b.op("pe", lambda e, ic=ic, tb=tb, og=og, pp=pp:
                                 e.matmul(v_ps[pp][:],
                                          xfm[:, ic * T + tb: ic * T + tb + 128],
                                          whalf[:, ic * 1024 + og * 512: ic * 1024 + (og + 1) * 512],
                                          start=(ic == 0), stop=(ic == 15)),
                                 reads=[("p1_wh", ic), ("p1_xfm", ic)],
                                 writes=[f"p1_vps{pp}"], group=grp, final=(ic == 15))
                        b.op("act", lambda e, pp=pp: e.copy(vout[pp][:], v_ps[pp][:]),
                             reads=[f"p1_vps{pp}"], writes=[f"p1_vout{pp}"])
                        b.dma("act", f"p1v{pp}", lambda e, pp=pp, tb=tb, oc0=oc0:
                              e.dma_start(out=v_h[tb:tb + 128, oc0:oc0 + 512], in_=vout[pp][:]),
                              reads=[f"p1_vout{pp}"])

        pump(10000)
        b.barrier()
        p1ps.close()
        p1.close()
        p01.close()


    if up_to >= 2:
        p2 = ExitStack()
        al = mkalloc(p2)
        qblk = al("p2_q", [128, 16 * WIN], BF16)
        kbuf = [al(f"p2_k{i}", [128, 16 * WIN], BF16) for i in range(2)]
        vbuf = [al(f"p2_v{i}", [128, 4 * D], BF16) for i in range(2)]
        ptl = [al(f"p2_pt{i}", [128, WIN], BF16) for i in range(2)]
        bcs = [al(f"p2_bc{i}", [128, WIN], F32) for i in range(2)]
        rec = [al(f"p2_rec{i}", [1, WIN], F32) for i in range(2)]
        oev = [al(f"p2_oev{i}", [128, WIN], BF16) for i in range(2)]
        wdst = [al(f"p2_wd{i}", [128, 2048], F32) for i in range(2)]
        wqd_sb = [al(f"p2_wqd{i}", [128, 2048], BF16) for i in range(2)]
        qa2 = al("p2_qa", [128, 2048], BF16)
        qb2q = al("p2_qb", [128, 2048], BF16)
        zr512 = al("p2_zr", [128, WIN], BF16)
        b.op("pool", lambda e: e.memset(zr512[:], 0.0), writes=["p2_zr"])
        p2ps = ExitStack()
        sc_ps = [p2ps.enter_context(nc.psum_tensor(f"p2_sc{i}", [128, WIN], F32)) for i in range(2)]
        ov_ps = [p2ps.enter_context(nc.psum_tensor(f"p2_ov{i}", [128, WIN], F32)) for i in range(2)]
        sm_ps = [p2ps.enter_context(nc.psum_tensor(f"p2_sm{i}", [1, WIN], F32)) for i in range(2)]
        bc_ps2 = [p2ps.enter_context(nc.psum_tensor(f"p2_bcp{i}", [128, WIN], F32)) for i in range(2)]

        def load_kv(blk, slot):
            for c in range(16):
                b.dma("sp", f"p2k{slot}", lambda e, c=c, blk=blk, slot=slot: e.dma_start(
                    out=kbuf[slot][:, c * WIN:(c + 1) * WIN],
                    in_=krot_h[c * 128:(c + 1) * 128, blk * WIN:(blk + 1) * WIN]),
                    writes=[(f"p2_k{slot}",)])
            for tt in range(4):
                b.dma("sp", f"p2v{slot}", lambda e, tt=tt, blk=blk, slot=slot: e.dma_start(
                    out=vbuf[slot][:, tt * D:(tt + 1) * D],
                    in_=v_h[blk * WIN + tt * 128: blk * WIN + (tt + 1) * 128, :]),
                    writes=[(f"p2_v{slot}",)])

        load_kv(0, 0)
        for blk in range(NBLK):
            cur = (blk + 1) % 2
            prv = blk % 2
            load_kv(blk + 1, cur)
            for c in range(16):
                b.dma("act", "p2q", lambda e, c=c, blk=blk: e.dma_start(
                    out=qblk[:, c * WIN:(c + 1) * WIN],
                    in_=qrot_h[c * 128:(c + 1) * 128, blk * WIN:(blk + 1) * WIN]),
                    writes=[("p2_q",)])
            def emit_wq_chunk(kind, c):
                sl = c % 2
                if kind == "d":
                    b.dma("sp", f"p2wd{sl}", lambda e, c=c, sl=sl: e.dma_start(
                        out=wdst[sl][:], in_=wdT[c * 128:(c + 1) * 128, :]),
                        writes=[f"p2_wd{sl}"])
                else:
                    rt, cq = c % 16, c // 16
                    b.dma("sp", f"p2wd{sl}", lambda e, rt=rt, cq=cq, sl=sl: e.dma_start(
                        out=wdst[sl][:], in_=wgT[rt * 128:(rt + 1) * 128,
                                                 cq * 2048:(cq + 1) * 2048]),
                        writes=[f"p2_wd{sl}"])
                b.op("dve", lambda e, sl=sl, kind=kind: e.tensor_scalar(
                    qa2[:], wdst[sl][:], taucol[kind][:], None, ALU.is_gt),
                    reads=[f"p2_wd{sl}", f"taucol_{kind}"], writes=["p2_qa"])
                b.op("dve", lambda e, sl=sl, kind=kind: e.tensor_scalar(
                    qb2q[:], wdst[sl][:], ntaucol[kind][:], None, ALU.is_lt),
                    reads=[f"p2_wd{sl}", f"ntaucol_{kind}"], writes=["p2_qb"])
                b.op("dve", lambda e, sl=sl: e.tensor_tensor(
                    wqd_sb[sl][:], qa2[:], qb2q[:], ALU.subtract),
                    reads=["p2_qa", "p2_qb"], writes=[f"p2_wqd{sl}"])
                if kind == "d":
                    b.dma("act", f"p2wdo{sl}", lambda e, c=c, sl=sl: e.dma_start(
                        out=wqd_h[c * 128:(c + 1) * 128, :], in_=wqd_sb[sl][:]),
                        reads=[f"p2_wqd{sl}"])
                else:
                    rt, cq = c % 16, c // 16
                    b.dma("act", f"p2wdo{sl}", lambda e, rt=rt, cq=cq, sl=sl: e.dma_start(
                        out=wqg_h[:, rt:rt + 1, cq * 2048:(cq + 1) * 2048],
                        in_=wqd_sb[sl][:]),
                        reads=[f"p2_wqd{sl}"])

            for hh in range(H):
                emit_wq_chunk("d", blk * 16 + hh)
                emit_wq_chunk("g", blk * 16 + hh)
                pp = hh % 2
                b.op("pe", lambda e, pp=pp: e.matmul(
                    sm_ps[pp][:], consts["ones_bf"][:], zr512[:],
                    start=True, stop=False),
                    reads=["c_ones_bf", "p2_zr"], writes=[f"p2_smp{pp}"])
                b.op("pe", lambda e, pp=pp: e.matmul(
                    ov_ps[pp][:], consts["mprev"][:], zr512[:],
                    start=True, stop=False),
                    reads=["c_mprev", "p2_zr"], writes=[f"p2_ovp{pp}"])
                for jc in range(8):
                    if jc < 4:
                        ksl, kc = prv, jc
                        q0, q1 = 0, (jc + 1) * 128
                        dg = jc * 128
                        mk, mkk = consts["mprev"], "c_mprev"
                    else:
                        ksl, kc = cur, jc - 4
                        q0, q1 = (jc - 4) * 128, WIN
                        dg = (jc - 4) * 128
                        mk, mkk = consts["mcur"], "c_mcur"
                    qw = q1 - q0
                    pj = jc % 2
                    b.op("pe", lambda e, pj=pj, ksl=ksl, kc=kc, hh=hh, q0=q0, q1=q1, qw=qw:
                         e.matmul(sc_ps[pj][:, 0:qw],
                                  kbuf[ksl][:, hh * WIN + kc * 128: hh * WIN + (kc + 1) * 128],
                                  qblk[:, hh * WIN + q0: hh * WIN + q1],
                                  start=True, stop=True),
                         reads=[(f"p2_k{ksl}",), ("p2_q",)], writes=[f"p2_scp{pj}"])
                    b.op("act", lambda e, pj=pj, qw=qw: e.activation(
                        ptl[pj][:, 0:qw], sc_ps[pj][:, 0:qw], AF.Exp, bias=cbias[:]),
                        reads=[f"p2_scp{pj}", "cbias"], writes=[f"p2_pt{pj}"])
                    b.op("dve", lambda e, pj=pj, dg=dg, q0=q0, mk=mk: e.tensor_tensor(
                        ptl[pj][:, dg - q0: dg - q0 + 128],
                        ptl[pj][:, dg - q0: dg - q0 + 128], mk[:], ALU.mult),
                        reads=[f"p2_pt{pj}", mkk], writes=[f"p2_pt{pj}"])
                    if blk == 0 and jc < 4:
                        b.op("dve", lambda e, pj=pj, qw=qw: e.tensor_scalar(
                            ptl[pj][:, 0:qw], ptl[pj][:, 0:qw], consts["zcol"][:], None, ALU.mult),
                            reads=[f"p2_pt{pj}", "c_zcol"], writes=[f"p2_pt{pj}"])
                    segs = []
                    if jc < 4:
                        segs.append((0, (jc + 1) * 128, False))
                    else:
                        segs.append((q0, WIN, False))
                    vsl = prv if jc < 4 else cur
                    vtt = kc
                    last = (jc == 7)
                    for si, (a0, a1, st) in enumerate(segs):
                        fin = last and si == len(segs) - 1
                        b.op("pe", lambda e, pp=pp, pj=pj, a0=a0, a1=a1, q0=q0, fin=fin:
                             e.matmul(sm_ps[pp][:, a0:a1], consts["ones_bf"][:],
                                      ptl[pj][:, a0 - q0:a1 - q0], start=False, stop=fin),
                             reads=["c_ones_bf", f"p2_pt{pj}"],
                             writes=[f"p2_smp{pp}"])
                        b.op("pe", lambda e, pp=pp, pj=pj, a0=a0, a1=a1, q0=q0,
                             vsl=vsl, vtt=vtt, hh=hh, fin=fin:
                             e.matmul(ov_ps[pp][:, a0:a1],
                                      vbuf[vsl][:, vtt * D + hh * 128: vtt * D + (hh + 1) * 128],
                                      ptl[pj][:, a0 - q0:a1 - q0], start=False, stop=fin),
                             reads=[(f"p2_v{vsl}",), f"p2_pt{pj}"],
                             writes=[f"p2_ovp{pp}"])
                b.op("dve", lambda e, pp=pp: e.reciprocal(rec[pp][:], sm_ps[pp][:]),
                     reads=[f"p2_smp{pp}"], writes=[f"p2_rec{pp}"])
                b.op("pe", lambda e, pp=pp: e.matmul(bc_ps2[pp][:], consts["onesrow_f"][:],
                                                     rec[pp][:], start=True, stop=True),
                     reads=["c_onesrow_f", f"p2_rec{pp}"], writes=[f"p2_bcpp{pp}"])
                b.op("act", lambda e, pp=pp: e.copy(bcs[pp][:], bc_ps2[pp][:]),
                     reads=[f"p2_bcpp{pp}"], writes=[f"p2_bc{pp}"])
                b.op("dve", lambda e, pp=pp: e.tensor_tensor(
                    oev[pp][:], ov_ps[pp][:], bcs[pp][:], ALU.mult),
                    reads=[f"p2_ovp{pp}", f"p2_bc{pp}"], writes=[f"p2_oev{pp}"])
                b.dma("act", f"p2o{pp}", lambda e, pp=pp, hh=hh, blk=blk: e.dma_start(
                    out=ofm_h[hh * 128:(hh + 1) * 128, blk * WIN:(blk + 1) * WIN],
                    in_=oev[pp][:]),
                    reads=[f"p2_oev{pp}"])
        b.barrier()
        p2ps.close()
        p2.close()


    if up_to >= 3:
        p3 = ExitStack()
        al = mkalloc(p3)
        wo_sb = al("p3_wo", [128, 16 * D], BF16)
        wst3 = [al(f"p3_wst{i}", [128, 1024], F32) for i in range(2)]
        for ic in range(16):
            for hf in range(2):
                sl = (ic * 2 + hf) % 2
                b.dma("sp", f"p3w{sl}", lambda e, ic=ic, hf=hf, sl=sl: e.dma_start(
                    out=wst3[sl][:],
                    in_=woT[ic * 128:(ic + 1) * 128, hf * 1024:(hf + 1) * 1024]),
                    writes=[f"p3_wst{sl}"])
                b.op("dve", lambda e, ic=ic, hf=hf, sl=sl: e.tensor_copy(
                    wo_sb[:, ic * D + hf * 1024: ic * D + (hf + 1) * 1024], wst3[sl][:]),
                    reads=[f"p3_wst{sl}"], writes=[("p3_wo", ic)])

        ofm_sb = al("p3_ofm", [128, 16 * WIN], BF16)
        xT_t = [al(f"p3_xT{i}", [128, WIN], F32) for i in range(2)]
        r1_sb = al("p3_r1", [128, 16 * WIN], F32)
        xs_row = al("xs_row", [1, WIN], F32)
        xsi_row = al("xsi_row", [1, WIN], F32)
        rbf = [al(f"p3_rbf{i}", [128, WIN], BF16) for i in range(2)]
        sqv = [al(f"p3_sq{i}", [128, WIN], BF16) for i in range(2)]
        srow = al("p3_srow", [1, WIN], F32)
        vrow = al("p3_vrow", [1, WIN], F32)
        rrow = al("p3_rrow", [1, WIN], F32)
        mrow = al("p3_mrow", [1, WIN], F32)
        rstdb = al("p3_rstdb", [128, WIN], F32)
        mrsb = al("p3_mrsb", [128, WIN], F32)
        tmax = al("p3_tmax", [128, WIN], F32)
        tmin = al("p3_tmin", [128, WIN], F32)
        ttmp = al("p3_ttmp", [128, WIN], F32)
        scol = al("p3_scol", [128, 4], F32)
        sicol = al("p3_sicol", [128, 4], F32)
        sib = al("p3_sib", [128, WIN], F32)
        qt = [al(f"p3_qt{i}", [128, WIN], F32) for i in range(2)]
        qi = [al(f"p3_qi{i}", [128, WIN], I32) for i in range(2)]
        qd = [al(f"p3_qd{i}", [128, WIN], F32) for i in range(2)]
        qb2 = [al(f"p3_qb{i}", [128, WIN], F32) for i in range(2)]
        qc = [al(f"p3_qc{i}", [128, WIN], F32) for i in range(2)]
        xqo = [al(f"p3_xqo{i}", [128, WIN], BF16) for i in range(2)]
        wust3 = [al(f"p3_wu{i}", [128, 1024], F32) for i in range(2)]
        qa3 = al("p3_qa", [128, 1024], BF16)
        qb3 = al("p3_qb", [128, 1024], BF16)
        wqu3 = [al(f"p3_wqu{i}", [128, 1024], BF16) for i in range(2)]

        p3ps = ExitStack()
        pr_ps = [p3ps.enter_context(nc.psum_tensor(f"p3_pr{i}", [128, WIN], F32)) for i in range(2)]
        st_ps = p3ps.enter_context(nc.psum_tensor("p3_st", [1, WIN], F32))
        sq_ps = p3ps.enter_context(nc.psum_tensor("p3_sq", [1, WIN], F32))
        b3_ps = p3ps.enter_context(nc.psum_tensor("p3_b3", [128, WIN], F32))
        tr_ps = p3ps.enter_context(nc.psum_tensor("p3_tr", [128, 128], F32))
        mi_ps = p3ps.enter_context(nc.psum_tensor("p3_mi", [1, 128], F32))

        for blk in range(NBLK):
            for c in range(16):
                b.dma("sp", "p3o", lambda e, c=c, blk=blk: e.dma_start(
                    out=ofm_sb[:, c * WIN:(c + 1) * WIN],
                    in_=ofm_h[c * 128:(c + 1) * 128, blk * WIN:(blk + 1) * WIN]),
                    writes=[("p3_ofm",)])
            def emit_wu_chunk(c):
                rt, cq = c % 16, c // 16
                sl = c % 2
                b.dma("sp", f"p3wu{sl}", lambda e, rt=rt, cq=cq, sl=sl: e.dma_start(
                    out=wust3[sl][:], in_=wuT[rt * 128:(rt + 1) * 128,
                                              cq * 1024:(cq + 1) * 1024]),
                    writes=[f"p3_wu{sl}"])
                b.op("dve", lambda e, sl=sl: e.tensor_scalar(
                    qa3[:], wust3[sl][:], taucol["u"][:], None, ALU.is_gt),
                    reads=[f"p3_wu{sl}", "taucol_u"], writes=["p3_qa3"])
                b.op("dve", lambda e, sl=sl: e.tensor_scalar(
                    qb3[:], wust3[sl][:], ntaucol["u"][:], None, ALU.is_lt),
                    reads=[f"p3_wu{sl}", "ntaucol_u"], writes=["p3_qb3"])
                b.op("dve", lambda e, sl=sl: e.tensor_tensor(
                    wqu3[sl][:], qa3[:], qb3[:], ALU.subtract),
                    reads=["p3_qa3", "p3_qb3"], writes=[f"p3_wqu{sl}"])
                b.dma("act", f"p3wuo{sl}", lambda e, rt=rt, cq=cq, sl=sl: e.dma_start(
                    out=wqu_h[:, rt:rt + 1, cq * 1024:(cq + 1) * 1024],
                    in_=wqu3[sl][:]),
                    reads=[f"p3_wqu{sl}"])

            for oc in range(16):
                emit_wu_chunk(blk * 32 + 2 * oc)
                emit_wu_chunk(blk * 32 + 2 * oc + 1)
                pp = oc % 2
                b.dma("act", f"p3x{pp}", lambda e, oc=oc, blk=blk, pp=pp: e.dma_start(
                    out=xT_t[pp][:],
                    in_=xT[oc * 128:(oc + 1) * 128, WIN + blk * WIN: WIN + (blk + 1) * WIN]),
                    writes=[f"p3_xT{pp}"])
                grp = f"pr{blk}_{oc}"
                for ic in range(16):
                    b.op("pe", lambda e, oc=oc, ic=ic, pp=pp: e.matmul(
                        pr_ps[pp][:],
                        wo_sb[:, ic * D + oc * 128: ic * D + (oc + 1) * 128],
                        ofm_sb[:, ic * WIN:(ic + 1) * WIN],
                        start=(ic == 0), stop=(ic == 15)),
                        reads=[("p3_wo", ic), ("p3_ofm",)],
                        writes=[f"p3_prp{pp}"], group=grp, final=(ic == 15))
                b.op("dve", lambda e, oc=oc, pp=pp: e.tensor_tensor(
                    r1_sb[:, oc * WIN:(oc + 1) * WIN], pr_ps[pp][:],
                    xT_t[pp][:], ALU.add),
                    reads=[f"p3_prp{pp}", f"p3_xT{pp}"], writes=[("p3_r1", oc)])
                b.op("act", lambda e, oc=oc, pp=pp: e.copy(
                    rbf[pp][:], r1_sb[:, oc * WIN:(oc + 1) * WIN]),
                    reads=[("p3_r1", oc)], writes=[f"p3_rbf{pp}"])
                b.op("act", lambda e, oc=oc, pp=pp: e.activation(
                    sqv[pp][:], r1_sb[:, oc * WIN:(oc + 1) * WIN], AF.Square),
                    reads=[("p3_r1", oc)], writes=[f"p3_sq{pp}"])
                b.op("pe", lambda e, oc=oc, pp=pp: e.matmul(
                    st_ps[:], consts["ones_bf"][:], rbf[pp][:],
                    start=(oc == 0), stop=(oc == 15)),
                    reads=["c_ones_bf", f"p3_rbf{pp}"], writes=["p3_stp"])
                b.op("pe", lambda e, oc=oc, pp=pp: e.matmul(
                    sq_ps[:], consts["ones_bf"][:], sqv[pp][:],
                    start=(oc == 0), stop=(oc == 15)),
                    reads=["c_ones_bf", f"p3_sq{pp}"], writes=["p3_sqp"])
            b.op("dve", lambda e: e.tensor_scalar(srow[:], st_ps[:], 1.0 / D, None, ALU.mult),
                 reads=["p3_stp"], writes=["p3_srow"])
            b.op("dve", lambda e: e.tensor_scalar(vrow[:], sq_ps[:], 1.0 / D, None, ALU.mult),
                 reads=["p3_sqp"], writes=["p3_vrow"])
            b.op("dve", lambda e: e.tensor_tensor(mrow[:], srow[:], srow[:], ALU.mult),
                 reads=["p3_srow"], writes=["p3_mrow"])
            b.op("dve", lambda e: e.tensor_tensor(vrow[:], vrow[:], mrow[:], ALU.subtract),
                 reads=["p3_vrow", "p3_mrow"], writes=["p3_vrow"])
            b.op("act", lambda e: e.activation(vrow[:], vrow[:], AF.Ln, bias=epsrow[:]),
                 reads=["p3_vrow", "epsrow"], writes=["p3_vrow"])
            b.op("act", lambda e: e.activation(rrow[:], vrow[:], AF.Exp, scale=-0.5),
                 reads=["p3_vrow"], writes=["p3_rrow"])
            b.op("dve", lambda e: e.tensor_tensor(mrow[:], srow[:], rrow[:], ALU.mult),
                 reads=["p3_srow", "p3_rrow"], writes=["p3_mrow"])
            b.op("pe", lambda e: e.matmul(b3_ps[:], consts["onesrow_f"][:], rrow[:],
                                          start=True, stop=True),
                 reads=["c_onesrow_f", "p3_rrow"], writes=["p3_b3p"])
            b.op("act", lambda e: e.copy(rstdb[:], b3_ps[:]), reads=["p3_b3p"], writes=["p3_rstdb"])
            b.op("pe", lambda e: e.matmul(b3_ps[:], consts["onesrow_f"][:], mrow[:],
                                          start=True, stop=True),
                 reads=["c_onesrow_f", "p3_mrow"], writes=["p3_b3p"])
            b.op("act", lambda e: e.copy(mrsb[:], b3_ps[:]), reads=["p3_b3p"], writes=["p3_mrsb"])
            for oc in range(16):
                b.op("dve", lambda e, oc=oc: e.tensor_tensor(
                    r1_sb[:, oc * WIN:(oc + 1) * WIN],
                    r1_sb[:, oc * WIN:(oc + 1) * WIN], rstdb[:], ALU.mult),
                    reads=[("p3_r1", oc), "p3_rstdb"], writes=[("p3_r1", oc)])
                b.op("dve", lambda e, oc=oc: e.tensor_tensor(
                    r1_sb[:, oc * WIN:(oc + 1) * WIN],
                    r1_sb[:, oc * WIN:(oc + 1) * WIN], mrsb[:], ALU.subtract),
                    reads=[("p3_r1", oc), "p3_mrsb"], writes=[("p3_r1", oc)])
                b.dma("sp", "p3r1", lambda e, oc=oc, blk=blk: e.dma_start(
                    out=r1_h[oc * 128:(oc + 1) * 128, blk * WIN:(blk + 1) * WIN],
                    in_=r1_sb[:, oc * WIN:(oc + 1) * WIN]),
                    reads=[("p3_r1", oc)])
                if oc == 0:
                    b.op("dve", lambda e: e.tensor_copy(tmax[:], r1_sb[:, 0:WIN]),
                         reads=[("p3_r1", 0)], writes=["p3_tmax"])
                    b.op("dve", lambda e: e.tensor_copy(tmin[:], r1_sb[:, 0:WIN]),
                         reads=[("p3_r1", 0)], writes=["p3_tmin"])
                else:
                    b.op("dve", lambda e, oc=oc: e.tensor_tensor(
                        tmax[:], tmax[:], r1_sb[:, oc * WIN:(oc + 1) * WIN], ALU.max),
                        reads=["p3_tmax", ("p3_r1", oc)], writes=["p3_tmax"])
                    b.op("dve", lambda e, oc=oc: e.tensor_tensor(
                        tmin[:], tmin[:], r1_sb[:, oc * WIN:(oc + 1) * WIN], ALU.min),
                        reads=["p3_tmin", ("p3_r1", oc)], writes=["p3_tmin"])
            b.op("dve", lambda e: e.tensor_scalar(ttmp[:], tmin[:], -1.0, None, ALU.mult),
                 reads=["p3_tmin"], writes=["p3_ttmp"])
            b.op("dve", lambda e: e.tensor_tensor(ttmp[:], ttmp[:], tmax[:], ALU.max),
                 reads=["p3_ttmp", "p3_tmax"], writes=["p3_ttmp"])
            for q in range(4):
                b.op("pe", lambda e, q=q: e.transpose(tr_ps[:], ttmp[:, q * 128:(q + 1) * 128],
                                                      consts["ident_f"][:]),
                     reads=["p3_ttmp", "c_ident_f"], writes=["p3_trp"])
                b.op("dve", lambda e, q=q: e.tensor_reduce(
                    scol[:, q:q + 1], tr_ps[:], AX.X, ALU.max),
                    reads=["p3_trp"], writes=["p3_scol"])
            b.op("dve", lambda e: e.tensor_scalar(scol[:], scol[:], 1.0 / 127.0, EPS,
                                                  ALU.mult, ALU.add),
                 reads=["p3_scol"], writes=["p3_scol"])
            b.op("dve", lambda e: e.reciprocal(sicol[:], scol[:]),
                 reads=["p3_scol"], writes=["p3_sicol"])
            for q in range(4):
                b.op("pe", lambda e, q=q: e.transpose(mi_ps[0:1, 0:128], scol[:, q:q + 1],
                                                      consts["ident_f"][:]),
                     reads=["p3_scol", "c_ident_f"], writes=["p3_mip"])
                b.op("act", lambda e, q=q: e.copy(
                    xs_row[:, q * 128:(q + 1) * 128], mi_ps[0:1, 0:128]),
                    reads=["p3_mip"], writes=[("xs_row",)])
                b.op("pe", lambda e, q=q: e.transpose(mi_ps[0:1, 0:128], sicol[:, q:q + 1],
                                                      consts["ident_f"][:]),
                     reads=["p3_sicol", "c_ident_f", ("xs_row",)], writes=["p3_mip"])
                b.op("act", lambda e, q=q: e.copy(
                    xsi_row[:, q * 128:(q + 1) * 128], mi_ps[0:1, 0:128]),
                    reads=["p3_mip"], writes=[("xsi_row",)])
            b.dma("sp", "p3xs", lambda e, blk=blk: e.dma_start(
                out=xs_h[0:1, blk * WIN:(blk + 1) * WIN], in_=xs_row[:]),
                reads=[("xs_row",)])
            b.op("pe", lambda e: e.matmul(
                b3_ps[:], consts["onesrow_f"][:], xsi_row[:], start=True, stop=True),
                reads=[("xsi_row",), "c_onesrow_f"], writes=["p3_b3p"])
            b.op("act", lambda e: e.copy(sib[:], b3_ps[:]), reads=["p3_b3p"], writes=["p3_sib"])
            for oc in range(16):
                pp = oc % 2
                b.op("dve", lambda e, oc=oc, pp=pp: e.tensor_tensor(
                    qt[pp][:], r1_sb[:, oc * WIN:(oc + 1) * WIN], sib[:], ALU.mult),
                    reads=[("p3_r1", oc), "p3_sib"], writes=[f"p3_qt{pp}"])
                b.op("act", lambda e, pp=pp: e.activation(qi[pp][:], qt[pp][:], AF.Copy),
                     reads=[f"p3_qt{pp}"], writes=[f"p3_qi{pp}"])
                b.op("dve", lambda e, pp=pp: e.tensor_tensor(qd[pp][:], qt[pp][:], qi[pp][:], ALU.subtract),
                     reads=[f"p3_qt{pp}", f"p3_qi{pp}"], writes=[f"p3_qd{pp}"])
                b.op("dve", lambda e, pp=pp: e.tensor_scalar(qb2[pp][:], qd[pp][:], -0.5, None, ALU.is_lt),
                     reads=[f"p3_qd{pp}"], writes=[f"p3_qb{pp}"])
                b.op("dve", lambda e, pp=pp: e.scalar_tensor_tensor(
                    qc[pp][:], qd[pp][:], 0.5, qb2[pp][:], ALU.is_gt, ALU.subtract),
                    reads=[f"p3_qd{pp}", f"p3_qb{pp}"], writes=[f"p3_qc{pp}"])
                b.op("dve", lambda e, pp=pp: e.tensor_tensor(
                    xqo[pp][:], qi[pp][:], qc[pp][:], ALU.add),
                    reads=[f"p3_qi{pp}", f"p3_qc{pp}"], writes=[f"p3_xqo{pp}"])
                b.dma("act", f"p3xq{pp}", lambda e, oc=oc, blk=blk, pp=pp: e.dma_start(
                    out=xq_h[oc * 128:(oc + 1) * 128, blk * WIN:(blk + 1) * WIN],
                    in_=xqo[pp][:]),
                    reads=[f"p3_xqo{pp}"])
        b.barrier()
        p3ps.close()
        p3.close()


    if up_to >= 4:
        p4 = ExitStack()
        al = mkalloc(p4)
        xq_sb = al("p4_xq", [128, 16 * S], BF16)
        for c in range(16):
            b.dma("sp", "p4xq", lambda e, c=c: e.dma_start(
                out=xq_sb[:, c * S:(c + 1) * S], in_=xq_h[c * 128:(c + 1) * 128, :]),
                writes=[("p4_xq",)])
        xsb = al("p4_xsb", [128, S], F32)
        xsrow4 = al("p4_xsrow", [1, S], F32)
        p4ps = ExitStack()
        xb_es = ExitStack()
        xb_ps = xb_es.enter_context(nc.psum_tensor("p4_xb", [128, 512], F32))
        b.dma("sp", "p4xs", lambda e: e.dma_start(out=xsrow4[:], in_=xs_h),
              writes=[("p4_xsrow",)])
        for g4 in range(4):
            b.op("pe", lambda e, g4=g4: e.matmul(
                xb_ps[:], consts["onesrow_f"][:], xsrow4[:, g4 * 512:(g4 + 1) * 512],
                start=True, stop=True),
                reads=[("p4_xsrow",), "c_onesrow_f"], writes=["p4_xbp"])
            b.op("act", lambda e, g4=g4: e.copy(xsb[:, g4 * 512:(g4 + 1) * 512], xb_ps[:]),
                 reads=["p4_xbp"], writes=[("p4_xsb",)])

        probeb = al("p4_probe", [1, 2], F32)
        wqg = [al(f"p4_wqg{i}", [128, 4096], BF16) for i in range(2)]
        wqu = [al(f"p4_wqu{i}", [128, 4096], BF16) for i in range(2)]
        zg = al("p4_zg", [128, 1024], F32)
        zu = al("p4_zu", [128, 1024], F32)
        sg = al("p4_sg", [128, 1024], F32)
        hsb = [al(f"p4_h{i}", [128, 1024], F32) for i in range(2)]
        hmx = al("p4_hmx", [128, 2048], F32)
        hab = [al(f"p4_hab{i}", [128, 1024], F32) for i in range(2)]
        b.barrier()
        xb_es.close()
        gu_es = ExitStack()
        g_ps = [gu_es.enter_context(nc.psum_tensor(f"p4_g{i}", [128, 1024], F32)) for i in range(2)]
        u_ps = [gu_es.enter_context(nc.psum_tensor(f"p4_u{i}", [128, 1024], F32)) for i in range(2)]

        b.op("pool", lambda e: e.memset(probeb[:], 0.0), writes=["p4_probeb"])

        def p4_wload(op):
            sl = op % 2
            for mat, src_h, dst in (("g", wqg_h, wqg), ("u", wqu_h, wqu)):
                b.dma("sp", f"p4w{mat}{sl}",
                      lambda e, op=op, src_h=src_h, dst=dst, sl=sl:
                      e.dma_start(out=dst[sl][:],
                                  in_=src_h[:, :, op * 256:(op + 1) * 256]),
                      writes=[(f"p4_wq{mat}{sl}",)])
                b.op("pool", lambda e: e.tensor_scalar(
                    probeb[:], probeb[:], 1.0, None, ALU.mult),
                    reads=["p4_probeb", (f"p4_wq{mat}{sl}",)],
                    writes=["p4_probeb", (f"p4_wq{mat}{sl}",)])

        def p4_compute(oc):
            sl = (oc // 2) % 2
            woff = (oc % 2) * 128
            hp = oc % 2
            for half in range(2):
                t0 = half * 1024
                gg = f"g{oc}_{half}"
                for ic in range(16):
                    for q2 in range(2):
                        b.op("pe", lambda e, ic=ic, sl=sl, t0=t0, half=half, q2=q2: e.matmul(
                            g_ps[half][:, q2 * 512:(q2 + 1) * 512],
                            wqg[sl][:, ic * 256 + woff: ic * 256 + woff + 128],
                            xq_sb[:, ic * S + t0 + q2 * 512: ic * S + t0 + (q2 + 1) * 512],
                            start=(ic == 0), stop=(ic == 15)),
                            reads=[(f"p4_wqg{sl}",), ("p4_xq",)],
                            writes=[f"p4_gp{half}"], group=gg,
                            final=(ic == 15 and q2 == 1))
                uu = f"u{oc}_{half}"
                for ic in range(16):
                    for q2 in range(2):
                        b.op("pe", lambda e, ic=ic, sl=sl, t0=t0, half=half, q2=q2: e.matmul(
                            u_ps[half][:, q2 * 512:(q2 + 1) * 512],
                            wqu[sl][:, ic * 256 + woff: ic * 256 + woff + 128],
                            xq_sb[:, ic * S + t0 + q2 * 512: ic * S + t0 + (q2 + 1) * 512],
                            start=(ic == 0), stop=(ic == 15)),
                            reads=[(f"p4_wqu{sl}",), ("p4_xq",)],
                            writes=[f"p4_up{half}"], group=uu,
                            final=(ic == 15 and q2 == 1))
                b.op("dve", lambda e, half=half, t0=t0: e.scalar_tensor_tensor(
                    zg[:], g_ps[half][:], wscol["g"][:], xsb[:, t0:t0 + 1024],
                    ALU.mult, ALU.mult),
                    reads=[f"p4_gp{half}", "wscol_g", ("p4_xsb",)], writes=["p4_zg"])
                b.op("act", lambda e: e.activation(sg[:], zg[:], AF.Silu),
                     reads=["p4_zg"], writes=["p4_sg"])
                b.op("dve", lambda e, half=half, t0=t0: e.scalar_tensor_tensor(
                    zu[:], u_ps[half][:], wscol["u"][:], xsb[:, t0:t0 + 1024],
                    ALU.mult, ALU.mult),
                    reads=[f"p4_up{half}", "wscol_u", ("p4_xsb",)], writes=["p4_zu"])
                hkey = f"p4_h{hp}"
                b.op("dve", lambda e, hp=hp: e.tensor_tensor(
                    hsb[hp][:], sg[:], zu[:], ALU.mult),
                    reads=["p4_sg", "p4_zu"], writes=[hkey])
                b.dma("sp", f"p4h{hp}", lambda e, oc=oc, hp=hp, t0=t0: e.dma_start(
                    out=h_h[oc * 128:(oc + 1) * 128, t0:t0 + 1024], in_=hsb[hp][:]),
                    reads=[hkey])
                b.op("act", lambda e, hp=hp: e.activation(
                    hab[hp][:], hsb[hp][:], AF.Abs),
                    reads=[hkey], writes=[(f"p4_hab{hp}",)])
                if oc == 0:
                    b.op("dve", lambda e, hp=hp, t0=t0: e.tensor_copy(
                        hmx[:, t0:t0 + 1024], hab[hp][:]),
                        reads=[(f"p4_hab{hp}",)], writes=[("p4_hmx", half)])
                else:
                    b.op("dve", lambda e, hp=hp, t0=t0: e.tensor_tensor(
                        hmx[:, t0:t0 + 1024], hmx[:, t0:t0 + 1024],
                        hab[hp][:], ALU.max),
                        reads=[("p4_hmx", half), (f"p4_hab{hp}",)],
                        writes=[("p4_hmx", half)])

        p4_wload(0)
        for op in range(II // 256):
            if op + 1 < II // 256:
                p4_wload(op + 1)
            p4_compute(2 * op)
            p4_compute(2 * op + 1)

        b.barrier()
        gu_es.close()
        tr4 = p4ps.enter_context(nc.psum_tensor("p4_tr", [128, 128], F32))
        mi4 = p4ps.enter_context(nc.psum_tensor("p4_mi", [1, 128], F32))
        s2col = al("p4_s2col", [128, 16], F32)
        s2_row = al("p4_s2row", [1, S], F32)
        s2i_row = al("p4_s2irow", [1, S], F32)
        dsc_row = al("p4_dscrow", [1, S], F32)
        for q in range(16):
            b.op("pe", lambda e, q=q: e.transpose(tr4[:], hmx[:, q * 128:(q + 1) * 128],
                                                  consts["ident_f"][:]),
                 reads=[("p4_hmx", 0), ("p4_hmx", 1), "c_ident_f"], writes=["p4_trp"])
            b.op("dve", lambda e, q=q: e.tensor_reduce(
                s2col[:, q:q + 1], tr4[:], AX.X, ALU.max),
                reads=["p4_trp"], writes=["p4_s2col"])
        b.op("dve", lambda e: e.tensor_scalar(s2col[:], s2col[:], 1.0 / 127.0, EPS,
                                              ALU.mult, ALU.add),
             reads=["p4_s2col"], writes=["p4_s2col"])
        for q in range(16):
            b.op("pe", lambda e, q=q: e.transpose(mi4[0:1, 0:128], s2col[:, q:q + 1],
                                                  consts["ident_f"][:]),
                 reads=["p4_s2col", "c_ident_f"], writes=["p4_mip"])
            b.op("act", lambda e, q=q: e.copy(s2_row[:, q * 128:(q + 1) * 128], mi4[:]),
                 reads=["p4_mip"], writes=[("s2_row",)])
        b.op("dve", lambda e: e.reciprocal(s2i_row[:], s2_row[:]),
             reads=[("s2_row",)], writes=[("s2i_row",)])
        b.op("dve", lambda e: e.tensor_scalar(dsc_row[:], s2_row[:],
                                              wssc["d"][0:1, 0:1], None, ALU.mult),
             reads=[("s2_row",), "wssc_d"], writes=[("dsc_row",)])
        b.dma("sp", "p4s2", lambda e: e.dma_start(out=s2i_h, in_=s2i_row[:]),
              reads=[("s2i_row",)])
        b.dma("sp", "p4s2", lambda e: e.dma_start(out=dsc_h, in_=dsc_row[:]),
              reads=[("dsc_row",)])

        b.barrier()
        p4ps.close()
        p4.close()


    if up_to >= 5:
        p5 = ExitStack()
        al = mkalloc(p5)
        wqd_sl = al("p5_wqd", [128, 64 * 1024], BF16)
        s2i_row5 = al("p5_s2irow", [1, S], F32)
        dsc_row5 = al("p5_dscrow", [1, S], F32)
        hload = [al(f"p5_hl{i}", [128, WIN], F32) for i in range(2)]
        s2ib = al("p5_s2ib", [128, WIN], F32)
        dscb = al("p5_dscb", [128, WIN], F32)
        r1t = [al(f"p5_r1t{i}", [128, WIN], F32) for i in range(2)]
        r2t = [al(f"p5_r2t{i}", [128, WIN], F32) for i in range(2)]
        yv = al("p5_yv", [128, WIN], F32)
        q5t = [al(f"p5_qt{i}", [128, WIN], F32) for i in range(2)]
        q5i = [al(f"p5_qi{i}", [128, WIN], I32) for i in range(2)]
        q5d = [al(f"p5_qd{i}", [128, WIN], F32) for i in range(2)]
        q5b = [al(f"p5_qb{i}", [128, WIN], F32) for i in range(2)]
        q5c = [al(f"p5_qc{i}", [128, WIN], F32) for i in range(2)]
        p5ps = ExitStack()
        dn_ps = [p5ps.enter_context(nc.psum_tensor(f"p5_dn{i}", [128, WIN], F32)) for i in range(8)]

        b.dma("sp", "p5rows", lambda e: e.dma_start(out=s2i_row5[:], in_=s2i_h),
              writes=[("s2i_row",)])
        b.dma("sp", "p5rows", lambda e: e.dma_start(out=dsc_row5[:], in_=dsc_h),
              writes=[("dsc_row",)])
        hq_rq = [al(f"p5_hqr{i}", [128, WIN], BF16) for i in range(4)]
        hq_m = [al(f"p5_hqm{i}", [128, WIN], BF16) for i in range(4)]
        for oslab in range(2):
            for ic in range(64):
                b.dma("sp", f"p5w{ic % 2}", lambda e, ic=ic, oslab=oslab: e.dma_start(
                    out=wqd_sl[:, ic * 1024:(ic + 1) * 1024],
                    in_=wqd_h[ic * 128:(ic + 1) * 128, oslab * 1024:(oslab + 1) * 1024]),
                    writes=[("p5_wqd",)])
            for ts in range(4):
                b.op("pe", lambda e, ts=ts: e.matmul(
                    dn_ps[0][:], consts["onesrow_f"][:], dsc_row5[:, ts * WIN:(ts + 1) * WIN],
                    start=True, stop=True),
                    reads=[("dsc_row",), "c_onesrow_f"], writes=["p5_dnp0"])
                b.op("act", lambda e: e.copy(dscb[:], dn_ps[0][:]), reads=["p5_dnp0"],
                     writes=["p5_dscb"])
                if oslab == 0:
                    b.op("pe", lambda e, ts=ts: e.matmul(
                        dn_ps[0][:], consts["onesrow_f"][:], s2i_row5[:, ts * WIN:(ts + 1) * WIN],
                        start=True, stop=True),
                        reads=[("s2i_row",), "c_onesrow_f"], writes=["p5_dnp0"])
                    b.op("act", lambda e: e.copy(s2ib[:], dn_ps[0][:]), reads=["p5_dnp0"],
                         writes=["p5_s2ib"])
                for ic in range(64):
                    sl = ic % 2
                    q4 = ic % 4
                    if oslab == 0:
                        hkey = f"p5_hqr{q4}"
                        b.dma("sp", f"p5h{sl}", lambda e, ic=ic, ts=ts, sl=sl:
                              e.dma_start(out=hload[sl][:],
                                          in_=h_h[ic * 128:(ic + 1) * 128,
                                                  ts * WIN:(ts + 1) * WIN]),
                              writes=[f"p5_hl{sl}"])
                        b.op("dve", lambda e, sl=sl: e.tensor_tensor(
                            q5t[sl][:], hload[sl][:], s2ib[:], ALU.mult),
                            reads=[f"p5_hl{sl}", "p5_s2ib"], writes=[f"p5_qt{sl}"])
                        b.op("act", lambda e, sl=sl: e.activation(q5i[sl][:], q5t[sl][:], AF.Copy),
                             reads=[f"p5_qt{sl}"], writes=[f"p5_qi{sl}"])
                        b.op("dve", lambda e, sl=sl: e.tensor_tensor(q5d[sl][:], q5t[sl][:], q5i[sl][:], ALU.subtract),
                             reads=[f"p5_qt{sl}", f"p5_qi{sl}"], writes=[f"p5_qd{sl}"])
                        b.op("dve", lambda e, sl=sl: e.tensor_scalar(q5b[sl][:], q5d[sl][:], -0.5, None, ALU.is_lt),
                             reads=[f"p5_qd{sl}"], writes=[f"p5_qb{sl}"])
                        b.op("dve", lambda e, sl=sl: e.scalar_tensor_tensor(
                            q5c[sl][:], q5d[sl][:], 0.5, q5b[sl][:], ALU.is_gt, ALU.subtract),
                            reads=[f"p5_qd{sl}", f"p5_qb{sl}"], writes=[f"p5_qc{sl}"])
                        b.op("dve", lambda e, sl=sl, q4=q4: e.tensor_tensor(
                            hq_rq[q4][:], q5i[sl][:], q5c[sl][:], ALU.add),
                            reads=[f"p5_qi{sl}", f"p5_qc{sl}"], writes=[hkey])
                        b.dma("sp", f"p5hqo{q4}", lambda e, ic=ic, ts=ts, q4=q4:
                              e.dma_start(out=hq_h[ic * 128:(ic + 1) * 128,
                                                   ts * WIN:(ts + 1) * WIN],
                                          in_=hq_rq[q4][:]),
                              reads=[hkey], writes=[("hqh", ts, ic)])
                        hq_ap = hq_rq[q4]
                    else:
                        hkey = f"p5_hqm{q4}"
                        b.dma("sp", f"p5hq{q4}", lambda e, ic=ic, ts=ts, q4=q4:
                              e.dma_start(out=hq_m[q4][:],
                                          in_=hq_h[ic * 128:(ic + 1) * 128,
                                                   ts * WIN:(ts + 1) * WIN]),
                              reads=[("hqh", ts, ic)], writes=[hkey])
                        hq_ap = hq_m[q4]
                    for ohc in range(8):
                        gname = f"dn{oslab}_{ts}_{ic}"
                        b.op("pe", lambda e, ic=ic, hq_ap=hq_ap, ohc=ohc: e.matmul(
                            dn_ps[ohc][:], wqd_sl[:, ic * 1024 + ohc * 128: ic * 1024 + (ohc + 1) * 128],
                            hq_ap[:], start=(ic == 0), stop=(ic == 63)),
                            reads=[("p5_wqd",), hkey],
                            writes=[f"p5_dnp{ohc}"], group=gname, final=(ohc == 7))
                for ohc in range(8):
                    orow = oslab * 1024 + ohc * 128
                    sl2 = ohc % 2
                    b.dma("sp", f"p5r1{sl2}", lambda e, orow=orow, ts=ts, sl2=sl2: e.dma_start(
                        out=r1t[sl2][:], in_=r1_h[orow:orow + 128, ts * WIN:(ts + 1) * WIN]),
                        writes=[f"p5_r1t{sl2}"])
                    b.op("dve", lambda e, ohc=ohc: e.tensor_tensor(
                        yv[:], dn_ps[ohc][:], dscb[:], ALU.mult),
                        reads=[f"p5_dnp{ohc}", "p5_dscb"], writes=["p5_yv"])
                    b.op("dve", lambda e, sl2=sl2: e.tensor_tensor(
                        r2t[sl2][:], yv[:], r1t[sl2][:], ALU.add),
                        reads=["p5_yv", f"p5_r1t{sl2}"], writes=[f"p5_r2t{sl2}"])
                    b.dma("sp", f"p5r2{sl2}", lambda e, orow=orow, ts=ts, sl2=sl2: e.dma_start(
                        out=r2_h[orow:orow + 128, ts * WIN:(ts + 1) * WIN],
                        in_=r2t[sl2][:]),
                        reads=[f"p5_r2t{sl2}"])

        b.barrier()
        p5ps.close()
        p5.close()


    if up_to >= 6:
        p6 = ExitStack()
        al = mkalloc(p6)
        r2_sb = al("p6_r2", [128, 16 * WIN], F32)
        rbf6 = [al(f"p6_rbf{i}", [128, WIN], BF16) for i in range(2)]
        sq6 = [al(f"p6_sq{i}", [128, WIN], BF16) for i in range(2)]
        srow6 = al("p6_srow", [1, WIN], F32)
        vrow6 = al("p6_vrow", [1, WIN], F32)
        rrow6 = al("p6_rrow", [1, WIN], F32)
        mrow6 = al("p6_mrow", [1, WIN], F32)
        rstdb6 = al("p6_rstdb", [128, WIN], F32)
        mrsb6 = al("p6_mrsb", [128, WIN], F32)
        oout = [al(f"p6_o{i}", [128, WIN], F32) for i in range(2)]
        p6ps = ExitStack()
        st6 = p6ps.enter_context(nc.psum_tensor("p6_st", [1, WIN], F32))
        sq6p = p6ps.enter_context(nc.psum_tensor("p6_sq", [1, WIN], F32))
        b6 = p6ps.enter_context(nc.psum_tensor("p6_b6", [128, WIN], F32))

        for blk in range(NBLK):
            for c in range(16):
                b.dma("sp", "p6r2", lambda e, c=c, blk=blk: e.dma_start(
                    out=r2_sb[:, c * WIN:(c + 1) * WIN],
                    in_=r2_h[c * 128:(c + 1) * 128, blk * WIN:(blk + 1) * WIN]),
                    writes=[("p6_r2",)])
            for oc in range(16):
                pp = oc % 2
                b.op("act", lambda e, oc=oc, pp=pp: e.copy(
                    rbf6[pp][:], r2_sb[:, oc * WIN:(oc + 1) * WIN]),
                    reads=[("p6_r2",)], writes=[f"p6_rbf{pp}"])
                b.op("act", lambda e, oc=oc, pp=pp: e.activation(
                    sq6[pp][:], r2_sb[:, oc * WIN:(oc + 1) * WIN], AF.Square),
                    reads=[("p6_r2",)], writes=[f"p6_sq{pp}"])
                b.op("pe", lambda e, oc=oc, pp=pp: e.matmul(
                    st6[:], consts["ones_bf"][:], rbf6[pp][:],
                    start=(oc == 0), stop=(oc == 15)),
                    reads=["c_ones_bf", f"p6_rbf{pp}"], writes=["p6_stp"])
                b.op("pe", lambda e, oc=oc, pp=pp: e.matmul(
                    sq6p[:], consts["ones_bf"][:], sq6[pp][:],
                    start=(oc == 0), stop=(oc == 15)),
                    reads=["c_ones_bf", f"p6_sq{pp}"], writes=["p6_sqp"])
            b.op("dve", lambda e: e.tensor_scalar(srow6[:], st6[:], 1.0 / D, None, ALU.mult),
                 reads=["p6_stp"], writes=["p6_srow"])
            b.op("dve", lambda e: e.tensor_scalar(vrow6[:], sq6p[:], 1.0 / D, None, ALU.mult),
                 reads=["p6_sqp"], writes=["p6_vrow"])
            b.op("dve", lambda e: e.tensor_tensor(mrow6[:], srow6[:], srow6[:], ALU.mult),
                 reads=["p6_srow"], writes=["p6_mrow"])
            b.op("dve", lambda e: e.tensor_tensor(vrow6[:], vrow6[:], mrow6[:], ALU.subtract),
                 reads=["p6_vrow", "p6_mrow"], writes=["p6_vrow"])
            b.op("act", lambda e: e.activation(vrow6[:], vrow6[:], AF.Ln, bias=epsrow[:]),
                 reads=["p6_vrow", "epsrow"], writes=["p6_vrow"])
            b.op("act", lambda e: e.activation(rrow6[:], vrow6[:], AF.Exp, scale=-0.5),
                 reads=["p6_vrow"], writes=["p6_rrow"])
            b.op("dve", lambda e: e.tensor_tensor(mrow6[:], srow6[:], rrow6[:], ALU.mult),
                 reads=["p6_srow", "p6_rrow"], writes=["p6_mrow"])
            b.op("pe", lambda e: e.matmul(b6[:], consts["onesrow_f"][:], rrow6[:],
                                          start=True, stop=True),
                 reads=["c_onesrow_f", "p6_rrow"], writes=["p6_b6p"])
            b.op("act", lambda e: e.copy(rstdb6[:], b6[:]), reads=["p6_b6p"], writes=["p6_rstdb"])
            b.op("pe", lambda e: e.matmul(b6[:], consts["onesrow_f"][:], mrow6[:],
                                          start=True, stop=True),
                 reads=["c_onesrow_f", "p6_mrow"], writes=["p6_b6p"])
            b.op("act", lambda e: e.copy(mrsb6[:], b6[:]), reads=["p6_b6p"], writes=["p6_mrsb"])
            for oc in range(16):
                pp = oc % 2
                b.op("dve", lambda e, oc=oc, pp=pp: e.tensor_tensor(
                    oout[pp][:], r2_sb[:, oc * WIN:(oc + 1) * WIN], rstdb6[:], ALU.mult),
                    reads=[("p6_r2",), "p6_rstdb"], writes=[f"p6_o{pp}"])
                b.op("dve", lambda e, pp=pp: e.tensor_tensor(
                    oout[pp][:], oout[pp][:], mrsb6[:], ALU.subtract),
                    reads=[f"p6_o{pp}", "p6_mrsb"], writes=[f"p6_o{pp}"])
                b.dma("sp", f"p6o{pp}", lambda e, oc=oc, blk=blk, pp=pp: e.dma_start(
                    out=outT[oc * 128:(oc + 1) * 128, blk * WIN:(blk + 1) * WIN],
                    in_=oout[pp][:]),
                    reads=[f"p6_o{pp}"])

        b.barrier()
        p6ps.close()
        p6.close()


    if up_to < 1:
        pump(10000)
        b.barrier()
        p01.close()
    if up_to < 6:
        dzs = ExitStack()
        dal = mkalloc(dzs)
        zout = dal("dz_out", [128, WIN], F32)
        b.op("dve", lambda e: e.memset(zout[:], 0.0), writes=["dz_out"])
        for blk in range(NBLK):
            for oc in range(16):
                b.dma("sp", "dzo", lambda e, oc=oc, blk=blk: e.dma_start(
                    out=outT[oc * 128:(oc + 1) * 128, blk * WIN:(blk + 1) * WIN],
                    in_=zout[:]),
                    reads=["dz_out"])
        b.barrier()
        dzs.close()

    b.finalize(nc, root)
    root.close()
    return nc



def _host_inputs(inputs):
    x = np.asarray(inputs["L0_attn_x"])[0]
    xT_full = np.ascontiguousarray(x.T)
    shared = {
        "wqT": np.ascontiguousarray(np.asarray(inputs["L0_attn_Wq"]).T),
        "wkT": np.ascontiguousarray(np.asarray(inputs["L0_attn_Wk"]).T),
        "wvT": np.ascontiguousarray(np.asarray(inputs["L0_attn_Wv"]).T),
        "woT": np.ascontiguousarray(np.asarray(inputs["L0_attn_Wo"]).T),
        "wgT": np.ascontiguousarray(np.asarray(inputs["L0_ffn_w_gate"]).T),
        "wuT": np.ascontiguousarray(np.asarray(inputs["L0_ffn_w_up"]).T),
        "wdT": np.ascontiguousarray(np.asarray(inputs["L0_ffn_w_down"]).T),
    }
    shuf = np.zeros((HD, HD), dtype=np.float32)
    for i in range(HD // 2):
        shuf[2 * i, 2 * i + 1] = -1.0
        shuf[2 * i + 1, 2 * i] = 1.0
    shared["shufT"] = np.ascontiguousarray(shuf.T).astype(NPBF)
    jj = np.arange(128)[:, None]
    ii = np.arange(128)[None, :]
    shared["mprev"] = (jj > ii).astype(NPBF)
    shared["mcur"] = (jj <= ii).astype(NPBF)
    shared["ones_bf"] = np.ones((128, 1), dtype=NPBF)
    shared["onesrow_f"] = np.ones((1, 128), dtype=np.float32)
    shared["ident_f"] = np.eye(128, dtype=np.float32)

    invfreq = (1.0 / 10000.0 ** (np.arange(0, HD, 2, dtype=np.float32) / HD))
    freq_col = np.repeat(invfreq, 2)[:, None]

    in_maps = []
    for c in range(NCORES):
        g0 = c * S
        if c == 0:
            halo = np.zeros((D, WIN), dtype=np.float32)
        else:
            halo = xT_full[:, g0 - WIN:g0]
        xT_c = np.ascontiguousarray(
            np.concatenate([halo, xT_full[:, g0:g0 + S]], axis=1))
        pos = np.arange(g0 - WIN, g0 + S, dtype=np.float32)[None, :]
        ang = pos * freq_col
        m = dict(shared)
        m["xT"] = xT_c
        m["ropeC"] = np.cos(ang).astype(np.float32)
        m["ropeS"] = np.sin(ang).astype(np.float32)
        m["zcol"] = np.full((128, 1), 0.0 if c == 0 else 1.0, dtype=np.float32)
        in_maps.append(m)
    return in_maps


def kernel(**inputs):
    in_maps = _host_inputs(inputs)
    nc = build_program()
    res = run_bass_kernel_spmd(nc, in_maps, core_ids=list(range(NCORES)))
    parts = [np.asarray(r["outT"]).T for r in res.results]
    out = np.concatenate(parts, axis=0)[None]
    return np.ascontiguousarray(out, dtype=np.float32)
